# version 01
# baseline (speedup 1.0000x reference)
import numpy as np
import ml_dtypes
import concourse.bass as bass
import concourse.mybir as mybir
from concourse.bass_utils import run_bass_kernel_spmd

F32 = mybir.dt.float32
BF16 = mybir.dt.bfloat16
I32 = mybir.dt.int32
AF = mybir.ActivationFunctionType
OP = mybir.AluOpType
AX = mybir.AxisListType

NCORE = 8
T = 2048
D = 1024
H = 16
FF = 4096
NL = 3
C = 128
NCH = 16
SLAB = 512
NSLAB = 4
CQ = 0.0625
LN8 = 2.0794415416798357
CS = 0.35355339059327373
LN_EPS = 1e-5
ATTN_EPS = 1e-6
INV127 = 1.0 / 127.0

ENGINES = ("tensor", "vector", "scalar", "gpsimd", "sync")
DEBUG = False


class Rec:

    SPACED = ("vector", "gpsimd", "scalar")

    def __init__(self):
        self.items = []
        self.cnt = {e: 0 for e in ENGINES}
        self.opi = {e: 0 for e in ENGINES}
        self.writer = {}
        self.readers = {}

    def barrier(self):
        snap = dict(self.cnt)
        self._pend = {e: {d: v for d, v in snap.items() if d != e and v > 0}
                      for e in ENGINES}

    def op(self, eng, fn, reads=(), writes=(), dma=False):
        inc = 16 if dma else 1
        waits = {}
        pend = getattr(self, "_pend", None)
        if pend and pend.get(eng):
            waits.update(pend.pop(eng))

        def need(dep):
            deng, dcnt = dep[0], dep[1]
            dep_dma = dep[3] if len(dep) > 3 else False
            if deng == eng and not dep_dma:
                return
            if waits.get(deng, 0) < dcnt:
                waits[deng] = dcnt

        if eng in self.SPACED:
            mind = 99
            for t in reads:
                w = self.writer.get(t)
                if w and w[0] == eng:
                    mind = min(mind, self.opi[eng] - w[2])
            npad = max(0, 3 - mind)
            for _ in range(npad):
                if eng == "scalar":
                    sfn = lambda e: e.memzero(self._junk[:])
                else:
                    sfn = lambda e: e.memset(self._junk[:], 0.0)
                self.cnt[eng] += 1
                self.opi[eng] += 1
                self.items.append((eng, sfn, {}, 1))

        for t in reads:
            w = self.writer.get(t)
            if w:
                need(w)
        for t in writes:
            w = self.writer.get(t)
            if w:
                need(w)
            for rd in self.readers.get(t, ()):
                need(rd)
        self.cnt[eng] += inc
        self.opi[eng] += 1
        here = (eng, self.cnt[eng], self.opi[eng], dma)
        for t in reads:
            self.readers.setdefault(t, []).append(here)
        for t in writes:
            self.writer[t] = here
            self.readers[t] = []
        self.items.append((eng, fn, waits, inc))

    def emit(self, block, sems):
        per = {e: [] for e in ENGINES}
        for eng, fn, waits, inc in self.items:
            per[eng].append((fn, waits, inc))
        finals = dict(self.cnt)
        hw = {e: {d: 0 for d in ENGINES} for e in ENGINES}
        fence = {e: set() for e in ENGINES}
        for eng, fn, waits, inc in self.items:
            for d, v in waits.items():
                fence[d].add(v)

        def mk(ename, items):
            fvals = sorted(fence[ename])
            has_dma = any(inc == 16 for _, _, inc in items)

            def body(eng):
                cnt = 0
                selfhw = 0
                fi = 0
                nonlocal fvals
                for fn, waits, inc in items:
                    if has_dma:
                        fmax = 0
                        while fi < len(fvals) and fvals[fi] <= cnt:
                            fmax = fvals[fi]
                            fi += 1
                        if fmax > selfhw:
                            eng.wait_ge(sems[ename], fmax)
                            selfhw = fmax
                    for d, v in sorted(waits.items()):
                        if hw[ename][d] < v:
                            eng.wait_ge(sems[d], v)
                            hw[ename][d] = v
                    fn(eng).then_inc(sems[ename], inc)
                    cnt += inc
                for d in ENGINES:
                    if d != ename and finals[d] > 0 and hw[ename][d] < finals[d]:
                        eng.wait_ge(sems[d], finals[d])
                if has_dma and finals[ename] > selfhw:
                    eng.wait_ge(sems[ename], finals[ename])
            return body

        for ename in ENGINES:
            getattr(block, ename)(mk(ename, per[ename]))


SBUF_BASE = 16512


class Alloc:
    def __init__(self, nc):
        self.nc = nc
        self.off = SBUF_BASE
        self.offs = {}

    def at(self, name, shape, dtype, offset):
        return self.nc.alloc_sbuf_tensor_at(name, shape, dtype, offset=offset)

    def buf(self, name, shape, dtype):
        per = int(np.prod(shape[1:])) * mybir.dt.size(dtype)
        per = (per + 31) // 32 * 32
        t = self.nc.alloc_sbuf_tensor_at(name, shape, dtype, offset=self.off)
        self.offs[name] = self.off
        self.off += per
        return t

    def mark(self):
        return self.off


def ceil32(x):
    return (x + 31) // 32 * 32


def build_launch(mode, la, lb):
    nc = bass.Bass("TRN2", target_bir_lowering=False, debug=False)
    r = Rec()

    dram = {}

    def din(name, shape, dtype):
        dram[name] = nc.dram_tensor(name, shape, dtype, kind="ExternalInput").ap()
        return dram[name]

    def dout(name, shape, dtype):
        dram[name] = nc.dram_tensor(name, shape, dtype, kind="ExternalOutput").ap()
        return dram[name]

    din("idf", [128, 128], F32)
    din("idb", [128, 128], BF16)
    din("triu", [128, 128], F32)
    din("e16", [128, 128], BF16)
    din("sel", [16, 1024], BF16)
    din("ones1f", [1, 128], F32)

    has_a = mode in ("A", "BA")
    has_b = mode in ("BA", "BF", "DBGC")
    has_f = mode == "BF"
    if mode == "DBGC":
        has_a = False

    if mode == "A":
        din("x0", [T, D], F32)
    else:
        din("xin", [8, NSLAB, 128, SLAB], F32)
    if mode == "DBGC":
        dout("dbgc_sbf", [128, 1040], BF16)
        dout("dbgc_kf0", [128, 1040], BF16)
    if has_a:
        din("wk_a", [D, D], F32)
        din("wv_a", [D, D], F32)
        din("bk_a", [128, 8], F32)
        din("bv_a", [1, D], F32)
        din("om_a", [128, 32], F32)
        dout("xout", [8, NSLAB, 128, SLAB], F32)
        dout("kf_out", [8, 128, T], BF16)
        dout("vt_out", [NSLAB, 128, 4160], BF16)
        dout("sch_out", [NCH, 64, 1040], BF16)
        dout("stot", [64, 1040], BF16)
    if has_b:
        din("kf_in", [8, 128, T], BF16)
        din("vt_in", [NSLAB, 128, 4160], BF16)
        din("sch_in", [NCH, 64, 1040], BF16)
        din("gst", [8, 64, 1040], BF16)
        din("pmask", [128, 8], F32)
        din("wq_b", [D, D], F32)
        din("wo_b", [D, D], F32)
        din("w1_b", [D, FF], F32)
        din("w2_b", [FF, D], F32)
        din("bq_b", [128, 8], F32)
        din("bo_b", [128, 8], F32)
        din("ln1g_b", [128, 8], F32)
        din("ln1b_b", [128, 8], F32)
        din("ln2g_b", [128, 8], F32)
        din("ln2b_b", [128, 8], F32)
        din("bf1_b", [128, 32], F32)
        din("bf2_b", [128, 8], F32)
        din("om_b", [128, 32], F32)
        dram["rmid"] = nc.dram_tensor("rmid", [NSLAB, 8, 128, SLAB], F32).ap()
        dram["xmid"] = nc.dram_tensor("xmid", [NSLAB, 8, 128, SLAB], F32).ap()
    if has_f:
        din("x0", [T, D], F32)
        din("rhs", [D, D], F32)
        dout("out", [T, D], F32)
    if DEBUG and has_b:
        dout("dbg_qf", [8, 128, SLAB], BF16)
        dout("dbg_sbf", [128, 1040], BF16)
        dout("dbg_atm", [128, 1024], F32)
        dout("dbg_hy", [8, 128, SLAB], F32)
        dout("dbg_num", [2, 128, 65], F32)
        dout("dbg_a2m", [2, 128, 128], BF16)

    al = Alloc(nc)
    sb = {}

    def B(name, shape, dtype):
        sb[name] = al.buf(name, shape, dtype)
        return sb[name]

    for i in range(8):
        B(f"xsb{i}", [128, SLAB], F32)
    for i in range(8):
        B(f"xbf{i}", [128, SLAB], BF16)
    B("Wa", [128, 8192], BF16)
    B("Wb", [128, 8192], BF16)
    kf_off = al.mark()
    for i in range(8):
        B(f"kf{i}", [128, T], BF16)
    vt_off = al.mark()
    for i in range(NSLAB):
        B(f"vt{i}", [128, 4160], BF16)
    for i in range(8):
        B(f"qs{i}", [128, SLAB], BF16)
    for i in range(8):
        B(f"qf{i}", [128, SLAB], BF16)
    B("srun", [128, 1040], F32)
    B("sbf", [128, 1040], BF16)
    B("schb0", [128, 1040], BF16)
    B("schb1", [128, 1040], BF16)
    B("atm", [128, 1024], F32)
    B("etm", [128, 1024], F32)
    B("escr", [128, SLAB], F32)
    B("sq0", [128, SLAB], BF16)
    B("sq1", [128, SLAB], BF16)
    B("t1", [128, SLAB], F32)
    for i in range(8):
        B(f"hy{i}", [128, SLAB], F32)
    B("a2m", [128, 128], BF16)
    B("cb", [128, 4], F32)
    r._junk = B("junk", [128, 64], F32)
    B("onesc", [128, 1], F32)
    B("oq", [128, 128], BF16)
    B("omsb", [128, 32], F32)
    B("dcol", [128, 2], F32)
    B("st8", [128, 12], F32)
    B("rowb", [1, SLAB], F32)
    B("rowb2", [1, SLAB], F32)
    if has_a:
        B("bvbc", [128, 1024], F32)
        B("stotsb", [128, 1040], F32)
        B("bvrow", [1, 1024], F32)
    B("bsum", [128, 8], F32)
    B("idf", [128, 128], F32)
    B("idb", [128, 128], BF16)
    B("triu", [128, 128], F32)
    B("e16", [128, 128], BF16)
    B("sel", [16, 1024], BF16)
    B("ones1f", [1, 128], F32)
    B("pm", [128, 8], F32)
    B("Ftile", [16, SLAB], F32)
    B("Fbf", [16, SLAB], BF16)
    for nm in ("bq", "bo", "l1g", "l1b", "l2g", "l2b", "bf2", "bk"):
        B(nm, [128, 8], F32)
    B("bf1", [128, 32], F32)
    if has_f:
        for i in range(8):
            sb[f"qrhs{i}"] = al.at(f"qrhs{i}", [128, 1024], BF16,
                                   offset=al.offs[f"xsb{i}"])
        sb["scolbc"] = al.at("scolbc", [128, 1024], F32, offset=al.offs["xbf0"])
        sb["rcolbc"] = al.at("rcolbc", [128, 1024], F32, offset=al.offs["xbf4"])
        B("scall", [128, 8], F32)
        B("rcall", [128, 8], F32)
        for i in range(8):
            sb[f"rhsf{i}"] = al.at(f"rhsf{i}", [128, 1024], F32,
                                   offset=kf_off + i * ceil32(T * 2))
        sb["x1t"] = al.at("x1t", [128, 1024], F32, offset=al.offs["qf0"])
        sb["qi"] = al.at("qi", [128, 1024], I32, offset=al.offs["qf4"])
        sb["qlf"] = al.at("qlf", [128, 1024], BF16, offset=al.offs["escr"])
        sb["x0t"] = al.at("x0t", [128, 1024], F32, offset=al.offs["srun"])

    assert al.off <= SBUF_BASE + 208 * 1024, f"SBUF over budget: {al.off}"

    ps = [nc.alloc_psum_tensor(f"ps{i}", [128, 512], F32) for i in range(8)]

    def dma(q, out_ap, in_ap, reads, writes):
        r.op(q, lambda e, o=out_ap, i=in_ap: e.dma_start(out=o, in_=i),
             reads=reads, writes=writes, dma=True)

    def mm(out_ap, lh, rh, st, sp, reads, writes, tile_position=None):
        r.op("tensor",
             lambda e, o=out_ap, a=lh, b=rh, s=st, p=sp, tp=tile_position:
             e.matmul(o, a, b, start=s, stop=p, tile_position=tp),
             reads=reads, writes=writes)

    def vop(fn, reads, writes, eng="vector"):
        r.op(eng, fn, reads=reads, writes=writes)

    for nm, key in (("idf", "idf"), ("idb", "idb"), ("triu", "triu"),
                    ("e16", "e16"), ("sel", "sel"), ("ones1f", "ones1f")):
        dma("sync", sb[nm][:], dram[key], [], [nm])
    if has_b:
        dma("sync", sb["pm"][:], dram["pmask"], [], ["pm"])
    vop(lambda e: e.memset(sb["onesc"][:], 1.0), [], ["onesc"])
    vop(lambda e: e.memset(sb["cb"][:, 0:1], -LN8), [], ["cb"])
    vop(lambda e: e.memset(sb["cb"][:, 1:2], LN_EPS), [], ["cb"])

    def load_layer_vecs(which):
        if which == "B":
            for nm, key in (("bq", "bq_b"), ("bo", "bo_b"), ("l1g", "ln1g_b"),
                            ("l1b", "ln1b_b"), ("l2g", "ln2g_b"), ("l2b", "ln2b_b"),
                            ("bf2", "bf2_b")):
                dma("sync", sb[nm][:], dram[key], [], [nm])
            dma("sync", sb["bf1"][:], dram["bf1_b"], [], ["bf1"])
            vop(lambda e: e.tensor_tensor(sb["bsum"][:], sb["l1b"][:], sb["bf2"][:], OP.add),
                ["l1b", "bf2"], ["bsum"])
        else:
            dma("sync", sb["bk"][:], dram["bk_a"], [], ["bk"])
            dma("sync", sb["bvrow"][:], dram["bv_a"], [], ["bvrow"])

    def build_oq(omkey):
        dma("sync", sb["omsb"][:], dram[omkey], [], ["om"])
        vop(lambda e: e.memset(sb["oq"][:], 0.0), [], ["oq"])
        for (r0, c0, sc) in ((0, 0, CS), (0, 32, -CS), (64, 64, CS), (64, 96, -CS)):
            vop(lambda e, r0=r0, c0=c0, sc=sc:
                e.tensor_scalar(sb["oq"][r0:r0 + 64, c0:c0 + 32],
                                sb["omsb"][r0:r0 + 64, 0:32], sc, None, OP.mult),
                ["om", "oq"], ["oq"])

    def build_bvbc():
        for n2 in range(2):
            mm(ps[2][0:128, 0:512], sb["ones1f"][0:1, 0:128],
               sb["bvrow"][0:1, 512 * n2:512 * n2 + 512], True, True,
               ["ones1f", "bvrow", "ps2"], ["ps2"])
            vop(lambda e, n2=n2: e.tensor_copy(sb["bvbc"][:, 512 * n2:512 * n2 + 512],
                                               ps[2][0:128, 0:512]),
                ["ps2"], ["bvbc"])

    def load_w(dst, key, ncols=D):
        for kc in range(8):
            dma("gpsimd", sb[dst][:, ncols * kc:ncols * kc + ncols],
                dram[key][128 * kc:128 * kc + 128, :], [dst], [dst])

    def load_w_slice(dst, key, r0, r1c, c0, c1):
        n = c1 - c0
        for kc in range((r1c - r0) // 128):
            dma("gpsimd", sb[dst][:, n * kc:n * kc + n],
                dram[key][r0 + 128 * kc:r0 + 128 * kc + 128, c0:c1], [dst], [dst])

    def favor_fm(s, wkey, biaskey, dst_pref, store_whole):
        col = 512 * s if store_whole else 0
        for mc in range(8):
            pb = ps[mc % 2]
            for kc in range(8):
                mm(pb[0:128, 0:512],
                   sb[wkey][:, 1024 * kc + 128 * mc:1024 * kc + 128 * mc + 128],
                   sb[f"xbf{kc}"][:, 0:512], kc == 0, kc == 7,
                   [wkey, f"xbf{kc}", f"ps{mc % 2}"], [f"ps{mc % 2}"])
            vop(lambda e, pb=pb, mc=mc: e.activation(
                sb[f"qs{mc}"][:, 0:512], pb[0:128, 0:512], AF.Identity,
                bias=sb[biaskey][:, mc:mc + 1], scale=1.0),
                [f"ps{mc % 2}", biaskey], [f"qs{mc}"], eng="scalar")
        for mc in range(8):
            sqb = f"sq{mc % 2}"
            vop(lambda e, mc=mc, sqb=sqb: e.activation(
                sb[sqb][:, 0:512], sb[f"qs{mc}"][:, 0:512], AF.Square),
                [f"qs{mc}"], [sqb], eng="scalar")
            mm(ps[3][0:16, 0:512], sb["e16"][:, 16 * mc:16 * mc + 16],
               sb[sqb][:, 0:512], mc == 0, mc == 7,
               ["e16", sqb, "ps3"], ["ps3"])
        vop(lambda e: e.activation(sb["Ftile"][:, 0:512], ps[3][0:16, 0:512],
                                   AF.Exp, bias=sb["cb"][0:16, 0:1], scale=-CQ),
            ["ps3", "cb"], ["Ftile"], eng="scalar")
        vop(lambda e: e.tensor_copy(sb["Fbf"][:, 0:512], sb["Ftile"][:, 0:512]),
            ["Ftile"], ["Fbf"])
        for d in range(8):
            mm(ps[4][0:128, 0:512], sb["oq"][:, 0:128], sb[f"qs{d}"][:, 0:512],
               True, True, ["oq", f"qs{d}", "ps4"], ["ps4"])
            mm(ps[5][0:128, 0:512], sb["sel"][0:16, 128 * d:128 * d + 128],
               sb["Fbf"][0:16, 0:512], True, True,
               ["sel", "Fbf", "ps5"], ["ps5"])
            vop(lambda e: e.activation(sb["escr"][:, 0:512], ps[4][0:128, 0:512], AF.Exp),
                ["ps4"], ["escr"], eng="scalar")
            tgt = f"{dst_pref}{d}"
            vop(lambda e, tgt=tgt, col=col: e.tensor_tensor(
                sb[tgt][:, col:col + 512], sb["escr"][:, 0:512],
                ps[5][0:128, 0:512], OP.mult),
                ["escr", "ps5"], [tgt])

    def phase_a(layer_first):
        build_oq("om_a")
        load_layer_vecs("A")
        build_bvbc()
        load_w("Wa", "wk_a")
        load_w("Wb", "wv_a")
        for s in range(NSLAB):
            for tcl in range(4):
                vop(lambda e, s=s, tcl=tcl: e.memset(
                    sb[f"vt{s}"][:, 1040 * tcl:1040 * tcl + 1040]
                    .rearrange("p (h j) -> p h j", j=65)[:, :, 64:65], 1.0),
                    [f"vt{s}"], [f"vt{s}"])
        vop(lambda e: e.memset(sb["stotsb"][:], 0.0), [], ["stotsb"])
        for s in range(NSLAB):
            if mode == "A":
                for tcl in range(4):
                    c = 4 * s + tcl
                    dma("sync", sb["atm"][:], dram["x0"][128 * c:128 * c + 128, :],
                        ["atm"], ["atm"])
                    for kc in range(8):
                        r.op("tensor", lambda e, kc=kc: e.transpose(
                            ps[6][0:128, 0:128],
                            sb["atm"][:, 128 * kc:128 * kc + 128], sb["idf"][:]),
                            reads=["atm", "idf", "ps6"], writes=["ps6"])
                        vop(lambda e, kc=kc, tcl=tcl: e.tensor_copy(
                            sb[f"xsb{kc}"][:, 128 * tcl:128 * tcl + 128],
                            ps[6][0:128, 0:128]),
                            ["ps6"], [f"xsb{kc}"])
                for kc in range(8):
                    dma("sync", dram["xout"][kc, s], sb[f"xsb{kc}"][:],
                        [f"xsb{kc}"], [])
            else:
                for kc in range(8):
                    dma("sync", sb[f"xsb{kc}"][:], dram["xmid"][s, kc],
                        [f"xsb{kc}", "xmid"], [f"xsb{kc}"])
            for kc in range(8):
                vop(lambda e, kc=kc: e.tensor_copy(sb[f"xbf{kc}"][:],
                                                   sb[f"xsb{kc}"][:]),
                    [f"xsb{kc}"], [f"xbf{kc}"], eng="gpsimd")
            for tcl in range(4):
                for n2 in range(2):
                    pb = ps[n2]
                    for kc in range(8):
                        mm(pb[0:128, 0:512],
                           sb[f"xbf{kc}"][:, 128 * tcl:128 * tcl + 128],
                           sb["Wb"][:, 1024 * kc + 512 * n2:1024 * kc + 512 * n2 + 512],
                           kc == 0, kc == 7,
                           [f"xbf{kc}", "Wb", f"ps{n2}"], [f"ps{n2}"])
                    vop(lambda e, s=s, tcl=tcl, n2=n2, pb=pb: e.tensor_tensor(
                        sb[f"vt{s}"][:, 1040 * tcl + 520 * n2:1040 * tcl + 520 * n2 + 520]
                        .rearrange("p (h j) -> p h j", j=65)[:, :, 0:64],
                        pb[0:128, 0:512].rearrange("p (h j) -> p h j", j=64),
                        sb["bvbc"][:, 512 * n2:512 * n2 + 512]
                        .rearrange("p (h j) -> p h j", j=64), OP.add),
                        [f"ps{n2}", "bvbc", f"vt{s}"], [f"vt{s}"])
            favor_fm(s, "Wa", "bk", "kf", True)
            for tcl in range(4):
                c = 4 * s + tcl
                for d in range(8):
                    pb = ps[2 + d // 4]
                    mm(pb[0:128, 128 * (d % 4):128 * (d % 4) + 128],
                       sb[f"qs{d}"][:, 128 * tcl:128 * tcl + 128],
                       sb["oq"][:, 0:128], True, True,
                       [f"qs{d}", "oq", f"ps{2 + d // 4}"], [f"ps{2 + d // 4}"])
                r.op("tensor", lambda e, tcl=tcl: e.transpose(
                    ps[6][0:128, 0:16],
                    sb["Ftile"][0:16, 128 * tcl:128 * tcl + 128],
                    sb["idf"][0:16, 0:16]),
                    reads=["Ftile", "idf", "ps6"], writes=["ps6"])
                for half in range(2):
                    vop(lambda e, half=half: e.activation(
                        sb["etm"][:, 512 * half:512 * half + 512],
                        ps[2 + half][0:128, 0:512], AF.Exp),
                        [f"ps{2 + half}"], ["etm"], eng="scalar")
                vop(lambda e: e.tensor_copy(sb["atm"][:, 0:16], ps[6][0:128, 0:16]),
                    ["ps6"], ["atm"])
                vop(lambda e: e.tensor_tensor(
                    sb["schb1"][:, 0:1024].rearrange("p (h j) -> p h j", j=64),
                    sb["etm"][:, 0:1024].rearrange("p (h j) -> p h j", j=64),
                    sb["atm"][:, 0:16].rearrange("p (h j) -> p h j", j=1)
                    .broadcast_to((128, 16, 64)), OP.mult),
                    ["etm", "atm"], ["schb1"])
                for h in range(16):
                    mm(ps[7][0:64, 0:65], sb["schb1"][:, 64 * h:64 * h + 64],
                       sb[f"vt{s}"][:, 1040 * tcl + 65 * h:1040 * tcl + 65 * h + 65],
                       True, True, ["schb1", f"vt{s}", "ps7"], ["ps7"])
                    vop(lambda e, h=h: e.tensor_copy(
                        sb["schb0"][0:64, 65 * h:65 * h + 65], ps[7][0:64, 0:65]),
                        ["ps7"], ["schb0"])
                vop(lambda e: e.tensor_tensor(sb["stotsb"][0:64, :],
                                              sb["stotsb"][0:64, :],
                                              sb["schb0"][0:64, :], OP.add),
                    ["schb0", "stotsb"], ["stotsb"])
                dma("sync", dram["sch_out"][c], sb["schb0"][0:64, :], ["schb0"], [])
        for d in range(8):
            dma("sync", dram["kf_out"][d], sb[f"kf{d}"][:], [f"kf{d}"], [])
        for s in range(NSLAB):
            dma("sync", dram["vt_out"][s], sb[f"vt{s}"][:], [f"vt{s}"], [])
        vop(lambda e: e.tensor_copy(sb["schb0"][0:64, :], sb["stotsb"][0:64, :]),
            ["stotsb"], ["schb0"])
        dma("sync", dram["stot"], sb["schb0"][0:64, :], ["schb0"], [])

    def phase_b():
        build_oq("om_b")
        load_layer_vecs("B")
        vop(lambda e: e.memset(sb["srun"][:], 0.0), [], ["srun"])
        for j in range(8):
            dma("sync", sb[f"kf{j}"][0:64, 0:1040], dram["gst"][j],
                [f"kf{j}"], [f"kf{j}"])
            dma("sync", sb[f"kf{j}"][64:128, 0:1040], dram["gst"][j],
                [f"kf{j}"], [f"kf{j}"])
        for j in range(8):
            vop(lambda e, j=j: e.scalar_tensor_tensor(
                sb["srun"][:], sb[f"kf{j}"][:, 0:1040], sb["pm"][:, j:j + 1],
                sb["srun"][:], OP.mult, OP.add),
                [f"kf{j}", "pm", "srun"], ["srun"])
        vop(lambda e: e.tensor_copy(sb["sbf"][:], sb["srun"][:]), ["srun"], ["sbf"])
        load_w("Wa", "wq_b")
        load_w("Wb", "wo_b")
        for d in range(8):
            dma("sync", sb[f"kf{d}"][:], dram["kf_in"][d], [f"kf{d}"], [f"kf{d}"])
        for s in range(NSLAB):
            dma("sync", sb[f"vt{s}"][:], dram["vt_in"][s], [f"vt{s}"], [f"vt{s}"])
        if DEBUG:
            dma("sync", dram["dbg_sbf"], sb["sbf"][:], ["sbf"], [])

        for s in range(NSLAB):
            for kc in range(8):
                dma("sync", sb[f"xsb{kc}"][:], dram["xin"][kc, s],
                    [f"xsb{kc}"], [f"xsb{kc}"])
                vop(lambda e, kc=kc: e.tensor_copy(sb[f"xbf{kc}"][:],
                                                   sb[f"xsb{kc}"][:]),
                    [f"xsb{kc}"], [f"xbf{kc}"], eng="gpsimd")
            favor_fm(s, "Wa", "bq", "qf", False)
            if DEBUG and s == 0:
                for dd in range(8):
                    dma("sync", dram["dbg_qf"][dd], sb[f"qf{dd}"][:], [f"qf{dd}"], [])
            for tcl in range(4):
                c = 4 * s + tcl
                if c > 0:
                    buf = f"schb{c % 2}"
                    dma("sync", sb[buf][0:64, :], dram["sch_in"][c - 1], [buf], [buf])
                    dma("sync", sb[buf][64:128, :], dram["sch_in"][c - 1], [buf], [buf])
                    vop(lambda e, buf=buf: e.tensor_tensor(
                        sb["srun"][:], sb["srun"][:], sb[buf][:], OP.add),
                        [buf, "srun"], ["srun"])
                    vop(lambda e: e.tensor_copy(sb["sbf"][:], sb["srun"][:]),
                        ["srun"], ["sbf"])
                for h in range(16):
                    d, base = h // 2, 64 * (h % 2)
                    mm(ps[3][0:128, 0:128],
                       sb[f"kf{d}"][base:base + 64, 128 * c:128 * c + 128],
                       sb[f"qf{d}"][base:base + 64, 128 * tcl:128 * tcl + 128],
                       True, True, [f"kf{d}", f"qf{d}", "ps3"], ["ps3"])
                    vop(lambda e: e.tensor_tensor(sb["a2m"][:], ps[3][0:128, 0:128],
                                                  sb["triu"][:], OP.mult),
                        ["ps3", "triu"], ["a2m"])
                    mm(ps[4][0:128, 0:65], sb["a2m"][:],
                       sb[f"vt{s}"][:, 1040 * tcl + 65 * h:1040 * tcl + 65 * h + 65],
                       True, False, ["a2m", f"vt{s}", "ps4"], ["ps4"])
                    mm(ps[4][0:128, 0:65],
                       sb[f"qf{d}"][base:base + 64, 128 * tcl:128 * tcl + 128],
                       sb["sbf"][base:base + 64, 65 * h:65 * h + 65],
                       False, True, [f"qf{d}", "sbf", "ps4"], ["ps4"])
                    if DEBUG and s == 0 and tcl == 0 and h < 2:
                        vop(lambda e, h=h: e.tensor_copy(sb["t1"][:, 0:65],
                                                         ps[4][0:128, 0:65]),
                            ["ps4"], ["t1"])
                        dma("sync", dram["dbg_num"][h], sb["t1"][:, 0:65],
                            ["t1"], [])
                        dma("sync", dram["dbg_a2m"][h], sb["a2m"][:],
                            ["a2m"], [])
                    vop(lambda e: e.tensor_scalar(sb["dcol"][:, 0:1],
                                                  ps[4][0:128, 64:65], ATTN_EPS,
                                                  None, OP.add),
                        ["ps4"], ["dcol"])
                    vop(lambda e: e.reciprocal(sb["dcol"][:, 1:2], sb["dcol"][:, 0:1]),
                        ["dcol"], ["dcol"])
                    vop(lambda e, h=h: e.tensor_scalar(
                        sb["atm"][:, 64 * h:64 * h + 64], ps[4][0:128, 0:64],
                        sb["dcol"][:, 1:2], None, OP.mult),
                        ["ps4", "dcol"], ["atm"])
                if DEBUG and s == 0 and tcl == 0:
                    dma("sync", dram["dbg_atm"], sb["atm"][:], ["atm"], [])
                for j in range(8):
                    r.op("tensor", lambda e, j=j: e.transpose(
                        ps[5][0:128, 0:128], sb["atm"][:, 128 * j:128 * j + 128],
                        sb["idf"][:]),
                        reads=["atm", "idf", "ps5"], writes=["ps5"])
                    vop(lambda e, j=j, tcl=tcl: e.tensor_copy(
                        sb[f"qs{j}"][:, 128 * tcl:128 * tcl + 128],
                        ps[5][0:128, 0:128]),
                        ["ps5"], [f"qs{j}"])
            for mc in range(8):
                pb = ps[mc % 2]
                for kc in range(8):
                    mm(pb[0:128, 0:512],
                       sb["Wb"][:, 1024 * kc + 128 * mc:1024 * kc + 128 * mc + 128],
                       sb[f"qs{kc}"][:, 0:512], kc == 0, kc == 7,
                       ["Wb", f"qs{kc}", f"ps{mc % 2}"], [f"ps{mc % 2}"])
                vop(lambda e, pb=pb, mc=mc: e.scalar_tensor_tensor(
                    sb[f"hy{mc}"][:], pb[0:128, 0:512], sb["bo"][:, mc:mc + 1],
                    sb[f"xsb{mc}"][:], OP.add, OP.add),
                    [f"ps{mc % 2}", "bo", f"xsb{mc}"], [f"hy{mc}"])
                dma("sync", dram["rmid"][s, mc], sb[f"hy{mc}"][:],
                    [f"hy{mc}"], ["rmid"])
                if DEBUG and s == 0:
                    dma("sync", dram["dbg_hy"][mc], sb[f"hy{mc}"][:],
                        [f"hy{mc}"], [])

        def ln_stats(src_tensors, src_tokens):
            for kc in range(8):
                mm(ps[6][0:1, 0:512], sb["onesc"][:, 0:1],
                   src_tensors[kc][:, 0:512], kc == 0, kc == 7,
                   ["onesc", src_tokens[kc], "ps6"], ["ps6"])
            for kc in range(8):
                vop(lambda e, kc=kc: e.activation(
                    sb["t1"][:, 0:512], src_tensors[kc][:, 0:512], AF.Square),
                    [src_tokens[kc]], ["t1"], eng="scalar")
                mm(ps[7][0:1, 0:512], sb["onesc"][:, 0:1],
                   sb["t1"][:, 0:512], kc == 0, kc == 7,
                   ["onesc", "t1", "ps7"], ["ps7"])
            vop(lambda e: e.tensor_scalar(sb["rowb"][0:1, :], ps[6][0:1, 0:512],
                                          1.0 / 1024.0, None, OP.mult),
                ["ps6"], ["rowb"])
            vop(lambda e: e.tensor_scalar(sb["rowb2"][0:1, :], ps[7][0:1, 0:512],
                                          1.0 / 1024.0, None, OP.mult),
                ["ps7"], ["rowb2"])
            vop(lambda e: e.tensor_tensor(sb["Ftile"][0:1, 0:512],
                                          sb["rowb"][0:1, :], sb["rowb"][0:1, :],
                                          OP.mult),
                ["rowb"], ["Ftile"])
            vop(lambda e: e.tensor_tensor(sb["rowb2"][0:1, :], sb["rowb2"][0:1, :],
                                          sb["Ftile"][0:1, 0:512], OP.subtract),
                ["rowb2", "Ftile"], ["rowb2"])
            vop(lambda e: e.activation(sb["rowb2"][0:1, :], sb["rowb2"][0:1, :],
                                       AF.Sqrt, bias=sb["cb"][0:1, 1:2], scale=1.0),
                ["rowb2", "cb"], ["rowb2"], eng="scalar")
            vop(lambda e: e.reciprocal(sb["rowb2"][0:1, :], sb["rowb2"][0:1, :]),
                ["rowb2"], ["rowb2"])
            vop(lambda e: e.tensor_tensor(sb["rowb"][0:1, :], sb["rowb"][0:1, :],
                                          sb["rowb2"][0:1, :], OP.mult),
                ["rowb", "rowb2"], ["rowb"])
            mm(ps[6][0:128, 0:512], sb["ones1f"][0:1, 0:128], sb["rowb2"][0:1, :],
               True, True, ["ones1f", "rowb2", "ps6"], ["ps6"])
            mm(ps[7][0:128, 0:512], sb["ones1f"][0:1, 0:128], sb["rowb"][0:1, :],
               True, True, ["ones1f", "rowb", "ps7"], ["ps7"])

        for s in range(NSLAB):
            for kc in range(8):
                dma("sync", sb[f"xsb{kc}"][:], dram["rmid"][s, kc],
                    [f"xsb{kc}", "rmid"], [f"xsb{kc}"])
            ln_stats([sb[f"xsb{kc}"] for kc in range(8)],
                     [f"xsb{kc}" for kc in range(8)])
            for kc in range(8):
                vop(lambda e, kc=kc: e.tensor_tensor(sb["t1"][:], sb[f"xsb{kc}"][:],
                                                     ps[6][0:128, 0:512], OP.mult),
                    [f"xsb{kc}", "ps6"], ["t1"])
                vop(lambda e: e.tensor_tensor(sb["t1"][:], sb["t1"][:],
                                              ps[7][0:128, 0:512], OP.subtract),
                    ["t1", "ps7"], ["t1"])
                vop(lambda e, kc=kc, s=s: e.tensor_scalar(
                    sb[f"kf{kc}"][:, 512 * s:512 * s + 512], sb["t1"][:],
                    sb["l1g"][:, kc:kc + 1], sb["l1b"][:, kc:kc + 1],
                    OP.mult, OP.add),
                    ["t1", "l1g", "l1b"], [f"kf{kc}"])
        for g in range(4):
            load_w_slice("Wa", "w1_b", 0, 1024, 1024 * g, 1024 * g + 1024)
            load_w_slice("Wb", "w2_b", 1024 * g, 1024 * g + 1024, 0, 1024)
            for s in range(NSLAB):
                for mc in range(8):
                    pb = ps[mc % 2]
                    for kc in range(8):
                        mm(pb[0:128, 0:512],
                           sb["Wa"][:, 1024 * kc + 128 * mc:1024 * kc + 128 * mc + 128],
                           sb[f"kf{kc}"][:, 512 * s:512 * s + 512], kc == 0, kc == 7,
                           ["Wa", f"kf{kc}", f"ps{mc % 2}"], [f"ps{mc % 2}"])
                    vop(lambda e, pb=pb, g=g, mc=mc: e.activation(
                        sb[f"qs{mc}"][:, 0:512], pb[0:128, 0:512], AF.Relu,
                        bias=sb["bf1"][:, 8 * g + mc:8 * g + mc + 1], scale=1.0),
                        [f"ps{mc % 2}", "bf1"], [f"qs{mc}"], eng="scalar")
                for mc2 in range(8):
                    pb = ps[2 + mc2 % 2]
                    for mc in range(8):
                        mm(pb[0:128, 0:512],
                           sb["Wb"][:, 1024 * mc + 128 * mc2:1024 * mc + 128 * mc2 + 128],
                           sb[f"qs{mc}"][:, 0:512], mc == 0, mc == 7,
                           ["Wb", f"qs{mc}", f"ps{2 + mc2 % 2}"], [f"ps{2 + mc2 % 2}"])
                    yt = f"vt{mc2 // 2}"
                    ycols = slice(2080 * (mc2 % 2) + 520 * 0 + 512 * s,
                                  2080 * (mc2 % 2) + 512 * s + 512)
                    if g == 0:
                        vop(lambda e, pb=pb, yt=yt, yc=ycols: e.tensor_copy(
                            sb[yt][:, yc], pb[0:128, 0:512]),
                            [f"ps{2 + mc2 % 2}"], [yt])
                    else:
                        vop(lambda e, pb=pb, yt=yt, yc=ycols: e.tensor_tensor(
                            sb[yt][:, yc], sb[yt][:, yc], pb[0:128, 0:512], OP.add),
                            [f"ps{2 + mc2 % 2}", yt], [yt])
        for s in range(NSLAB):
            for kc in range(8):
                dma("sync", sb[f"xsb{kc}"][:], dram["rmid"][s, kc],
                    [f"xsb{kc}", "rmid"], [f"xsb{kc}"])
            ln_stats([sb[f"xsb{kc}"] for kc in range(8)],
                     [f"xsb{kc}" for kc in range(8)])
            for kc in range(8):
                vop(lambda e, kc=kc: e.tensor_tensor(sb["t1"][:], sb[f"xsb{kc}"][:],
                                                     ps[6][0:128, 0:512], OP.mult),
                    [f"xsb{kc}", "ps6"], ["t1"])
                vop(lambda e: e.tensor_tensor(sb["t1"][:], sb["t1"][:],
                                              ps[7][0:128, 0:512], OP.subtract),
                    ["t1", "ps7"], ["t1"])
                vop(lambda e, kc=kc: e.tensor_scalar(
                    sb["t1"][:], sb["t1"][:], sb["l1g"][:, kc:kc + 1],
                    sb["bsum"][:, kc:kc + 1], OP.mult, OP.add),
                    ["t1", "l1g", "bsum"], ["t1"])
                yt = f"vt{kc // 2}"
                ycols = slice(2080 * (kc % 2) + 512 * s, 2080 * (kc % 2) + 512 * s + 512)
                vop(lambda e, yt=yt, yc=ycols, kc=kc: e.tensor_tensor(
                    sb[f"hy{kc}"][:], sb["t1"][:], sb[yt][:, yc], OP.add),
                    ["t1", yt], [f"hy{kc}"])
            ln_stats([sb[f"hy{kc}"] for kc in range(8)],
                     [f"hy{kc}" for kc in range(8)])
            for kc in range(8):
                vop(lambda e, kc=kc: e.tensor_tensor(sb["t1"][:], sb[f"hy{kc}"][:],
                                                     ps[6][0:128, 0:512], OP.mult),
                    [f"hy{kc}", "ps6"], ["t1"])
                vop(lambda e: e.tensor_tensor(sb["t1"][:], sb["t1"][:],
                                              ps[7][0:128, 0:512], OP.subtract),
                    ["t1", "ps7"], ["t1"])
                vop(lambda e, kc=kc: e.tensor_scalar(
                    sb[f"xsb{kc}"][:], sb["t1"][:], sb["l2g"][:, kc:kc + 1],
                    sb["l2b"][:, kc:kc + 1], OP.mult, OP.add),
                    ["t1", "l2g", "l2b"], [f"xsb{kc}"])
                dma("sync", dram["xmid"][s, kc], sb[f"xsb{kc}"][:],
                    [f"xsb{kc}"], ["xmid"])
                if has_a:
                    dma("sync", dram["xout"][kc, s], sb[f"xsb{kc}"][:],
                        [f"xsb{kc}"], [])

    def rhs_prep():
        for kc in range(8):
            dma("sync", sb[f"rhsf{kc}"][:], dram["rhs"][128 * kc:128 * kc + 128, :],
                [f"kf{kc}"], [f"kf{kc}"])
        for nc2 in range(8):
            for kc in range(8):
                r.op("tensor", lambda e, kc=kc, nc2=nc2: e.transpose(
                    ps[5][0:128, 0:128],
                    sb[f"rhsf{kc}"][:, 128 * nc2:128 * nc2 + 128], sb["idf"][:]),
                    reads=[f"kf{kc}", "idf", "ps5"], writes=["ps5"])
                vop(lambda e, kc=kc: e.tensor_copy(
                    sb["etm"][:, 128 * kc:128 * kc + 128], ps[5][0:128, 0:128]),
                    ["ps5"], ["etm"])
            vop(lambda e: e.tensor_reduce(sb["st8"][:, 0:1], sb["etm"][:],
                                          axis=AX.X, op=OP.max,
                                          apply_absolute_value=True),
                ["etm"], ["st8"])
            vop(lambda e: e.tensor_scalar(sb["st8"][:, 1:2], sb["st8"][:, 0:1],
                                          1e-12, INV127, OP.max, OP.mult),
                ["st8"], ["st8"])
            vop(lambda e, nc2=nc2: e.tensor_copy(sb["scall"][:, nc2:nc2 + 1],
                                                 sb["st8"][:, 1:2]),
                ["st8"], ["scall"])
            vop(lambda e, nc2=nc2: e.reciprocal(sb["rcall"][:, nc2:nc2 + 1],
                                                sb["st8"][:, 1:2]),
                ["st8"], ["rcall"])
        for which, src, dst in (("s", "scall", "scolbc"), ("r", "rcall", "rcolbc")):
            for nc2 in range(8):
                mm(ps[5][0:1, 0:128], sb[src][:, nc2:nc2 + 1], sb["idf"][:],
                   True, True, [src, "idf", "ps5"], ["ps5"])
                vop(lambda e: e.tensor_copy(sb["rowb"][0:1, 0:128], ps[5][0:1, 0:128]),
                    ["ps5"], ["rowb"])
                mm(ps[6][0:128, 0:128], sb["ones1f"][0:1, 0:128],
                   sb["rowb"][0:1, 0:128], True, True,
                   ["ones1f", "rowb", "ps6"], ["ps6"])
                vop(lambda e, dst=dst, nc2=nc2: e.tensor_copy(
                    sb[dst][:, 128 * nc2:128 * nc2 + 128], ps[6][0:128, 0:128]),
                    ["ps6"], [dst])
        for kc in range(8):
            vop(lambda e, kc=kc: e.tensor_tensor(sb["atm"][:], sb[f"rhsf{kc}"][:],
                                                 sb["rcolbc"][:], OP.mult),
                [f"kf{kc}", "rcolbc"], ["atm"])
            vop(lambda e: e.tensor_copy(sb["qi"][:], sb["atm"][:]), ["atm"], ["qi"])
            vop(lambda e, kc=kc: e.tensor_copy(sb[f"qrhs{kc}"][:], sb["qi"][:]),
                ["qi"], [f"qrhs{kc}"])

    def ln_plain_tm(src_key, out_key, out_tensor, src_tensor):
        vop(lambda e: e.tensor_reduce(sb["st8"][:, 2:3], src_tensor[:],
                                      axis=AX.X, op=OP.add),
            [src_key], ["st8"])
        vop(lambda e: e.activation(sb["atm"][:], src_tensor[:], AF.Square),
            [src_key], ["atm"], eng="scalar")
        vop(lambda e: e.tensor_reduce(sb["st8"][:, 3:4], sb["atm"][:],
                                      axis=AX.X, op=OP.add),
            ["atm"], ["st8"])
        vop(lambda e: e.tensor_scalar(sb["st8"][:, 4:5], sb["st8"][:, 2:3],
                                      1.0 / 1024.0, None, OP.mult),
            ["st8"], ["st8"])
        vop(lambda e: e.tensor_scalar(sb["st8"][:, 5:6], sb["st8"][:, 3:4],
                                      1.0 / 1024.0, None, OP.mult),
            ["st8"], ["st8"])
        vop(lambda e: e.tensor_tensor(sb["st8"][:, 6:7], sb["st8"][:, 4:5],
                                      sb["st8"][:, 4:5], OP.mult),
            ["st8"], ["st8"])
        vop(lambda e: e.tensor_tensor(sb["st8"][:, 6:7], sb["st8"][:, 5:6],
                                      sb["st8"][:, 6:7], OP.subtract),
            ["st8"], ["st8"])
        vop(lambda e: e.activation(sb["st8"][:, 7:8], sb["st8"][:, 6:7],
                                   AF.Sqrt, bias=sb["cb"][:, 1:2], scale=1.0),
            ["st8", "cb"], ["st8"], eng="scalar")
        vop(lambda e: e.reciprocal(sb["st8"][:, 8:9], sb["st8"][:, 7:8]),
            ["st8"], ["st8"])
        vop(lambda e: e.tensor_scalar(out_tensor[:], src_tensor[:],
                                      sb["st8"][:, 4:5], sb["st8"][:, 8:9],
                                      OP.subtract, OP.mult),
            [src_key, "st8"], [out_key])

    def final_block():
        for tc in range(16):
            s, tcl = tc // 4, tc % 4
            for kc in range(8):
                dma("sync", sb["atm"][:, 128 * kc:128 * kc + 128],
                    dram["xmid"][s, kc][:, 128 * tcl:128 * tcl + 128],
                    ["atm", "xmid"], ["atm"])
            dma("sync", sb["x0t"][:], dram["x0"][128 * tc:128 * tc + 128, :],
                ["x0t"], ["x0t"])
            for kc in range(8):
                r.op("tensor", lambda e, kc=kc: e.transpose(
                    ps[5][0:128, 0:128], sb["atm"][:, 128 * kc:128 * kc + 128],
                    sb["idf"][:]),
                    reads=["atm", "idf", "ps5"], writes=["ps5"])
                vop(lambda e, kc=kc: e.tensor_tensor(
                    sb["etm"][:, 128 * kc:128 * kc + 128], ps[5][0:128, 0:128],
                    sb["x0t"][:, 128 * kc:128 * kc + 128], OP.add),
                    ["ps5", "x0t"], ["etm"])
            ln_plain_tm("etm", "x1t", sb["x1t"], sb["etm"])
            vop(lambda e: e.tensor_reduce(sb["st8"][:, 0:1], sb["x1t"][:],
                                          axis=AX.X, op=OP.max,
                                          apply_absolute_value=True),
                ["x1t"], ["st8"])
            vop(lambda e: e.tensor_scalar(sb["st8"][:, 1:2], sb["st8"][:, 0:1],
                                          1e-12, INV127, OP.max, OP.mult),
                ["st8"], ["st8"])
            vop(lambda e: e.reciprocal(sb["st8"][:, 9:10], sb["st8"][:, 1:2]),
                ["st8"], ["st8"])
            vop(lambda e: e.tensor_scalar(sb["atm"][:], sb["x1t"][:],
                                          sb["st8"][:, 9:10], None, OP.mult),
                ["x1t", "st8"], ["atm"])
            vop(lambda e: e.tensor_copy(sb["qi"][:], sb["atm"][:]), ["atm"], ["qi"])
            vop(lambda e: e.tensor_copy(sb["atm"][:], sb["qi"][:]), ["qi"], ["atm"])
            for j in range(8):
                r.op("tensor", lambda e, j=j: e.transpose(
                    ps[5][0:128, 0:128], sb["atm"][:, 128 * j:128 * j + 128],
                    sb["idf"][:]),
                    reads=["atm", "idf", "ps5"], writes=["ps5"])
                vop(lambda e, j=j: e.tensor_copy(
                    sb["qlf"][:, 128 * j:128 * j + 128], ps[5][0:128, 0:128]),
                    ["ps5"], ["qlf"])
            for n2 in range(2):
                pb = ps[n2]
                for kc in range(8):
                    mm(pb[0:128, 0:512], sb["qlf"][:, 128 * kc:128 * kc + 128],
                       sb[f"qrhs{kc}"][:, 512 * n2:512 * n2 + 512], kc == 0, kc == 7,
                       ["qlf", f"qrhs{kc}", f"ps{n2}"], [f"ps{n2}"])
                vop(lambda e, pb=pb, n2=n2: e.scalar_tensor_tensor(
                    sb[f"hy{n2}"][:], pb[0:128, 0:512], sb["st8"][:, 1:2],
                    sb["scolbc"][:, 512 * n2:512 * n2 + 512], OP.mult, OP.mult),
                    [f"ps{n2}", "st8", "scolbc"], [f"hy{n2}"])
                vop(lambda e, n2=n2: e.tensor_tensor(
                    sb[f"hy{n2}"][:], sb[f"hy{n2}"][:],
                    sb["x1t"][:, 512 * n2:512 * n2 + 512], OP.add),
                    [f"hy{n2}", "x1t"], [f"hy{n2}"])
            vop(lambda e: e.tensor_reduce(sb["st8"][:, 2:3], sb["hy0"][:],
                                          axis=AX.X, op=OP.add),
                ["hy0"], ["st8"])
            vop(lambda e: e.tensor_reduce(sb["st8"][:, 3:4], sb["hy1"][:],
                                          axis=AX.X, op=OP.add),
                ["hy1"], ["st8"])
            vop(lambda e: e.tensor_tensor(sb["st8"][:, 2:3], sb["st8"][:, 2:3],
                                          sb["st8"][:, 3:4], OP.add),
                ["st8"], ["st8"])
            vop(lambda e: e.activation(sb["atm"][:, 0:512], sb["hy0"][:], AF.Square),
                ["hy0"], ["atm"], eng="scalar")
            vop(lambda e: e.activation(sb["atm"][:, 512:1024], sb["hy1"][:], AF.Square),
                ["hy1"], ["atm"], eng="scalar")
            vop(lambda e: e.tensor_reduce(sb["st8"][:, 3:4], sb["atm"][:],
                                          axis=AX.X, op=OP.add),
                ["atm"], ["st8"])
            vop(lambda e: e.tensor_scalar(sb["st8"][:, 4:5], sb["st8"][:, 2:3],
                                          1.0 / 1024.0, None, OP.mult),
                ["st8"], ["st8"])
            vop(lambda e: e.tensor_scalar(sb["st8"][:, 5:6], sb["st8"][:, 3:4],
                                          1.0 / 1024.0, None, OP.mult),
                ["st8"], ["st8"])
            vop(lambda e: e.tensor_tensor(sb["st8"][:, 6:7], sb["st8"][:, 4:5],
                                          sb["st8"][:, 4:5], OP.mult),
                ["st8"], ["st8"])
            vop(lambda e: e.tensor_tensor(sb["st8"][:, 6:7], sb["st8"][:, 5:6],
                                          sb["st8"][:, 6:7], OP.subtract),
                ["st8"], ["st8"])
            vop(lambda e: e.activation(sb["st8"][:, 7:8], sb["st8"][:, 6:7],
                                       AF.Sqrt, bias=sb["cb"][:, 1:2], scale=1.0),
                ["st8", "cb"], ["st8"], eng="scalar")
            vop(lambda e: e.reciprocal(sb["st8"][:, 8:9], sb["st8"][:, 7:8]),
                ["st8"], ["st8"])
            for n2 in range(2):
                vop(lambda e, n2=n2: e.tensor_scalar(
                    sb["x1t"][:, 512 * n2:512 * n2 + 512], sb[f"hy{n2}"][:],
                    sb["st8"][:, 4:5], sb["st8"][:, 8:9], OP.subtract, OP.mult),
                    [f"hy{n2}", "st8"], ["x1t"])
            dma("sync", dram["out"][128 * tc:128 * tc + 128, :], sb["x1t"][:],
                ["x1t"], [])

    if mode == "DBGC":
        vop(lambda e: e.memset(sb["srun"][:], 0.0), [], ["srun"])
        for j in range(8):
            dma("sync", sb[f"kf{j}"][0:64, 0:1040], dram["gst"][j],
                [f"kf{j}"], [f"kf{j}"])
            dma("sync", sb[f"kf{j}"][64:128, 0:1040], dram["gst"][j],
                [f"kf{j}"], [f"kf{j}"])
        for j in range(8):
            vop(lambda e, j=j: e.scalar_tensor_tensor(
                sb["srun"][:], sb[f"kf{j}"][:, 0:1040], sb["pm"][:, j:j + 1],
                sb["srun"][:], OP.mult, OP.add),
                [f"kf{j}", "pm", "srun"], ["srun"])
        vop(lambda e: e.tensor_copy(sb["sbf"][:], sb["srun"][:]), ["srun"], ["sbf"])
        dma("sync", dram["dbgc_sbf"], sb["sbf"][:], ["sbf"], [])
        dma("sync", dram["dbgc_kf0"], sb["kf0"][:, 0:1040], ["kf0"], [])
    if has_b and mode != "DBGC":
        phase_b()
    if has_a:
        phase_a(la)
    if has_f:
        r.barrier()
        rhs_prep()
        final_block()

    sems = {}
    from contextlib import ExitStack
    stk = ExitStack()
    for e in ENGINES:
        sems[e] = stk.enter_context(nc.semaphore(f"sem_{e}"))
    block = stk.enter_context(nc.Block())
    r.emit(block, sems)
    stk.close()
    nc._rec_counts = dict(r.cnt)
    return nc


def _consts():
    idf = np.eye(128, dtype=np.float32)
    idb = np.eye(128).astype(ml_dtypes.bfloat16)
    tk, tq = np.meshgrid(np.arange(128), np.arange(128), indexing="ij")
    triu = (tq >= tk).astype(np.float32)
    e16 = np.zeros((128, 128), np.float32)
    for c in range(8):
        e16[0:64, 16 * c + 2 * c] = 1.0
        e16[64:128, 16 * c + 2 * c + 1] = 1.0
    sel = np.zeros((16, 1024), np.float32)
    for d in range(8):
        sel[2 * d, 128 * d:128 * d + 64] = 1.0
        sel[2 * d + 1, 128 * d + 64:128 * d + 128] = 1.0
    ones1f = np.ones((1, 128), np.float32)
    return dict(idf=idf, idb=idb, triu=triu,
                e16=e16.astype(ml_dtypes.bfloat16),
                sel=sel.astype(ml_dtypes.bfloat16), ones1f=ones1f)


def _fm8(v):
    return np.ascontiguousarray(np.asarray(v).reshape(8, 128).T)


def _fm32(v):
    return np.ascontiguousarray(np.asarray(v).reshape(32, 128).T)


def kernel(L0_attn_x, L0_attn_Wq, L0_attn_bq, L0_attn_Wk, L0_attn_bk,
           L0_attn_Wv, L0_attn_bv, L0_attn_Wo, L0_attn_bo, L0_attn_omega,
           L0_attn_ln1_g, L0_attn_ln1_b, L0_attn_ln2_g, L0_attn_ln2_b,
           L0_attn_W1, L0_attn_bf1, L0_attn_W2, L0_attn_bf2, L0_ffn_rhs):
    x = np.ascontiguousarray(np.asarray(L0_attn_x, dtype=np.float32))
    Wq = np.asarray(L0_attn_Wq, np.float32)
    Wk = np.asarray(L0_attn_Wk, np.float32)
    Wv = np.asarray(L0_attn_Wv, np.float32)
    Wo = np.asarray(L0_attn_Wo, np.float32)
    W1 = np.asarray(L0_attn_W1, np.float32)
    W2 = np.asarray(L0_attn_W2, np.float32)
    bq = np.asarray(L0_attn_bq, np.float32)
    bk = np.asarray(L0_attn_bk, np.float32)
    bv = np.asarray(L0_attn_bv, np.float32)
    bo = np.asarray(L0_attn_bo, np.float32)
    bf1 = np.asarray(L0_attn_bf1, np.float32)
    bf2 = np.asarray(L0_attn_bf2, np.float32)
    l1g = np.asarray(L0_attn_ln1_g, np.float32)
    l1b = np.asarray(L0_attn_ln1_b, np.float32)
    l2g = np.asarray(L0_attn_ln2_g, np.float32)
    l2b = np.asarray(L0_attn_ln2_b, np.float32)
    om = np.asarray(L0_attn_omega, np.float32)
    rhs = np.ascontiguousarray(np.asarray(L0_ffn_rhs, np.float32))

    consts = _consts()
    x0s = [np.ascontiguousarray(x[c // 4, 2048 * (c % 4):2048 * (c % 4) + 2048, :])
           for c in range(NCORE)]
    pmasks = []
    for c in range(NCORE):
        g0 = (c // 4) * 4
        m = np.zeros(8, np.float32)
        m[g0:c] = 1.0
        pmasks.append(np.broadcast_to(m, (128, 8)).copy())

    def layer_vecs(l):
        return dict(
            bq_b=_fm8(bq[l]), bo_b=_fm8(bo[l]), ln1g_b=_fm8(l1g[l]),
            ln1b_b=_fm8(l1b[l]), ln2g_b=_fm8(l2g[l]), ln2b_b=_fm8(l2b[l]),
            bf1_b=_fm32(bf1[l]), bf2_b=_fm8(bf2[l]),
            wq_b=np.ascontiguousarray(Wq[l]), wo_b=np.ascontiguousarray(Wo[l]),
            w1_b=np.ascontiguousarray(W1[l]), w2_b=np.ascontiguousarray(W2[l]),
            om_b=np.ascontiguousarray(np.vstack([om[l], om[l]])))

    def layer_a_in(l):
        return dict(
            wk_a=np.ascontiguousarray(Wk[l]), wv_a=np.ascontiguousarray(Wv[l]),
            bk_a=_fm8(bk[l]), bv_a=np.ascontiguousarray(bv[l]).reshape(1, D),
            om_a=np.ascontiguousarray(np.vstack([om[l], om[l]])))

    state = [dict() for _ in range(NCORE)]

    nc0 = build_launch("A", 0, None)
    in_maps = []
    for c in range(NCORE):
        m = dict(consts)
        m.update(layer_a_in(0))
        m["x0"] = x0s[c]
        in_maps.append(m)
    res = run_bass_kernel_spmd(nc0, in_maps, core_ids=list(range(NCORE)))
    for c in range(NCORE):
        state[c] = {k: res.results[c][k]
                    for k in ("xout", "kf_out", "vt_out", "sch_out", "stot")}

    for k in (1, 2, 3):
        lb = k - 1
        la = k if k < 3 else None
        mode = "BA" if k < 3 else "BF"
        nck = build_launch(mode, la, lb)
        gst = np.stack([state[c]["stot"] for c in range(NCORE)])
        in_maps = []
        for c in range(NCORE):
            m = dict(consts)
            m.update(layer_vecs(lb))
            m["xin"] = state[c]["xout"]
            m["kf_in"] = state[c]["kf_out"]
            m["vt_in"] = state[c]["vt_out"]
            m["sch_in"] = state[c]["sch_out"]
            m["gst"] = gst
            m["pmask"] = pmasks[c]
            if mode == "BA":
                m.update(layer_a_in(la))
            else:
                m["x0"] = x0s[c]
                m["rhs"] = rhs
            in_maps.append(m)
        res = run_bass_kernel_spmd(nck, in_maps, core_ids=list(range(NCORE)))
        if mode == "BA":
            ns = []
            for c in range(NCORE):
                ns.append({kk: res.results[c][kk]
                           for kk in ("xout", "kf_out", "vt_out", "sch_out", "stot")})
            state = ns
        else:
            outs = [res.results[c]["out"] for c in range(NCORE)]

    full = np.empty((2, 8192, D), np.float32)
    for c in range(NCORE):
        full[c // 4, 2048 * (c % 4):2048 * (c % 4) + 2048, :] = outs[c]
    return full


# version 03
# speedup vs baseline: 1.0341x; 1.0341x over previous
import numpy as np
import ml_dtypes
import concourse.bass as bass
import concourse.mybir as mybir
from concourse.bass_utils import run_bass_kernel_spmd

F32 = mybir.dt.float32
BF16 = mybir.dt.bfloat16
I32 = mybir.dt.int32
AF = mybir.ActivationFunctionType
OP = mybir.AluOpType
AX = mybir.AxisListType

NCORE = 8
T = 2048
D = 1024
H = 16
FF = 4096
NL = 3
C = 128
NCH = 16
SLAB = 512
NSLAB = 4
CQ = 0.0625
LN8 = 2.0794415416798357
CS = 0.35355339059327373
LN_EPS = 1e-5
ATTN_EPS = 1e-6
INV127 = 1.0 / 127.0

ENGINES = ("tensor", "vector", "scalar", "gpsimd", "sync")
DEBUG = False


class Rec:

    SPACED = ("vector", "gpsimd", "scalar")

    def __init__(self):
        self.items = []
        self.cnt = {e: 0 for e in ENGINES}
        self.opi = {e: 0 for e in ENGINES}
        self.writer = {}
        self.readers = {}

    def barrier(self):
        snap = dict(self.cnt)
        self._pend = {e: {d: v for d, v in snap.items() if d != e and v > 0}
                      for e in ENGINES}

    def op(self, eng, fn, reads=(), writes=(), dma=False, big=False):
        inc = 16 if dma else 1
        waits = {}
        pend = getattr(self, "_pend", None)
        if pend and pend.get(eng):
            waits.update(pend.pop(eng))

        def need(dep):
            deng, dcnt = dep[0], dep[1]
            dep_dma = dep[3] if len(dep) > 3 else False
            if deng == eng and not dep_dma:
                return
            if waits.get(deng, 0) < dcnt:
                waits[deng] = dcnt

        if eng in self.SPACED:
            mind = 99
            for t in reads:
                w = self.writer.get(t)
                if w and w[0] == eng and not w[4]:
                    mind = min(mind, self.opi[eng] - w[2])
            npad = max(0, 3 - mind)
            for _ in range(npad):
                if eng == "scalar":
                    sfn = lambda e: e.memzero(self._junk[:])
                else:
                    sfn = lambda e: e.memset(self._junk[:], 0.0)
                self.cnt[eng] += 1
                self.opi[eng] += 1
                self.items.append((eng, sfn, {}, 1))

        for t in reads:
            w = self.writer.get(t)
            if w:
                need(w)
        for t in writes:
            w = self.writer.get(t)
            if w:
                need(w)
            for rd in self.readers.get(t, ()):
                need(rd)
        self.cnt[eng] += inc
        self.opi[eng] += 1
        here = (eng, self.cnt[eng], self.opi[eng], dma, big or dma)
        for t in reads:
            self.readers.setdefault(t, []).append(here)
        for t in writes:
            self.writer[t] = here
            self.readers[t] = []
        self.items.append((eng, fn, waits, inc))

    def emit(self, block, sems):
        per = {e: [] for e in ENGINES}
        for eng, fn, waits, inc in self.items:
            per[eng].append((fn, waits, inc))
        finals = dict(self.cnt)
        hw = {e: {d: 0 for d in ENGINES} for e in ENGINES}
        fence = {e: set() for e in ENGINES}
        for eng, fn, waits, inc in self.items:
            for d, v in waits.items():
                fence[d].add(v)

        def mk(ename, items):
            fvals = sorted(fence[ename])
            has_dma = any(inc == 16 for _, _, inc in items)

            def body(eng):
                cnt = 0
                selfhw = 0
                fi = 0
                nonlocal fvals
                for fn, waits, inc in items:
                    if has_dma:
                        fmax = 0
                        while fi < len(fvals) and fvals[fi] <= cnt:
                            fmax = fvals[fi]
                            fi += 1
                        if fmax > selfhw:
                            eng.wait_ge(sems[ename], fmax)
                            selfhw = fmax
                    for d, v in sorted(waits.items()):
                        if hw[ename][d] < v:
                            eng.wait_ge(sems[d], v)
                            hw[ename][d] = v
                    fn(eng).then_inc(sems[ename], inc)
                    cnt += inc
                for d in ENGINES:
                    if d != ename and finals[d] > 0 and hw[ename][d] < finals[d]:
                        eng.wait_ge(sems[d], finals[d])
                if has_dma and finals[ename] > selfhw:
                    eng.wait_ge(sems[ename], finals[ename])
            return body

        for ename in ENGINES:
            getattr(block, ename)(mk(ename, per[ename]))


SBUF_BASE = 16512


class Alloc:
    def __init__(self, nc):
        self.nc = nc
        self.off = SBUF_BASE
        self.offs = {}

    def at(self, name, shape, dtype, offset):
        return self.nc.alloc_sbuf_tensor_at(name, shape, dtype, offset=offset)

    def buf(self, name, shape, dtype):
        per = int(np.prod(shape[1:])) * mybir.dt.size(dtype)
        per = (per + 31) // 32 * 32
        t = self.nc.alloc_sbuf_tensor_at(name, shape, dtype, offset=self.off)
        self.offs[name] = self.off
        self.off += per
        return t

    def mark(self):
        return self.off


def ceil32(x):
    return (x + 31) // 32 * 32


def build_launch(mode, la, lb):
    nc = bass.Bass("TRN2", target_bir_lowering=False, debug=False)
    r = Rec()

    dram = {}

    def din(name, shape, dtype):
        dram[name] = nc.dram_tensor(name, shape, dtype, kind="ExternalInput").ap()
        return dram[name]

    def dout(name, shape, dtype):
        dram[name] = nc.dram_tensor(name, shape, dtype, kind="ExternalOutput").ap()
        return dram[name]

    din("idf", [128, 128], F32)
    din("idb", [128, 128], BF16)
    din("triu", [128, 128], F32)
    din("e16", [128, 128], BF16)
    din("sel", [16, 1024], BF16)
    din("ones1f", [1, 128], F32)

    has_a = mode in ("A", "BA")
    has_b = mode in ("BA", "BF", "DBGC")
    has_f = mode == "BF"
    if mode == "DBGC":
        has_a = False

    if mode == "A":
        din("x0", [T, D], F32)
    else:
        din("xin", [8, NSLAB, 128, SLAB], F32)
    if mode == "DBGC":
        dout("dbgc_sbf", [128, 1040], BF16)
        dout("dbgc_kf0", [128, 1040], BF16)
    if has_a:
        din("wk_a", [D, D], F32)
        din("wv_a", [D, D], F32)
        din("bk_a", [128, 8], F32)
        din("bv_a", [1, D], F32)
        din("om_a", [128, 32], F32)
        dout("xout", [8, NSLAB, 128, SLAB], F32)
        dout("kf_out", [8, 128, T], BF16)
        dout("vt_out", [NSLAB, 128, 4160], BF16)
        dout("sch_out", [NCH, 64, 1040], BF16)
        dout("stot", [64, 1040], BF16)
    if has_b:
        din("kf_in", [8, 128, T], BF16)
        din("vt_in", [NSLAB, 128, 4160], BF16)
        din("sch_in", [NCH, 64, 1040], BF16)
        din("gst", [8, 64, 1040], BF16)
        din("pmask", [128, 8], F32)
        din("wq_b", [D, D], F32)
        din("wo_b", [D, D], F32)
        din("w1_b", [D, FF], F32)
        din("w2_b", [FF, D], F32)
        din("bq_b", [128, 8], F32)
        din("bo_b", [128, 8], F32)
        din("ln1g_b", [128, 8], F32)
        din("ln1b_b", [128, 8], F32)
        din("ln2g_b", [128, 8], F32)
        din("ln2b_b", [128, 8], F32)
        din("bf1_b", [128, 32], F32)
        din("bf2_b", [128, 8], F32)
        din("om_b", [128, 32], F32)
        dram["rmid"] = nc.dram_tensor("rmid", [NSLAB, 8, 128, SLAB], F32).ap()
        dram["xmid"] = nc.dram_tensor("xmid", [NSLAB, 8, 128, SLAB], F32).ap()
    if has_f:
        din("x0", [T, D], F32)
        din("rhs", [D, D], F32)
        dout("out", [T, D], F32)
    if DEBUG and has_b:
        dout("dbg_qf", [8, 128, SLAB], BF16)
        dout("dbg_sbf", [128, 1040], BF16)
        dout("dbg_atm", [128, 1024], F32)
        dout("dbg_hy", [8, 128, SLAB], F32)
        dout("dbg_num", [2, 128, 65], F32)
        dout("dbg_a2m", [2, 128, 128], BF16)

    al = Alloc(nc)
    sb = {}

    def B(name, shape, dtype):
        sb[name] = al.buf(name, shape, dtype)
        return sb[name]

    for i in range(8):
        B(f"xsb{i}", [128, SLAB], F32)
    for i in range(8):
        B(f"xbf{i}", [128, SLAB], BF16)
    B("Wa", [128, 8192], BF16)
    B("Wb", [128, 8192], BF16)
    kf_off = al.mark()
    for i in range(8):
        B(f"kf{i}", [128, T], BF16)
    vt_off = al.mark()
    for i in range(NSLAB):
        B(f"vt{i}", [128, 4160], BF16)
    for i in range(8):
        B(f"qs{i}", [128, SLAB], BF16)
    for i in range(8):
        B(f"qf{i}", [128, SLAB], BF16)
    B("srun", [128, 1040], F32)
    B("sbf", [128, 1040], BF16)
    B("schb0", [128, 1040], BF16)
    B("schb1", [128, 1040], BF16)
    B("atm", [128, 1024], F32)
    B("etm", [128, 1024], F32)
    B("escr", [128, SLAB], F32)
    B("sq0", [128, SLAB], BF16)
    B("sq1", [128, SLAB], BF16)
    B("t1", [128, SLAB], F32)
    for i in range(8):
        B(f"hy{i}", [128, SLAB], F32)
    B("a2m", [128, 128], BF16)
    B("cb", [128, 4], F32)
    r._junk = B("junk", [128, 64], F32)
    B("onesc", [128, 1], F32)
    B("oq", [128, 128], BF16)
    B("omsb", [128, 32], F32)
    B("dcol", [128, 2], F32)
    B("st8", [128, 12], F32)
    B("rowb", [1, SLAB], F32)
    B("rowb2", [1, SLAB], F32)
    if has_a:
        B("bvbc", [128, 1024], F32)
        B("stotsb", [128, 1040], F32)
        B("bvrow", [1, 1024], F32)
    B("bsum", [128, 8], F32)
    B("idf", [128, 128], F32)
    B("idb", [128, 128], BF16)
    B("triu", [128, 128], F32)
    B("e16", [128, 128], BF16)
    B("sel", [16, 1024], BF16)
    B("ones1f", [1, 128], F32)
    B("pm", [128, 8], F32)
    B("Ftile", [16, SLAB], F32)
    B("Fbf", [16, SLAB], BF16)
    for nm in ("bq", "bo", "l1g", "l1b", "l2g", "l2b", "bf2", "bk"):
        B(nm, [128, 8], F32)
    B("bf1", [128, 32], F32)
    if has_f:
        for i in range(8):
            sb[f"qrhs{i}"] = al.at(f"qrhs{i}", [128, 1024], BF16,
                                   offset=al.offs[f"xsb{i}"])
        sb["scolbc"] = al.at("scolbc", [128, 1024], F32, offset=al.offs["xbf0"])
        sb["rcolbc"] = al.at("rcolbc", [128, 1024], F32, offset=al.offs["xbf4"])
        B("scall", [128, 8], F32)
        B("rcall", [128, 8], F32)
        for i in range(8):
            sb[f"rhsf{i}"] = al.at(f"rhsf{i}", [128, 1024], F32,
                                   offset=kf_off + i * ceil32(T * 2))
        sb["x1t"] = al.at("x1t", [128, 1024], F32, offset=al.offs["qf0"])
        sb["qi"] = al.at("qi", [128, 1024], I32, offset=al.offs["qf4"])
        sb["qlf"] = al.at("qlf", [128, 1024], BF16, offset=al.offs["escr"])
        sb["x0t"] = al.at("x0t", [128, 1024], F32, offset=al.offs["srun"])

    assert al.off <= SBUF_BASE + 208 * 1024, f"SBUF over budget: {al.off}"

    ps = [nc.alloc_psum_tensor(f"ps{i}", [128, 512], F32) for i in range(8)]

    def dma(q, out_ap, in_ap, reads, writes):
        r.op(q, lambda e, o=out_ap, i=in_ap: e.dma_start(out=o, in_=i),
             reads=reads, writes=writes, dma=True)

    def mm(out_ap, lh, rh, st, sp, reads, writes, tile_position=None):
        r.op("tensor",
             lambda e, o=out_ap, a=lh, b=rh, s=st, p=sp, tp=tile_position:
             e.matmul(o, a, b, start=s, stop=p, tile_position=tp),
             reads=reads, writes=writes)

    def vop(fn, reads, writes, eng="vector", big=True):
        r.op(eng, fn, reads=reads, writes=writes, big=big)

    for nm, key in (("idf", "idf"), ("idb", "idb"), ("triu", "triu"),
                    ("e16", "e16"), ("sel", "sel"), ("ones1f", "ones1f")):
        dma("sync", sb[nm][:], dram[key], [], [nm])
    if has_b:
        dma("sync", sb["pm"][:], dram["pmask"], [], ["pm"])
    vop(lambda e: e.memset(sb["onesc"][:], 1.0), [], ["onesc"])
    vop(lambda e: e.memset(sb["cb"][:, 0:1], -LN8), [], ["cb"])
    vop(lambda e: e.memset(sb["cb"][:, 1:2], LN_EPS), [], ["cb"])
    vop(lambda e: e.memset(sb["cb"][:, 2:3], ATTN_EPS), [], ["cb"])

    def load_layer_vecs(which):
        if which == "B":
            for nm, key in (("bq", "bq_b"), ("bo", "bo_b"), ("l1g", "ln1g_b"),
                            ("l1b", "ln1b_b"), ("l2g", "ln2g_b"), ("l2b", "ln2b_b"),
                            ("bf2", "bf2_b")):
                dma("sync", sb[nm][:], dram[key], [], [nm])
            dma("sync", sb["bf1"][:], dram["bf1_b"], [], ["bf1"])
            vop(lambda e: e.tensor_tensor(sb["bsum"][:], sb["l1b"][:], sb["bf2"][:], OP.add),
                ["l1b", "bf2"], ["bsum"])
        else:
            dma("sync", sb["bk"][:], dram["bk_a"], [], ["bk"])
            dma("sync", sb["bvrow"][:], dram["bv_a"], [], ["bvrow"])

    def build_oq(omkey):
        dma("sync", sb["omsb"][:], dram[omkey], [], ["om"])
        vop(lambda e: e.memset(sb["oq"][:], 0.0), [], ["oq"])
        for (r0, c0, sc) in ((0, 0, CS), (0, 32, -CS), (64, 64, CS), (64, 96, -CS)):
            vop(lambda e, r0=r0, c0=c0, sc=sc:
                e.tensor_scalar(sb["oq"][r0:r0 + 64, c0:c0 + 32],
                                sb["omsb"][r0:r0 + 64, 0:32], sc, None, OP.mult),
                ["om", "oq"], ["oq"], big=False)

    def build_bvbc():
        for n2 in range(2):
            mm(ps[2][0:128, 0:512], sb["ones1f"][0:1, 0:128],
               sb["bvrow"][0:1, 512 * n2:512 * n2 + 512], True, True,
               ["ones1f", "bvrow", "ps2"], ["ps2"])
            vop(lambda e, n2=n2: e.tensor_copy(sb["bvbc"][:, 512 * n2:512 * n2 + 512],
                                               ps[2][0:128, 0:512]),
                ["ps2"], ["bvbc"])

    def load_w(dst, key, ncols=D):
        for kc in range(8):
            dma("gpsimd", sb[dst][:, ncols * kc:ncols * kc + ncols],
                dram[key][128 * kc:128 * kc + 128, :], [dst], [dst])

    def load_w_slice(dst, key, r0, r1c, c0, c1):
        n = c1 - c0
        for kc in range((r1c - r0) // 128):
            dma("gpsimd", sb[dst][:, n * kc:n * kc + n],
                dram[key][r0 + 128 * kc:r0 + 128 * kc + 128, c0:c1], [dst], [dst])

    def favor_fm(s, wkey, biaskey, dst_pref, store_whole):
        col = 512 * s if store_whole else 0
        for mc in range(8):
            pb = ps[mc % 2]
            for kc in range(8):
                mm(pb[0:128, 0:512],
                   sb[wkey][:, 1024 * kc + 128 * mc:1024 * kc + 128 * mc + 128],
                   sb[f"xbf{kc}"][:, 0:512], kc == 0, kc == 7,
                   [wkey, f"xbf{kc}", f"ps{mc % 2}"], [f"ps{mc % 2}"])
            vop(lambda e, pb=pb, mc=mc: e.activation(
                sb[f"qs{mc}"][:, 0:512], pb[0:128, 0:512], AF.Identity,
                bias=sb[biaskey][:, mc:mc + 1], scale=1.0),
                [f"ps{mc % 2}", biaskey], [f"qs{mc}"], eng="scalar")
        for mc in range(8):
            sqb = f"sq{mc % 2}"
            vop(lambda e, mc=mc, sqb=sqb: e.activation(
                sb[sqb][:, 0:512], sb[f"qs{mc}"][:, 0:512], AF.Square),
                [f"qs{mc}"], [sqb], eng="scalar")
            mm(ps[3][0:16, 0:512], sb["e16"][:, 16 * mc:16 * mc + 16],
               sb[sqb][:, 0:512], mc == 0, mc == 7,
               ["e16", sqb, "ps3"], ["ps3"])
        vop(lambda e: e.activation(sb["Ftile"][:, 0:512], ps[3][0:16, 0:512],
                                   AF.Exp, bias=sb["cb"][0:16, 0:1], scale=-CQ),
            ["ps3", "cb"], ["Ftile"], eng="scalar")
        vop(lambda e: e.tensor_copy(sb["Fbf"][:, 0:512], sb["Ftile"][:, 0:512]),
            ["Ftile"], ["Fbf"])
        for d in range(8):
            mm(ps[4][0:128, 0:512], sb["oq"][:, 0:128], sb[f"qs{d}"][:, 0:512],
               True, True, ["oq", f"qs{d}", "ps4"], ["ps4"])
            mm(ps[5][0:128, 0:512], sb["sel"][0:16, 128 * d:128 * d + 128],
               sb["Fbf"][0:16, 0:512], True, True,
               ["sel", "Fbf", "ps5"], ["ps5"])
            vop(lambda e: e.activation(sb["escr"][:, 0:512], ps[4][0:128, 0:512], AF.Exp),
                ["ps4"], ["escr"], eng="scalar")
            tgt = f"{dst_pref}{d}"
            vop(lambda e, tgt=tgt, col=col: e.tensor_tensor(
                sb[tgt][:, col:col + 512], sb["escr"][:, 0:512],
                ps[5][0:128, 0:512], OP.mult),
                ["escr", "ps5"], [tgt])

    def phase_a(layer_first):
        build_oq("om_a")
        load_layer_vecs("A")
        build_bvbc()
        load_w("Wa", "wk_a")
        load_w("Wb", "wv_a")
        for s in range(NSLAB):
            for tcl in range(4):
                vop(lambda e, s=s, tcl=tcl: e.memset(
                    sb[f"vt{s}"][:, 1040 * tcl:1040 * tcl + 1040]
                    .rearrange("p (h j) -> p h j", j=65)[:, :, 64:65], 1.0),
                    [f"vt{s}"], [f"vt{s}"])
        vop(lambda e: e.memset(sb["stotsb"][:], 0.0), [], ["stotsb"])
        for s in range(NSLAB):
            if mode == "A":
                for tcl in range(4):
                    c = 4 * s + tcl
                    dma("sync", sb["atm"][:], dram["x0"][128 * c:128 * c + 128, :],
                        ["atm"], ["atm"])
                    for kc in range(8):
                        r.op("tensor", lambda e, kc=kc: e.transpose(
                            ps[6][0:128, 0:128],
                            sb["atm"][:, 128 * kc:128 * kc + 128], sb["idf"][:]),
                            reads=["atm", "idf", "ps6"], writes=["ps6"])
                        vop(lambda e, kc=kc, tcl=tcl: e.tensor_copy(
                            sb[f"xsb{kc}"][:, 128 * tcl:128 * tcl + 128],
                            ps[6][0:128, 0:128]),
                            ["ps6"], [f"xsb{kc}"])
                for kc in range(8):
                    dma("scalar", dram["xout"][kc, s], sb[f"xsb{kc}"][:],
                        [f"xsb{kc}"], [])
            else:
                for kc in range(8):
                    dma("sync", sb[f"xsb{kc}"][:], dram["xmid"][s, kc],
                        [f"xsb{kc}", "xmid"], [f"xsb{kc}"])
            for kc in range(8):
                vop(lambda e, kc=kc: e.tensor_copy(sb[f"xbf{kc}"][:],
                                                   sb[f"xsb{kc}"][:]),
                    [f"xsb{kc}"], [f"xbf{kc}"], eng="gpsimd")
            for tcl in range(4):
                for n2 in range(2):
                    pb = ps[n2]
                    for kc in range(8):
                        mm(pb[0:128, 0:512],
                           sb[f"xbf{kc}"][:, 128 * tcl:128 * tcl + 128],
                           sb["Wb"][:, 1024 * kc + 512 * n2:1024 * kc + 512 * n2 + 512],
                           kc == 0, kc == 7,
                           [f"xbf{kc}", "Wb", f"ps{n2}"], [f"ps{n2}"])
                    vop(lambda e, s=s, tcl=tcl, n2=n2, pb=pb: e.tensor_tensor(
                        sb[f"vt{s}"][:, 1040 * tcl + 520 * n2:1040 * tcl + 520 * n2 + 520]
                        .rearrange("p (h j) -> p h j", j=65)[:, :, 0:64],
                        pb[0:128, 0:512].rearrange("p (h j) -> p h j", j=64),
                        sb["bvbc"][:, 512 * n2:512 * n2 + 512]
                        .rearrange("p (h j) -> p h j", j=64), OP.add),
                        [f"ps{n2}", "bvbc", f"vt{s}"], [f"vt{s}"])
            favor_fm(s, "Wa", "bk", "kf", True)
            for tcl in range(4):
                c = 4 * s + tcl
                for d in range(8):
                    pb = ps[2 + d // 4]
                    mm(pb[0:128, 128 * (d % 4):128 * (d % 4) + 128],
                       sb[f"qs{d}"][:, 128 * tcl:128 * tcl + 128],
                       sb["oq"][:, 0:128], True, True,
                       [f"qs{d}", "oq", f"ps{2 + d // 4}"], [f"ps{2 + d // 4}"])
                r.op("tensor", lambda e, tcl=tcl: e.transpose(
                    ps[6][0:128, 0:16],
                    sb["Ftile"][0:16, 128 * tcl:128 * tcl + 128],
                    sb["idf"][0:16, 0:16]),
                    reads=["Ftile", "idf", "ps6"], writes=["ps6"])
                for half in range(2):
                    vop(lambda e, half=half: e.activation(
                        sb["etm"][:, 512 * half:512 * half + 512],
                        ps[2 + half][0:128, 0:512], AF.Exp),
                        [f"ps{2 + half}"], ["etm"], eng="scalar")
                vop(lambda e: e.tensor_copy(sb["atm"][:, 0:16], ps[6][0:128, 0:16]),
                    ["ps6"], ["atm"], big=False)
                vop(lambda e: e.tensor_tensor(
                    sb["schb1"][:, 0:1024].rearrange("p (h j) -> p h j", j=64),
                    sb["etm"][:, 0:1024].rearrange("p (h j) -> p h j", j=64),
                    sb["atm"][:, 0:16].rearrange("p (h j) -> p h j", j=1)
                    .broadcast_to((128, 16, 64)), OP.mult),
                    ["etm", "atm"], ["schb1"])
                for h in range(16):
                    mm(ps[7][0:64, 0:65], sb["schb1"][:, 64 * h:64 * h + 64],
                       sb[f"vt{s}"][:, 1040 * tcl + 65 * h:1040 * tcl + 65 * h + 65],
                       True, True, ["schb1", f"vt{s}", "ps7"], ["ps7"])
                    vop(lambda e, h=h: e.tensor_copy(
                        sb["schb0"][0:64, 65 * h:65 * h + 65], ps[7][0:64, 0:65]),
                        ["ps7"], ["schb0"], big=False)
                vop(lambda e: e.tensor_tensor(sb["stotsb"][0:64, :],
                                              sb["stotsb"][0:64, :],
                                              sb["schb0"][0:64, :], OP.add),
                    ["schb0", "stotsb"], ["stotsb"])
                dma("scalar", dram["sch_out"][c], sb["schb0"][0:64, :], ["schb0"], [])
        for d in range(8):
            dma("scalar", dram["kf_out"][d], sb[f"kf{d}"][:], [f"kf{d}"], [])
        for s in range(NSLAB):
            dma("scalar", dram["vt_out"][s], sb[f"vt{s}"][:], [f"vt{s}"], [])
        vop(lambda e: e.tensor_copy(sb["schb0"][0:64, :], sb["stotsb"][0:64, :]),
            ["stotsb"], ["schb0"])
        dma("scalar", dram["stot"], sb["schb0"][0:64, :], ["schb0"], [])

    def phase_b():
        build_oq("om_b")
        load_layer_vecs("B")
        vop(lambda e: e.memset(sb["srun"][:], 0.0), [], ["srun"])
        for j in range(8):
            dma("sync", sb[f"kf{j}"][0:64, 0:1040], dram["gst"][j],
                [f"kf{j}"], [f"kf{j}"])
            dma("sync", sb[f"kf{j}"][64:128, 0:1040], dram["gst"][j],
                [f"kf{j}"], [f"kf{j}"])
        for j in range(8):
            vop(lambda e, j=j: e.scalar_tensor_tensor(
                sb["srun"][:], sb[f"kf{j}"][:, 0:1040], sb["pm"][:, j:j + 1],
                sb["srun"][:], OP.mult, OP.add),
                [f"kf{j}", "pm", "srun"], ["srun"])
        vop(lambda e: e.tensor_copy(sb["sbf"][:], sb["srun"][:]), ["srun"], ["sbf"])
        load_w("Wa", "wq_b")
        load_w("Wb", "wo_b")
        for d in range(8):
            dma("sync", sb[f"kf{d}"][:], dram["kf_in"][d], [f"kf{d}"], [f"kf{d}"])
        for s in range(NSLAB):
            dma("sync", sb[f"vt{s}"][:], dram["vt_in"][s], [f"vt{s}"], [f"vt{s}"])
        if DEBUG:
            dma("sync", dram["dbg_sbf"], sb["sbf"][:], ["sbf"], [])

        for s in range(NSLAB):
            for kc in range(8):
                dma("sync", sb[f"xsb{kc}"][:], dram["xin"][kc, s],
                    [f"xsb{kc}"], [f"xsb{kc}"])
                vop(lambda e, kc=kc: e.tensor_copy(sb[f"xbf{kc}"][:],
                                                   sb[f"xsb{kc}"][:]),
                    [f"xsb{kc}"], [f"xbf{kc}"], eng="gpsimd")
            favor_fm(s, "Wa", "bq", "qf", False)
            if DEBUG and s == 0:
                for dd in range(8):
                    dma("sync", dram["dbg_qf"][dd], sb[f"qf{dd}"][:], [f"qf{dd}"], [])
            for tcl in range(4):
                c = 4 * s + tcl
                if c > 0:
                    buf = f"schb{c % 2}"
                    dma("sync", sb[buf][0:64, :], dram["sch_in"][c - 1], [buf], [buf])
                    dma("sync", sb[buf][64:128, :], dram["sch_in"][c - 1], [buf], [buf])
                    vop(lambda e, buf=buf: e.tensor_tensor(
                        sb["srun"][:], sb["srun"][:], sb[buf][:], OP.add),
                        [buf, "srun"], ["srun"])
                    vop(lambda e: e.tensor_copy(sb["sbf"][:], sb["srun"][:]),
                        ["srun"], ["sbf"])
                for h in range(16):
                    d, base = h // 2, 64 * (h % 2)
                    mm(ps[3][0:128, 0:128],
                       sb[f"kf{d}"][base:base + 64, 128 * c:128 * c + 128],
                       sb[f"qf{d}"][base:base + 64, 128 * tcl:128 * tcl + 128],
                       True, True, [f"kf{d}", f"qf{d}", "ps3"], ["ps3"])
                    vop(lambda e: e.tensor_tensor(sb["a2m"][:], ps[3][0:128, 0:128],
                                                  sb["triu"][:], OP.mult),
                        ["ps3", "triu"], ["a2m"])
                    mm(ps[4][0:128, 0:65], sb["a2m"][:],
                       sb[f"vt{s}"][:, 1040 * tcl + 65 * h:1040 * tcl + 65 * h + 65],
                       True, False, ["a2m", f"vt{s}", "ps4"], ["ps4"])
                    mm(ps[4][0:128, 0:65],
                       sb[f"qf{d}"][base:base + 64, 128 * tcl:128 * tcl + 128],
                       sb["sbf"][base:base + 64, 65 * h:65 * h + 65],
                       False, True, [f"qf{d}", "sbf", "ps4"], ["ps4"])
                    if DEBUG and s == 0 and tcl == 0 and h < 2:
                        vop(lambda e, h=h: e.tensor_copy(sb["t1"][:, 0:65],
                                                         ps[4][0:128, 0:65]),
                            ["ps4"], ["t1"])
                        dma("sync", dram["dbg_num"][h], sb["t1"][:, 0:65],
                            ["t1"], [])
                        dma("sync", dram["dbg_a2m"][h], sb["a2m"][:],
                            ["a2m"], [])
                    vop(lambda e: e.activation(sb["dcol"][:, 0:1],
                                                ps[4][0:128, 64:65], AF.Identity,
                                                bias=sb["cb"][:, 2:3], scale=1.0),
                        ["ps4", "cb"], ["dcol"], eng="scalar", big=False)
                    vop(lambda e: e.reciprocal(sb["dcol"][:, 1:2], sb["dcol"][:, 0:1]),
                        ["dcol"], ["dcol"], big=False)
                    vop(lambda e, h=h: e.activation(
                        sb["atm"][:, 64 * h:64 * h + 64], ps[4][0:128, 0:64],
                        AF.Identity, bias=0.0, scale=sb["dcol"][:, 1:2]),
                        ["ps4", "dcol"], ["atm"], eng="scalar", big=False)
                if DEBUG and s == 0 and tcl == 0:
                    dma("sync", dram["dbg_atm"], sb["atm"][:], ["atm"], [])
                for j in range(8):
                    r.op("tensor", lambda e, j=j: e.transpose(
                        ps[5][0:128, 0:128], sb["atm"][:, 128 * j:128 * j + 128],
                        sb["idf"][:]),
                        reads=["atm", "idf", "ps5"], writes=["ps5"])
                    vop(lambda e, j=j, tcl=tcl: e.tensor_copy(
                        sb[f"qs{j}"][:, 128 * tcl:128 * tcl + 128],
                        ps[5][0:128, 0:128]),
                        ["ps5"], [f"qs{j}"])
            for mc in range(8):
                pb = ps[mc % 2]
                for kc in range(8):
                    mm(pb[0:128, 0:512],
                       sb["Wb"][:, 1024 * kc + 128 * mc:1024 * kc + 128 * mc + 128],
                       sb[f"qs{kc}"][:, 0:512], kc == 0, kc == 7,
                       ["Wb", f"qs{kc}", f"ps{mc % 2}"], [f"ps{mc % 2}"])
                vop(lambda e, pb=pb, mc=mc: e.scalar_tensor_tensor(
                    sb[f"hy{mc}"][:], pb[0:128, 0:512], sb["bo"][:, mc:mc + 1],
                    sb[f"xsb{mc}"][:], OP.add, OP.add),
                    [f"ps{mc % 2}", "bo", f"xsb{mc}"], [f"hy{mc}"])
                dma("scalar", dram["rmid"][s, mc], sb[f"hy{mc}"][:],
                    [f"hy{mc}"], ["rmid"])
                if DEBUG and s == 0:
                    dma("sync", dram["dbg_hy"][mc], sb[f"hy{mc}"][:],
                        [f"hy{mc}"], [])

        def ln_stats(src_tensors, src_tokens):
            for kc in range(8):
                mm(ps[6][0:1, 0:512], sb["onesc"][:, 0:1],
                   src_tensors[kc][:, 0:512], kc == 0, kc == 7,
                   ["onesc", src_tokens[kc], "ps6"], ["ps6"])
            for kc in range(8):
                vop(lambda e, kc=kc: e.activation(
                    sb["t1"][:, 0:512], src_tensors[kc][:, 0:512], AF.Square),
                    [src_tokens[kc]], ["t1"], eng="scalar")
                mm(ps[7][0:1, 0:512], sb["onesc"][:, 0:1],
                   sb["t1"][:, 0:512], kc == 0, kc == 7,
                   ["onesc", "t1", "ps7"], ["ps7"])
            vop(lambda e: e.tensor_scalar(sb["rowb"][0:1, :], ps[6][0:1, 0:512],
                                          1.0 / 1024.0, None, OP.mult),
                ["ps6"], ["rowb"])
            vop(lambda e: e.tensor_scalar(sb["rowb2"][0:1, :], ps[7][0:1, 0:512],
                                          1.0 / 1024.0, None, OP.mult),
                ["ps7"], ["rowb2"])
            vop(lambda e: e.tensor_tensor(sb["Ftile"][0:1, 0:512],
                                          sb["rowb"][0:1, :], sb["rowb"][0:1, :],
                                          OP.mult),
                ["rowb"], ["Ftile"])
            vop(lambda e: e.tensor_tensor(sb["rowb2"][0:1, :], sb["rowb2"][0:1, :],
                                          sb["Ftile"][0:1, 0:512], OP.subtract),
                ["rowb2", "Ftile"], ["rowb2"])
            vop(lambda e: e.activation(sb["rowb2"][0:1, :], sb["rowb2"][0:1, :],
                                       AF.Sqrt, bias=sb["cb"][0:1, 1:2], scale=1.0),
                ["rowb2", "cb"], ["rowb2"], eng="scalar")
            vop(lambda e: e.reciprocal(sb["rowb2"][0:1, :], sb["rowb2"][0:1, :]),
                ["rowb2"], ["rowb2"])
            vop(lambda e: e.tensor_tensor(sb["rowb"][0:1, :], sb["rowb"][0:1, :],
                                          sb["rowb2"][0:1, :], OP.mult),
                ["rowb", "rowb2"], ["rowb"])
            mm(ps[6][0:128, 0:512], sb["ones1f"][0:1, 0:128], sb["rowb2"][0:1, :],
               True, True, ["ones1f", "rowb2", "ps6"], ["ps6"])
            mm(ps[7][0:128, 0:512], sb["ones1f"][0:1, 0:128], sb["rowb"][0:1, :],
               True, True, ["ones1f", "rowb", "ps7"], ["ps7"])

        for s in range(NSLAB):
            for kc in range(8):
                dma("sync", sb[f"xsb{kc}"][:], dram["rmid"][s, kc],
                    [f"xsb{kc}", "rmid"], [f"xsb{kc}"])
            ln_stats([sb[f"xsb{kc}"] for kc in range(8)],
                     [f"xsb{kc}" for kc in range(8)])
            for kc in range(8):
                vop(lambda e, kc=kc: e.tensor_tensor(sb["t1"][:], sb[f"xsb{kc}"][:],
                                                     ps[6][0:128, 0:512], OP.mult),
                    [f"xsb{kc}", "ps6"], ["t1"])
                vop(lambda e: e.tensor_tensor(sb["t1"][:], sb["t1"][:],
                                              ps[7][0:128, 0:512], OP.subtract),
                    ["t1", "ps7"], ["t1"])
                vop(lambda e, kc=kc, s=s: e.tensor_scalar(
                    sb[f"kf{kc}"][:, 512 * s:512 * s + 512], sb["t1"][:],
                    sb["l1g"][:, kc:kc + 1], sb["l1b"][:, kc:kc + 1],
                    OP.mult, OP.add),
                    ["t1", "l1g", "l1b"], [f"kf{kc}"])
        for g in range(4):
            load_w_slice("Wa", "w1_b", 0, 1024, 1024 * g, 1024 * g + 1024)
            load_w_slice("Wb", "w2_b", 1024 * g, 1024 * g + 1024, 0, 1024)
            for s in range(NSLAB):
                for mc in range(8):
                    pb = ps[mc % 2]
                    for kc in range(8):
                        mm(pb[0:128, 0:512],
                           sb["Wa"][:, 1024 * kc + 128 * mc:1024 * kc + 128 * mc + 128],
                           sb[f"kf{kc}"][:, 512 * s:512 * s + 512], kc == 0, kc == 7,
                           ["Wa", f"kf{kc}", f"ps{mc % 2}"], [f"ps{mc % 2}"])
                    vop(lambda e, pb=pb, g=g, mc=mc: e.activation(
                        sb[f"qs{mc}"][:, 0:512], pb[0:128, 0:512], AF.Relu,
                        bias=sb["bf1"][:, 8 * g + mc:8 * g + mc + 1], scale=1.0),
                        [f"ps{mc % 2}", "bf1"], [f"qs{mc}"], eng="scalar")
                for mc2 in range(8):
                    pb = ps[2 + mc2 % 2]
                    for mc in range(8):
                        mm(pb[0:128, 0:512],
                           sb["Wb"][:, 1024 * mc + 128 * mc2:1024 * mc + 128 * mc2 + 128],
                           sb[f"qs{mc}"][:, 0:512], mc == 0, mc == 7,
                           ["Wb", f"qs{mc}", f"ps{2 + mc2 % 2}"], [f"ps{2 + mc2 % 2}"])
                    yt = f"vt{mc2 // 2}"
                    ycols = slice(2080 * (mc2 % 2) + 520 * 0 + 512 * s,
                                  2080 * (mc2 % 2) + 512 * s + 512)
                    if g == 0:
                        vop(lambda e, pb=pb, yt=yt, yc=ycols: e.tensor_copy(
                            sb[yt][:, yc], pb[0:128, 0:512]),
                            [f"ps{2 + mc2 % 2}"], [yt])
                    else:
                        vop(lambda e, pb=pb, yt=yt, yc=ycols: e.tensor_tensor(
                            sb[yt][:, yc], sb[yt][:, yc], pb[0:128, 0:512], OP.add),
                            [f"ps{2 + mc2 % 2}", yt], [yt])
        for s in range(NSLAB):
            for kc in range(8):
                dma("sync", sb[f"xsb{kc}"][:], dram["rmid"][s, kc],
                    [f"xsb{kc}", "rmid"], [f"xsb{kc}"])
            ln_stats([sb[f"xsb{kc}"] for kc in range(8)],
                     [f"xsb{kc}" for kc in range(8)])
            for kc in range(8):
                vop(lambda e, kc=kc: e.tensor_tensor(sb["t1"][:], sb[f"xsb{kc}"][:],
                                                     ps[6][0:128, 0:512], OP.mult),
                    [f"xsb{kc}", "ps6"], ["t1"])
                vop(lambda e: e.tensor_tensor(sb["t1"][:], sb["t1"][:],
                                              ps[7][0:128, 0:512], OP.subtract),
                    ["t1", "ps7"], ["t1"])
                vop(lambda e, kc=kc: e.tensor_scalar(
                    sb["t1"][:], sb["t1"][:], sb["l1g"][:, kc:kc + 1],
                    sb["bsum"][:, kc:kc + 1], OP.mult, OP.add),
                    ["t1", "l1g", "bsum"], ["t1"])
                yt = f"vt{kc // 2}"
                ycols = slice(2080 * (kc % 2) + 512 * s, 2080 * (kc % 2) + 512 * s + 512)
                vop(lambda e, yt=yt, yc=ycols, kc=kc: e.tensor_tensor(
                    sb[f"hy{kc}"][:], sb["t1"][:], sb[yt][:, yc], OP.add),
                    ["t1", yt], [f"hy{kc}"])
            ln_stats([sb[f"hy{kc}"] for kc in range(8)],
                     [f"hy{kc}" for kc in range(8)])
            for kc in range(8):
                vop(lambda e, kc=kc: e.tensor_tensor(sb["t1"][:], sb[f"hy{kc}"][:],
                                                     ps[6][0:128, 0:512], OP.mult),
                    [f"hy{kc}", "ps6"], ["t1"])
                vop(lambda e: e.tensor_tensor(sb["t1"][:], sb["t1"][:],
                                              ps[7][0:128, 0:512], OP.subtract),
                    ["t1", "ps7"], ["t1"])
                vop(lambda e, kc=kc: e.tensor_scalar(
                    sb[f"xsb{kc}"][:], sb["t1"][:], sb["l2g"][:, kc:kc + 1],
                    sb["l2b"][:, kc:kc + 1], OP.mult, OP.add),
                    ["t1", "l2g", "l2b"], [f"xsb{kc}"])
                dma("scalar", dram["xmid"][s, kc], sb[f"xsb{kc}"][:],
                    [f"xsb{kc}"], ["xmid"])
                if has_a:
                    dma("scalar", dram["xout"][kc, s], sb[f"xsb{kc}"][:],
                        [f"xsb{kc}"], [])

    def rhs_prep():
        for kc in range(8):
            dma("sync", sb[f"rhsf{kc}"][:], dram["rhs"][128 * kc:128 * kc + 128, :],
                [f"kf{kc}"], [f"kf{kc}"])
        for nc2 in range(8):
            for kc in range(8):
                r.op("tensor", lambda e, kc=kc, nc2=nc2: e.transpose(
                    ps[5][0:128, 0:128],
                    sb[f"rhsf{kc}"][:, 128 * nc2:128 * nc2 + 128], sb["idf"][:]),
                    reads=[f"kf{kc}", "idf", "ps5"], writes=["ps5"])
                vop(lambda e, kc=kc: e.tensor_copy(
                    sb["etm"][:, 128 * kc:128 * kc + 128], ps[5][0:128, 0:128]),
                    ["ps5"], ["etm"])
            vop(lambda e: e.tensor_reduce(sb["st8"][:, 0:1], sb["etm"][:],
                                          axis=AX.X, op=OP.max,
                                          apply_absolute_value=True),
                ["etm"], ["st8"], big=False)
            vop(lambda e: e.tensor_scalar(sb["st8"][:, 1:2], sb["st8"][:, 0:1],
                                          1e-12, INV127, OP.max, OP.mult),
                ["st8"], ["st8"], big=False)
            vop(lambda e, nc2=nc2: e.tensor_copy(sb["scall"][:, nc2:nc2 + 1],
                                                 sb["st8"][:, 1:2]),
                ["st8"], ["scall"], big=False)
            vop(lambda e, nc2=nc2: e.reciprocal(sb["rcall"][:, nc2:nc2 + 1],
                                                sb["st8"][:, 1:2]),
                ["st8"], ["rcall"], big=False)
        for which, src, dst in (("s", "scall", "scolbc"), ("r", "rcall", "rcolbc")):
            for nc2 in range(8):
                mm(ps[5][0:1, 0:128], sb[src][:, nc2:nc2 + 1], sb["idf"][:],
                   True, True, [src, "idf", "ps5"], ["ps5"])
                vop(lambda e: e.tensor_copy(sb["rowb"][0:1, 0:128], ps[5][0:1, 0:128]),
                    ["ps5"], ["rowb"])
                mm(ps[6][0:128, 0:128], sb["ones1f"][0:1, 0:128],
                   sb["rowb"][0:1, 0:128], True, True,
                   ["ones1f", "rowb", "ps6"], ["ps6"])
                vop(lambda e, dst=dst, nc2=nc2: e.tensor_copy(
                    sb[dst][:, 128 * nc2:128 * nc2 + 128], ps[6][0:128, 0:128]),
                    ["ps6"], [dst])
        for kc in range(8):
            vop(lambda e, kc=kc: e.tensor_tensor(sb["atm"][:], sb[f"rhsf{kc}"][:],
                                                 sb["rcolbc"][:], OP.mult),
                [f"kf{kc}", "rcolbc"], ["atm"])
            vop(lambda e: e.tensor_copy(sb["qi"][:], sb["atm"][:]), ["atm"], ["qi"])
            vop(lambda e, kc=kc: e.tensor_copy(sb[f"qrhs{kc}"][:], sb["qi"][:]),
                ["qi"], [f"qrhs{kc}"])

    def ln_plain_tm(src_key, out_key, out_tensor, src_tensor):
        vop(lambda e: e.tensor_reduce(sb["st8"][:, 2:3], src_tensor[:],
                                      axis=AX.X, op=OP.add),
            [src_key], ["st8"])
        vop(lambda e: e.activation(sb["atm"][:], src_tensor[:], AF.Square),
            [src_key], ["atm"], eng="scalar")
        vop(lambda e: e.tensor_reduce(sb["st8"][:, 3:4], sb["atm"][:],
                                      axis=AX.X, op=OP.add),
            ["atm"], ["st8"], big=False)
        vop(lambda e: e.tensor_scalar(sb["st8"][:, 4:5], sb["st8"][:, 2:3],
                                      1.0 / 1024.0, None, OP.mult),
            ["st8"], ["st8"], big=False)
        vop(lambda e: e.tensor_scalar(sb["st8"][:, 5:6], sb["st8"][:, 3:4],
                                      1.0 / 1024.0, None, OP.mult),
            ["st8"], ["st8"], big=False)
        vop(lambda e: e.tensor_tensor(sb["st8"][:, 6:7], sb["st8"][:, 4:5],
                                      sb["st8"][:, 4:5], OP.mult),
            ["st8"], ["st8"], big=False)
        vop(lambda e: e.tensor_tensor(sb["st8"][:, 6:7], sb["st8"][:, 5:6],
                                      sb["st8"][:, 6:7], OP.subtract),
            ["st8"], ["st8"], big=False)
        vop(lambda e: e.activation(sb["st8"][:, 7:8], sb["st8"][:, 6:7],
                                   AF.Sqrt, bias=sb["cb"][:, 1:2], scale=1.0),
            ["st8", "cb"], ["st8"], eng="scalar", big=False)
        vop(lambda e: e.reciprocal(sb["st8"][:, 8:9], sb["st8"][:, 7:8]),
            ["st8"], ["st8"], big=False)
        vop(lambda e: e.tensor_scalar(out_tensor[:], src_tensor[:],
                                      sb["st8"][:, 4:5], sb["st8"][:, 8:9],
                                      OP.subtract, OP.mult),
            [src_key, "st8"], [out_key])

    def final_block():
        for tc in range(16):
            s, tcl = tc // 4, tc % 4
            for kc in range(8):
                dma("sync", sb["atm"][:, 128 * kc:128 * kc + 128],
                    dram["xmid"][s, kc][:, 128 * tcl:128 * tcl + 128],
                    ["atm", "xmid"], ["atm"])
            dma("sync", sb["x0t"][:], dram["x0"][128 * tc:128 * tc + 128, :],
                ["x0t"], ["x0t"])
            for kc in range(8):
                r.op("tensor", lambda e, kc=kc: e.transpose(
                    ps[5][0:128, 0:128], sb["atm"][:, 128 * kc:128 * kc + 128],
                    sb["idf"][:]),
                    reads=["atm", "idf", "ps5"], writes=["ps5"])
                vop(lambda e, kc=kc: e.tensor_tensor(
                    sb["etm"][:, 128 * kc:128 * kc + 128], ps[5][0:128, 0:128],
                    sb["x0t"][:, 128 * kc:128 * kc + 128], OP.add),
                    ["ps5", "x0t"], ["etm"])
            ln_plain_tm("etm", "x1t", sb["x1t"], sb["etm"])
            vop(lambda e: e.tensor_reduce(sb["st8"][:, 0:1], sb["x1t"][:],
                                          axis=AX.X, op=OP.max,
                                          apply_absolute_value=True),
                ["x1t"], ["st8"], big=False)
            vop(lambda e: e.tensor_scalar(sb["st8"][:, 1:2], sb["st8"][:, 0:1],
                                          1e-12, INV127, OP.max, OP.mult),
                ["st8"], ["st8"], big=False)
            vop(lambda e: e.reciprocal(sb["st8"][:, 9:10], sb["st8"][:, 1:2]),
                ["st8"], ["st8"], big=False)
            vop(lambda e: e.tensor_scalar(sb["atm"][:], sb["x1t"][:],
                                          sb["st8"][:, 9:10], None, OP.mult),
                ["x1t", "st8"], ["atm"])
            vop(lambda e: e.tensor_copy(sb["qi"][:], sb["atm"][:]), ["atm"], ["qi"])
            vop(lambda e: e.tensor_copy(sb["atm"][:], sb["qi"][:]), ["qi"], ["atm"])
            for j in range(8):
                r.op("tensor", lambda e, j=j: e.transpose(
                    ps[5][0:128, 0:128], sb["atm"][:, 128 * j:128 * j + 128],
                    sb["idf"][:]),
                    reads=["atm", "idf", "ps5"], writes=["ps5"])
                vop(lambda e, j=j: e.tensor_copy(
                    sb["qlf"][:, 128 * j:128 * j + 128], ps[5][0:128, 0:128]),
                    ["ps5"], ["qlf"])
            for n2 in range(2):
                pb = ps[n2]
                for kc in range(8):
                    mm(pb[0:128, 0:512], sb["qlf"][:, 128 * kc:128 * kc + 128],
                       sb[f"qrhs{kc}"][:, 512 * n2:512 * n2 + 512], kc == 0, kc == 7,
                       ["qlf", f"qrhs{kc}", f"ps{n2}"], [f"ps{n2}"])
                vop(lambda e, pb=pb, n2=n2: e.scalar_tensor_tensor(
                    sb[f"hy{n2}"][:], pb[0:128, 0:512], sb["st8"][:, 1:2],
                    sb["scolbc"][:, 512 * n2:512 * n2 + 512], OP.mult, OP.mult),
                    [f"ps{n2}", "st8", "scolbc"], [f"hy{n2}"])
                vop(lambda e, n2=n2: e.tensor_tensor(
                    sb[f"hy{n2}"][:], sb[f"hy{n2}"][:],
                    sb["x1t"][:, 512 * n2:512 * n2 + 512], OP.add),
                    [f"hy{n2}", "x1t"], [f"hy{n2}"])
            vop(lambda e: e.tensor_reduce(sb["st8"][:, 2:3], sb["hy0"][:],
                                          axis=AX.X, op=OP.add),
                ["hy0"], ["st8"], big=False)
            vop(lambda e: e.tensor_reduce(sb["st8"][:, 3:4], sb["hy1"][:],
                                          axis=AX.X, op=OP.add),
                ["hy1"], ["st8"], big=False)
            vop(lambda e: e.tensor_tensor(sb["st8"][:, 2:3], sb["st8"][:, 2:3],
                                          sb["st8"][:, 3:4], OP.add),
                ["st8"], ["st8"], big=False)
            vop(lambda e: e.activation(sb["atm"][:, 0:512], sb["hy0"][:], AF.Square),
                ["hy0"], ["atm"], eng="scalar")
            vop(lambda e: e.activation(sb["atm"][:, 512:1024], sb["hy1"][:], AF.Square),
                ["hy1"], ["atm"], eng="scalar")
            vop(lambda e: e.tensor_reduce(sb["st8"][:, 3:4], sb["atm"][:],
                                          axis=AX.X, op=OP.add),
                ["atm"], ["st8"], big=False)
            vop(lambda e: e.tensor_scalar(sb["st8"][:, 4:5], sb["st8"][:, 2:3],
                                          1.0 / 1024.0, None, OP.mult),
                ["st8"], ["st8"], big=False)
            vop(lambda e: e.tensor_scalar(sb["st8"][:, 5:6], sb["st8"][:, 3:4],
                                          1.0 / 1024.0, None, OP.mult),
                ["st8"], ["st8"], big=False)
            vop(lambda e: e.tensor_tensor(sb["st8"][:, 6:7], sb["st8"][:, 4:5],
                                          sb["st8"][:, 4:5], OP.mult),
                ["st8"], ["st8"], big=False)
            vop(lambda e: e.tensor_tensor(sb["st8"][:, 6:7], sb["st8"][:, 5:6],
                                          sb["st8"][:, 6:7], OP.subtract),
                ["st8"], ["st8"], big=False)
            vop(lambda e: e.activation(sb["st8"][:, 7:8], sb["st8"][:, 6:7],
                                       AF.Sqrt, bias=sb["cb"][:, 1:2], scale=1.0),
                ["st8", "cb"], ["st8"], eng="scalar", big=False)
            vop(lambda e: e.reciprocal(sb["st8"][:, 8:9], sb["st8"][:, 7:8]),
                ["st8"], ["st8"], big=False)
            for n2 in range(2):
                vop(lambda e, n2=n2: e.tensor_scalar(
                    sb["x1t"][:, 512 * n2:512 * n2 + 512], sb[f"hy{n2}"][:],
                    sb["st8"][:, 4:5], sb["st8"][:, 8:9], OP.subtract, OP.mult),
                    [f"hy{n2}", "st8"], ["x1t"])
            dma("scalar", dram["out"][128 * tc:128 * tc + 128, :], sb["x1t"][:],
                ["x1t"], [])

    if mode == "DBGC":
        vop(lambda e: e.memset(sb["srun"][:], 0.0), [], ["srun"])
        for j in range(8):
            dma("sync", sb[f"kf{j}"][0:64, 0:1040], dram["gst"][j],
                [f"kf{j}"], [f"kf{j}"])
            dma("sync", sb[f"kf{j}"][64:128, 0:1040], dram["gst"][j],
                [f"kf{j}"], [f"kf{j}"])
        for j in range(8):
            vop(lambda e, j=j: e.scalar_tensor_tensor(
                sb["srun"][:], sb[f"kf{j}"][:, 0:1040], sb["pm"][:, j:j + 1],
                sb["srun"][:], OP.mult, OP.add),
                [f"kf{j}", "pm", "srun"], ["srun"])
        vop(lambda e: e.tensor_copy(sb["sbf"][:], sb["srun"][:]), ["srun"], ["sbf"])
        dma("sync", dram["dbgc_sbf"], sb["sbf"][:], ["sbf"], [])
        dma("sync", dram["dbgc_kf0"], sb["kf0"][:, 0:1040], ["kf0"], [])
    if has_b and mode != "DBGC":
        phase_b()
    if has_a:
        phase_a(la)
    if has_f:
        r.barrier()
        rhs_prep()
        final_block()

    sems = {}
    from contextlib import ExitStack
    stk = ExitStack()
    for e in ENGINES:
        sems[e] = stk.enter_context(nc.semaphore(f"sem_{e}"))
    block = stk.enter_context(nc.Block())
    r.emit(block, sems)
    stk.close()
    nc._rec_counts = dict(r.cnt)
    return nc


def _consts():
    idf = np.eye(128, dtype=np.float32)
    idb = np.eye(128).astype(ml_dtypes.bfloat16)
    tk, tq = np.meshgrid(np.arange(128), np.arange(128), indexing="ij")
    triu = (tq >= tk).astype(np.float32)
    e16 = np.zeros((128, 128), np.float32)
    for c in range(8):
        e16[0:64, 16 * c + 2 * c] = 1.0
        e16[64:128, 16 * c + 2 * c + 1] = 1.0
    sel = np.zeros((16, 1024), np.float32)
    for d in range(8):
        sel[2 * d, 128 * d:128 * d + 64] = 1.0
        sel[2 * d + 1, 128 * d + 64:128 * d + 128] = 1.0
    ones1f = np.ones((1, 128), np.float32)
    return dict(idf=idf, idb=idb, triu=triu,
                e16=e16.astype(ml_dtypes.bfloat16),
                sel=sel.astype(ml_dtypes.bfloat16), ones1f=ones1f)


def _fm8(v):
    return np.ascontiguousarray(np.asarray(v).reshape(8, 128).T)


def _fm32(v):
    return np.ascontiguousarray(np.asarray(v).reshape(32, 128).T)


def kernel(L0_attn_x, L0_attn_Wq, L0_attn_bq, L0_attn_Wk, L0_attn_bk,
           L0_attn_Wv, L0_attn_bv, L0_attn_Wo, L0_attn_bo, L0_attn_omega,
           L0_attn_ln1_g, L0_attn_ln1_b, L0_attn_ln2_g, L0_attn_ln2_b,
           L0_attn_W1, L0_attn_bf1, L0_attn_W2, L0_attn_bf2, L0_ffn_rhs):
    x = np.ascontiguousarray(np.asarray(L0_attn_x, dtype=np.float32))
    Wq = np.asarray(L0_attn_Wq, np.float32)
    Wk = np.asarray(L0_attn_Wk, np.float32)
    Wv = np.asarray(L0_attn_Wv, np.float32)
    Wo = np.asarray(L0_attn_Wo, np.float32)
    W1 = np.asarray(L0_attn_W1, np.float32)
    W2 = np.asarray(L0_attn_W2, np.float32)
    bq = np.asarray(L0_attn_bq, np.float32)
    bk = np.asarray(L0_attn_bk, np.float32)
    bv = np.asarray(L0_attn_bv, np.float32)
    bo = np.asarray(L0_attn_bo, np.float32)
    bf1 = np.asarray(L0_attn_bf1, np.float32)
    bf2 = np.asarray(L0_attn_bf2, np.float32)
    l1g = np.asarray(L0_attn_ln1_g, np.float32)
    l1b = np.asarray(L0_attn_ln1_b, np.float32)
    l2g = np.asarray(L0_attn_ln2_g, np.float32)
    l2b = np.asarray(L0_attn_ln2_b, np.float32)
    om = np.asarray(L0_attn_omega, np.float32)
    rhs = np.ascontiguousarray(np.asarray(L0_ffn_rhs, np.float32))

    consts = _consts()
    x0s = [np.ascontiguousarray(x[c // 4, 2048 * (c % 4):2048 * (c % 4) + 2048, :])
           for c in range(NCORE)]
    pmasks = []
    for c in range(NCORE):
        g0 = (c // 4) * 4
        m = np.zeros(8, np.float32)
        m[g0:c] = 1.0
        pmasks.append(np.broadcast_to(m, (128, 8)).copy())

    def layer_vecs(l):
        return dict(
            bq_b=_fm8(bq[l]), bo_b=_fm8(bo[l]), ln1g_b=_fm8(l1g[l]),
            ln1b_b=_fm8(l1b[l]), ln2g_b=_fm8(l2g[l]), ln2b_b=_fm8(l2b[l]),
            bf1_b=_fm32(bf1[l]), bf2_b=_fm8(bf2[l]),
            wq_b=np.ascontiguousarray(Wq[l]), wo_b=np.ascontiguousarray(Wo[l]),
            w1_b=np.ascontiguousarray(W1[l]), w2_b=np.ascontiguousarray(W2[l]),
            om_b=np.ascontiguousarray(np.vstack([om[l], om[l]])))

    def layer_a_in(l):
        return dict(
            wk_a=np.ascontiguousarray(Wk[l]), wv_a=np.ascontiguousarray(Wv[l]),
            bk_a=_fm8(bk[l]), bv_a=np.ascontiguousarray(bv[l]).reshape(1, D),
            om_a=np.ascontiguousarray(np.vstack([om[l], om[l]])))

    state = [dict() for _ in range(NCORE)]

    nc0 = build_launch("A", 0, None)
    in_maps = []
    for c in range(NCORE):
        m = dict(consts)
        m.update(layer_a_in(0))
        m["x0"] = x0s[c]
        in_maps.append(m)
    res = run_bass_kernel_spmd(nc0, in_maps, core_ids=list(range(NCORE)))
    for c in range(NCORE):
        state[c] = {k: res.results[c][k]
                    for k in ("xout", "kf_out", "vt_out", "sch_out", "stot")}

    for k in (1, 2, 3):
        lb = k - 1
        la = k if k < 3 else None
        mode = "BA" if k < 3 else "BF"
        nck = build_launch(mode, la, lb)
        gst = np.stack([state[c]["stot"] for c in range(NCORE)])
        in_maps = []
        for c in range(NCORE):
            m = dict(consts)
            m.update(layer_vecs(lb))
            m["xin"] = state[c]["xout"]
            m["kf_in"] = state[c]["kf_out"]
            m["vt_in"] = state[c]["vt_out"]
            m["sch_in"] = state[c]["sch_out"]
            m["gst"] = gst
            m["pmask"] = pmasks[c]
            if mode == "BA":
                m.update(layer_a_in(la))
            else:
                m["x0"] = x0s[c]
                m["rhs"] = rhs
            in_maps.append(m)
        res = run_bass_kernel_spmd(nck, in_maps, core_ids=list(range(NCORE)))
        if mode == "BA":
            ns = []
            for c in range(NCORE):
                ns.append({kk: res.results[c][kk]
                           for kk in ("xout", "kf_out", "vt_out", "sch_out", "stot")})
            state = ns
        else:
            outs = [res.results[c]["out"] for c in range(NCORE)]

    full = np.empty((2, 8192, D), np.float32)
    for c in range(NCORE):
        full[c // 4, 2048 * (c % 4):2048 * (c % 4) + 2048, :] = outs[c]
    return full


# version 06
# speedup vs baseline: 1.0988x; 1.0626x over previous
import numpy as np
import ml_dtypes
import concourse.bass as bass
import concourse.mybir as mybir
from concourse.bass_utils import run_bass_kernel_spmd

F32 = mybir.dt.float32
BF16 = mybir.dt.bfloat16
I32 = mybir.dt.int32
AF = mybir.ActivationFunctionType
OP = mybir.AluOpType
AX = mybir.AxisListType

NCORE = 8
T = 2048
D = 1024
H = 16
FF = 4096
NL = 3
C = 128
NCH = 16
SLAB = 512
NSLAB = 4
CQ = 0.0625
LN8 = 2.0794415416798357
CS = 0.35355339059327373
LN_EPS = 1e-5
ATTN_EPS = 1e-6
INV127 = 1.0 / 127.0

ENGINES = ("tensor", "vector", "scalar", "gpsimd", "sync", "gdma", "sdma")
QUEUE_LABELS = {"tensor": ("tensor",), "vector": ("vector",),
                "scalar": ("scalar", "sdma"), "gpsimd": ("gpsimd", "gdma"),
                "sync": ("sync",)}
DEBUG = False
SCAN_BATCHED = 0
SCAN_SER_COLS = False


class Rec:

    SPACED = ("vector", "gpsimd", "scalar")

    def __init__(self):
        self.items = []
        self.cnt = {e: 0 for e in ENGINES}
        self.opi = {e: 0 for e in ENGINES}
        self.writer = {}
        self.readers = {}

    def barrier(self):
        snap = dict(self.cnt)
        self._pend = {e: {d: v for d, v in snap.items() if d != e and v > 0}
                      for e in ENGINES}

    def op(self, eng, fn, reads=(), writes=(), dma=False, big=False):
        inc = 16 if dma else 1
        waits = {}
        pend = getattr(self, "_pend", None)
        if pend and pend.get(eng):
            waits.update(pend.pop(eng))

        def need(dep):
            deng, dcnt = dep[0], dep[1]
            dep_dma = dep[3] if len(dep) > 3 else False
            if deng == eng and not dep_dma and eng in ("tensor", "sync"):
                return
            if (eng == "gdma" and deng == "gpsimd") or (eng == "sdma" and deng == "scalar"):
                return
            if waits.get(deng, 0) < dcnt:
                waits[deng] = dcnt

        for t in reads:
            w = self.writer.get(t)
            if w:
                need(w)
        for t in writes:
            w = self.writer.get(t)
            if w:
                need(w)
            for rd in self.readers.get(t, ()):
                need(rd)
        self.cnt[eng] += inc
        self.opi[eng] += 1
        here = (eng, self.cnt[eng], self.opi[eng], dma, big or dma)
        for t in reads:
            self.readers.setdefault(t, []).append(here)
        for t in writes:
            self.writer[t] = here
            self.readers[t] = []
        self.items.append((eng, fn, waits, inc))

    def emit(self, block, sems):
        finals = dict(self.cnt)
        fence = {e: set() for e in ENGINES}
        for eng, fn, waits, inc in self.items:
            for d, v in waits.items():
                fence[d].add(v)

        def mk(qname, items):
            cnt = {l: 0 for l in QUEUE_LABELS[qname]}
            selfhw = {l: 0 for l in QUEUE_LABELS[qname]}
            fi = {l: 0 for l in QUEUE_LABELS[qname]}
            fv = {l: sorted(fence[l]) for l in QUEUE_LABELS[qname]}
            dma_lbl = [l for l in QUEUE_LABELS[qname] if l in ("sync", "gdma", "sdma")]
            hwq = {d: 0 for d in ENGINES}

            def body(eng):
                for lbl, fn, waits, inc in items:
                    if lbl in dma_lbl or inc == 16:
                        fmax = 0
                        while fi[lbl] < len(fv[lbl]) and fv[lbl][fi[lbl]] <= cnt[lbl]:
                            fmax = fv[lbl][fi[lbl]]
                            fi[lbl] += 1
                        if fmax > selfhw[lbl]:
                            eng.wait_ge(sems[lbl], fmax)
                            selfhw[lbl] = fmax
                    for d, v in sorted(waits.items()):
                        if hwq[d] < v:
                            eng.wait_ge(sems[d], v)
                            hwq[d] = v
                    fn(eng).then_inc(sems[lbl], inc)
                    cnt[lbl] += inc
                for d in ENGINES:
                    if d not in QUEUE_LABELS[qname] and finals[d] > 0 and hwq[d] < finals[d]:
                        eng.wait_ge(sems[d], finals[d])
                for l in QUEUE_LABELS[qname]:
                    if finals[l] > cnt[l] or (finals[l] > selfhw[l] and l in dma_lbl):
                        eng.wait_ge(sems[l], finals[l])
            return body

        for qname in QUEUE_LABELS:
            qitems = [(l, fn, w, inc) for (l, fn, w, inc) in
                      [(e, f, w, i) for (e, f, w, i) in self.items]
                      if l in QUEUE_LABELS[qname]]
            getattr(block, qname)(mk(qname, qitems))


SBUF_BASE = 16512


class Alloc:
    def __init__(self, nc):
        self.nc = nc
        self.off = SBUF_BASE
        self.offs = {}

    def at(self, name, shape, dtype, offset):
        return self.nc.alloc_sbuf_tensor_at(name, shape, dtype, offset=offset)

    def buf(self, name, shape, dtype):
        per = int(np.prod(shape[1:])) * mybir.dt.size(dtype)
        per = (per + 31) // 32 * 32
        t = self.nc.alloc_sbuf_tensor_at(name, shape, dtype, offset=self.off)
        self.offs[name] = self.off
        self.off += per
        return t

    def mark(self):
        return self.off


def ceil32(x):
    return (x + 31) // 32 * 32


def build_launch(mode, la, lb):
    nc = bass.Bass("TRN2", target_bir_lowering=False, debug=False)
    r = Rec()

    dram = {}

    def din(name, shape, dtype):
        dram[name] = nc.dram_tensor(name, shape, dtype, kind="ExternalInput").ap()
        return dram[name]

    def dout(name, shape, dtype):
        dram[name] = nc.dram_tensor(name, shape, dtype, kind="ExternalOutput").ap()
        return dram[name]

    din("idf", [128, 128], F32)
    din("idb", [128, 128], BF16)
    din("triu", [128, 128], F32)
    din("e16", [128, 128], BF16)
    din("sel", [16, 1024], BF16)
    din("ones1f", [1, 128], F32)

    has_a = mode in ("A", "BA")
    has_b = mode in ("BA", "BF", "DBGC")
    has_f = mode == "BF"
    if mode == "DBGC":
        has_a = False

    if mode == "A":
        din("x0", [T, D], F32)
    else:
        din("xin", [8, NSLAB, 128, SLAB], F32)
    if mode == "DBGC":
        dout("dbgc_sbf", [128, 1040], BF16)
        dout("dbgc_kf0", [128, 1040], BF16)
    if has_a:
        din("wk_a", [D, D], F32)
        din("wv_a", [D, D], F32)
        din("bk_a", [128, 8], F32)
        din("bv_a", [1, D], F32)
        din("om_a", [128, 32], F32)
        dout("xout", [8, NSLAB, 128, SLAB], F32)
        dout("kf_out", [8, 128, T], BF16)
        dout("vt_out", [NSLAB, 128, 4160], BF16)
        dout("sch_out", [NCH, 64, 1040], BF16)
        dout("stot", [64, 1040], BF16)
    if has_b:
        din("kf_in", [8, 128, T], BF16)
        din("vt_in", [NSLAB, 128, 4160], BF16)
        din("sch_in", [NCH, 64, 1040], BF16)
        din("gst", [8, 64, 1040], BF16)
        din("pmask", [128, 8], F32)
        din("wq_b", [D, D], F32)
        din("wo_b", [D, D], F32)
        din("w1_b", [D, FF], F32)
        din("w2_b", [FF, D], F32)
        din("bq_b", [128, 8], F32)
        din("bo_b", [128, 8], F32)
        din("ln1g_b", [128, 8], F32)
        din("ln1b_b", [128, 8], F32)
        din("ln2g_b", [128, 8], F32)
        din("ln2b_b", [128, 8], F32)
        din("bf1_b", [128, 32], F32)
        din("bf2_b", [128, 8], F32)
        din("om_b", [128, 32], F32)
        dram["rmid"] = nc.dram_tensor("rmid", [NSLAB, 8, 128, SLAB], F32).ap()
        dram["xmid"] = nc.dram_tensor("xmid", [NSLAB, 8, 128, SLAB], F32).ap()
    if has_f:
        din("x0", [T, D], F32)
        din("rhs", [D, D], F32)
        dout("out", [T, D], F32)
    if DEBUG and has_b:
        dout("dbg_qf", [8, 128, SLAB], BF16)
        dout("dbg_sbf", [128, 1040], BF16)
        dout("dbg_atm", [128, 1024], F32)
        dout("dbg_hy", [8, 128, SLAB], F32)
        dout("dbg_num", [2, 128, 65], F32)
        dout("dbg_a2m", [2, 128, 128], BF16)

    al = Alloc(nc)
    sb = {}

    def B(name, shape, dtype):
        sb[name] = al.buf(name, shape, dtype)
        return sb[name]

    for i in range(8):
        B(f"xsb{i}", [128, SLAB], F32)
    for i in range(8):
        B(f"xbf{i}", [128, SLAB], BF16)
    B("Wa", [128, 8192], BF16)
    B("Wb", [128, 8192], BF16)
    kf_off = al.mark()
    for i in range(8):
        B(f"kf{i}", [128, T], BF16)
    vt_off = al.mark()
    for i in range(NSLAB):
        B(f"vt{i}", [128, 4160], BF16)
    for i in range(8):
        B(f"qs{i}", [128, SLAB], BF16)
    for i in range(8):
        B(f"qf{i}", [128, SLAB], BF16)
    B("srun", [128, 1040], F32)
    B("sbf", [128, 1040], BF16)
    B("schb0", [128, 1040], BF16)
    B("schb1", [128, 1040], BF16)
    B("atm", [128, 1024], F32)
    B("etm", [128, 1024], F32)
    B("escr", [128, SLAB], F32)
    B("sq0", [128, SLAB], BF16)
    B("sq1", [128, SLAB], BF16)
    B("t1", [128, SLAB], F32)
    for i in range(8):
        B(f"hy{i}", [128, SLAB], F32)
    B("a2m", [128, 2048], BF16)
    B("cb", [128, 4], F32)
    r._junk = B("junk", [128, 64], F32)
    B("onesc", [128, 1], F32)
    B("oq", [128, 128], BF16)
    B("omsb", [128, 32], F32)
    B("dcol", [128, 32], F32)
    B("st8", [128, 12], F32)
    B("rowb", [1, SLAB], F32)
    B("rowb2", [1, SLAB], F32)
    if has_a:
        B("bvbc", [128, 1024], F32)
        B("stotsb", [128, 1040], F32)
        B("bvrow", [1, 1024], F32)
    B("bsum", [128, 8], F32)
    B("idf", [128, 128], F32)
    B("idb", [128, 128], BF16)
    B("triu", [128, 128], F32)
    B("e16", [128, 128], BF16)
    B("sel", [16, 1024], BF16)
    B("ones1f", [1, 128], F32)
    B("pm", [128, 8], F32)
    B("Ftile", [16, SLAB], F32)
    B("Fbf", [16, SLAB], BF16)
    for nm in ("bq", "bo", "l1g", "l1b", "l2g", "l2b", "bf2", "bk"):
        B(nm, [128, 8], F32)
    B("bf1", [128, 32], F32)
    if has_f:
        for i in range(8):
            sb[f"qrhs{i}"] = al.at(f"qrhs{i}", [128, 1024], BF16,
                                   offset=al.offs[f"xsb{i}"])
        sb["scolbc"] = al.at("scolbc", [128, 1024], F32, offset=al.offs["xbf0"])
        sb["rcolbc"] = al.at("rcolbc", [128, 1024], F32, offset=al.offs["xbf4"])
        B("scall", [128, 8], F32)
        B("rcall", [128, 8], F32)
        for i in range(8):
            sb[f"rhsf{i}"] = al.at(f"rhsf{i}", [128, 1024], F32,
                                   offset=kf_off + i * ceil32(T * 2))
        sb["x1t"] = al.at("x1t", [128, 1024], F32, offset=al.offs["qf0"])
        sb["qi"] = al.at("qi", [128, 1024], I32, offset=al.offs["qf4"])
        sb["qlf"] = al.at("qlf", [128, 1024], BF16, offset=al.offs["escr"])
        sb["x0t"] = al.at("x0t", [128, 1024], F32, offset=al.offs["srun"])

    assert al.off <= SBUF_BASE + 208 * 1024, f"SBUF over budget: {al.off}"

    ps = [nc.alloc_psum_tensor(f"ps{i}", [128, 512], F32) for i in range(8)]

    def dma(q, out_ap, in_ap, reads, writes):
        lbl = {"gpsimd": "gdma", "scalar": "sdma"}.get(q, q)
        r.op(lbl, lambda e, o=out_ap, i=in_ap: e.dma_start(out=o, in_=i),
             reads=reads, writes=writes, dma=True)

    def mm(out_ap, lh, rh, st, sp, reads, writes, tile_position=None):
        r.op("tensor",
             lambda e, o=out_ap, a=lh, b=rh, s=st, p=sp, tp=tile_position:
             e.matmul(o, a, b, start=s, stop=p, tile_position=tp),
             reads=reads, writes=writes)

    def vop(fn, reads, writes, eng="vector", big=True):
        r.op(eng, fn, reads=reads, writes=writes, big=big)

    for nm, key in (("idf", "idf"), ("idb", "idb"), ("triu", "triu"),
                    ("e16", "e16"), ("sel", "sel"), ("ones1f", "ones1f")):
        dma("sync", sb[nm][:], dram[key], [], [nm])
    if has_b:
        dma("sync", sb["pm"][:], dram["pmask"], [], ["pm"])
    vop(lambda e: e.memset(sb["onesc"][:], 1.0), [], ["onesc"])
    vop(lambda e: e.memset(sb["cb"][:, 0:1], -LN8), [], ["cb"])
    vop(lambda e: e.memset(sb["cb"][:, 1:2], LN_EPS), [], ["cb"])
    vop(lambda e: e.memset(sb["cb"][:, 2:3], ATTN_EPS), [], ["cb"])

    def load_layer_vecs(which):
        if which == "B":
            for nm, key in (("bq", "bq_b"), ("bo", "bo_b"), ("l1g", "ln1g_b"),
                            ("l1b", "ln1b_b"), ("l2g", "ln2g_b"), ("l2b", "ln2b_b"),
                            ("bf2", "bf2_b")):
                dma("sync", sb[nm][:], dram[key], [], [nm])
            dma("sync", sb["bf1"][:], dram["bf1_b"], [], ["bf1"])
            vop(lambda e: e.tensor_tensor(sb["bsum"][:], sb["l1b"][:], sb["bf2"][:], OP.add),
                ["l1b", "bf2"], ["bsum"])
        else:
            dma("sync", sb["bk"][:], dram["bk_a"], [], ["bk"])
            dma("sync", sb["bvrow"][:], dram["bv_a"], [], ["bvrow"])

    def build_oq(omkey):
        dma("sync", sb["omsb"][:], dram[omkey], [], ["om"])
        vop(lambda e: e.memset(sb["oq"][:], 0.0), [], ["oq"])
        for (r0, c0, sc) in ((0, 0, CS), (0, 32, -CS), (64, 64, CS), (64, 96, -CS)):
            vop(lambda e, r0=r0, c0=c0, sc=sc:
                e.tensor_scalar(sb["oq"][r0:r0 + 64, c0:c0 + 32],
                                sb["omsb"][r0:r0 + 64, 0:32], sc, None, OP.mult),
                ["om", "oq"], ["oq"], big=False)

    def build_bvbc():
        for n2 in range(2):
            mm(ps[2][0:128, 0:512], sb["ones1f"][0:1, 0:128],
               sb["bvrow"][0:1, 512 * n2:512 * n2 + 512], True, True,
               ["ones1f", "bvrow", "ps2"], ["ps2"])
            vop(lambda e, n2=n2: e.tensor_copy(sb["bvbc"][:, 512 * n2:512 * n2 + 512],
                                               ps[2][0:128, 0:512]),
                ["ps2"], ["bvbc"])

    def load_w(dst, key, ncols=D):
        for kc in range(8):
            dma("gpsimd", sb[dst][:, ncols * kc:ncols * kc + ncols],
                dram[key][128 * kc:128 * kc + 128, :], [dst], [dst])

    def load_w_slice(dst, key, r0, r1c, c0, c1):
        n = c1 - c0
        for kc in range((r1c - r0) // 128):
            dma("gpsimd", sb[dst][:, n * kc:n * kc + n],
                dram[key][r0 + 128 * kc:r0 + 128 * kc + 128, c0:c1], [dst], [dst])

    def favor_fm(s, wkey, biaskey, dst_pref, store_whole):
        col = 512 * s if store_whole else 0
        for mc in range(8):
            pb = ps[mc % 2]
            for kc in range(8):
                mm(pb[0:128, 0:512],
                   sb[wkey][:, 1024 * kc + 128 * mc:1024 * kc + 128 * mc + 128],
                   sb[f"xbf{kc}"][:, 0:512], kc == 0, kc == 7,
                   [wkey, f"xbf{kc}", f"ps{mc % 2}"], [f"ps{mc % 2}"])
            vop(lambda e, pb=pb, mc=mc: e.activation(
                sb[f"qs{mc}"][:, 0:512], pb[0:128, 0:512], AF.Identity,
                bias=sb[biaskey][:, mc:mc + 1], scale=1.0),
                [f"ps{mc % 2}", biaskey], [f"qs{mc}"], eng="scalar")
        for mc in range(8):
            sqb = f"sq{mc % 2}"
            vop(lambda e, mc=mc, sqb=sqb: e.activation(
                sb[sqb][:, 0:512], sb[f"qs{mc}"][:, 0:512], AF.Square),
                [f"qs{mc}"], [sqb], eng="scalar")
            mm(ps[3][0:16, 0:512], sb["e16"][:, 16 * mc:16 * mc + 16],
               sb[sqb][:, 0:512], mc == 0, mc == 7,
               ["e16", sqb, "ps3"], ["ps3"])
        vop(lambda e: e.activation(sb["Ftile"][:, 0:512], ps[3][0:16, 0:512],
                                   AF.Exp, bias=sb["cb"][0:16, 0:1], scale=-CQ),
            ["ps3", "cb"], ["Ftile"], eng="scalar")
        vop(lambda e: e.tensor_copy(sb["Fbf"][:, 0:512], sb["Ftile"][:, 0:512]),
            ["Ftile"], ["Fbf"])
        for d in range(8):
            mm(ps[4][0:128, 0:512], sb["oq"][:, 0:128], sb[f"qs{d}"][:, 0:512],
               True, True, ["oq", f"qs{d}", "ps4"], ["ps4"])
            mm(ps[5][0:128, 0:512], sb["sel"][0:16, 128 * d:128 * d + 128],
               sb["Fbf"][0:16, 0:512], True, True,
               ["sel", "Fbf", "ps5"], ["ps5"])
            vop(lambda e: e.activation(sb["escr"][:, 0:512], ps[4][0:128, 0:512], AF.Exp),
                ["ps4"], ["escr"], eng="scalar")
            tgt = f"{dst_pref}{d}"
            vop(lambda e, tgt=tgt, col=col: e.tensor_tensor(
                sb[tgt][:, col:col + 512], sb["escr"][:, 0:512],
                ps[5][0:128, 0:512], OP.mult),
                ["escr", "ps5"], [tgt])

    def phase_a(layer_first):
        build_oq("om_a")
        load_layer_vecs("A")
        build_bvbc()
        load_w("Wa", "wk_a")
        load_w("Wb", "wv_a")
        for s in range(NSLAB):
            for tcl in range(4):
                vop(lambda e, s=s, tcl=tcl: e.memset(
                    sb[f"vt{s}"][:, 1040 * tcl:1040 * tcl + 1040]
                    .rearrange("p (h j) -> p h j", j=65)[:, :, 64:65], 1.0),
                    [f"vt{s}"], [f"vt{s}"])
        vop(lambda e: e.memset(sb["stotsb"][:], 0.0), [], ["stotsb"])
        for s in range(NSLAB):
            if mode == "A":
                for tcl in range(4):
                    c = 4 * s + tcl
                    dma("sync", sb["atm"][:], dram["x0"][128 * c:128 * c + 128, :],
                        ["atm"], ["atm"])
                    for kc in range(8):
                        r.op("tensor", lambda e, kc=kc: e.transpose(
                            ps[6][0:128, 0:128],
                            sb["atm"][:, 128 * kc:128 * kc + 128], sb["idf"][:]),
                            reads=["atm", "idf", "ps6"], writes=["ps6"])
                        vop(lambda e, kc=kc, tcl=tcl: e.tensor_copy(
                            sb[f"xsb{kc}"][:, 128 * tcl:128 * tcl + 128],
                            ps[6][0:128, 0:128]),
                            ["ps6"], [f"xsb{kc}"])
                for kc in range(8):
                    dma("scalar", dram["xout"][kc, s], sb[f"xsb{kc}"][:],
                        [f"xsb{kc}"], [])
            else:
                for kc in range(8):
                    dma("sync", sb[f"xsb{kc}"][:], dram["xmid"][s, kc],
                        [f"xsb{kc}", "xmid"], [f"xsb{kc}"])
            for kc in range(8):
                vop(lambda e, kc=kc: e.tensor_copy(sb[f"xbf{kc}"][:],
                                                   sb[f"xsb{kc}"][:]),
                    [f"xsb{kc}"], [f"xbf{kc}"], eng="gpsimd")
            for tcl in range(4):
                for n2 in range(2):
                    pb = ps[n2]
                    for kc in range(8):
                        mm(pb[0:128, 0:512],
                           sb[f"xbf{kc}"][:, 128 * tcl:128 * tcl + 128],
                           sb["Wb"][:, 1024 * kc + 512 * n2:1024 * kc + 512 * n2 + 512],
                           kc == 0, kc == 7,
                           [f"xbf{kc}", "Wb", f"ps{n2}"], [f"ps{n2}"])
                    vop(lambda e, s=s, tcl=tcl, n2=n2, pb=pb: e.tensor_tensor(
                        sb[f"vt{s}"][:, 1040 * tcl + 520 * n2:1040 * tcl + 520 * n2 + 520]
                        .rearrange("p (h j) -> p h j", j=65)[:, :, 0:64],
                        pb[0:128, 0:512].rearrange("p (h j) -> p h j", j=64),
                        sb["bvbc"][:, 512 * n2:512 * n2 + 512]
                        .rearrange("p (h j) -> p h j", j=64), OP.add),
                        [f"ps{n2}", "bvbc", f"vt{s}"], [f"vt{s}"])
            favor_fm(s, "Wa", "bk", "kf", True)
            for tcl in range(4):
                c = 4 * s + tcl
                for d in range(8):
                    pb = ps[2 + d // 4]
                    mm(pb[0:128, 128 * (d % 4):128 * (d % 4) + 128],
                       sb[f"qs{d}"][:, 128 * tcl:128 * tcl + 128],
                       sb["oq"][:, 0:128], True, True,
                       [f"qs{d}", "oq", f"ps{2 + d // 4}"], [f"ps{2 + d // 4}"])
                r.op("tensor", lambda e, tcl=tcl: e.transpose(
                    ps[7][0:128, 0:16],
                    sb["Ftile"][0:16, 128 * tcl:128 * tcl + 128],
                    sb["idf"][0:16, 0:16]),
                    reads=["Ftile", "idf", "ps7"], writes=["ps7"])
                for half in range(2):
                    vop(lambda e, half=half: e.activation(
                        sb["etm"][:, 512 * half:512 * half + 512],
                        ps[2 + half][0:128, 0:512], AF.Exp),
                        [f"ps{2 + half}"], ["etm"], eng="scalar")
                vop(lambda e: e.tensor_copy(sb["atm"][:, 0:16], ps[7][0:128, 0:16]),
                    ["ps7"], ["atm"], big=False)
                vop(lambda e: e.tensor_tensor(
                    sb["schb1"][:, 0:1024].rearrange("p (h j) -> p h j", j=64),
                    sb["etm"][:, 0:1024].rearrange("p (h j) -> p h j", j=64),
                    sb["atm"][:, 0:16].rearrange("p (h j) -> p h j", j=1)
                    .broadcast_to((128, 16, 64)), OP.mult),
                    ["etm", "atm"], ["schb1"])
                for h in range(16):
                    bk, col = ("ps4", 65 * h) if h < 7 else (
                        ("ps5", 65 * (h - 7)) if h < 14 else ("ps6", 65 * (h - 14)))
                    mm(ps[int(bk[2])][0:64, col:col + 65],
                       sb["schb1"][:, 64 * h:64 * h + 64],
                       sb[f"vt{s}"][:, 1040 * tcl + 65 * h:1040 * tcl + 65 * h + 65],
                       True, True, ["schb1", f"vt{s}", bk], [bk])
                for (bk, c0, w) in (("ps4", 0, 455), ("ps5", 455, 455), ("ps6", 910, 130)):
                    vop(lambda e, bk=bk, c0=c0, w=w: e.tensor_copy(
                        sb["schb0"][0:64, c0:c0 + w], ps[int(bk[2])][0:64, 0:w]),
                        [bk], ["schb0"])
                vop(lambda e: e.tensor_tensor(sb["stotsb"][0:64, :],
                                              sb["stotsb"][0:64, :],
                                              sb["schb0"][0:64, :], OP.add),
                    ["schb0", "stotsb"], ["stotsb"])
                dma("scalar", dram["sch_out"][c], sb["schb0"][0:64, :], ["schb0"], [])
        for d in range(8):
            dma("scalar", dram["kf_out"][d], sb[f"kf{d}"][:], [f"kf{d}"], [])
        for s in range(NSLAB):
            dma("scalar", dram["vt_out"][s], sb[f"vt{s}"][:], [f"vt{s}"], [])
        vop(lambda e: e.tensor_copy(sb["schb0"][0:64, :], sb["stotsb"][0:64, :]),
            ["stotsb"], ["schb0"])
        dma("scalar", dram["stot"], sb["schb0"][0:64, :], ["schb0"], [])

    def phase_b():
        build_oq("om_b")
        load_layer_vecs("B")
        vop(lambda e: e.memset(sb["srun"][:], 0.0), [], ["srun"])
        for j in range(8):
            dma("sync", sb[f"kf{j}"][0:64, 0:1040], dram["gst"][j],
                [f"kf{j}"], [f"kf{j}"])
            dma("sync", sb[f"kf{j}"][64:128, 0:1040], dram["gst"][j],
                [f"kf{j}"], [f"kf{j}"])
        for j in range(8):
            vop(lambda e, j=j: e.scalar_tensor_tensor(
                sb["srun"][:], sb[f"kf{j}"][:, 0:1040], sb["pm"][:, j:j + 1],
                sb["srun"][:], OP.mult, OP.add),
                [f"kf{j}", "pm", "srun"], ["srun"])
        vop(lambda e: e.tensor_copy(sb["sbf"][:], sb["srun"][:]), ["srun"], ["sbf"])
        load_w("Wa", "wq_b")
        load_w("Wb", "wo_b")
        for d in range(8):
            dma("sync", sb[f"kf{d}"][:], dram["kf_in"][d], [f"kf{d}"], [f"kf{d}"])
        for s in range(NSLAB):
            dma("sync", sb[f"vt{s}"][:], dram["vt_in"][s], [f"vt{s}"], [f"vt{s}"])
        if DEBUG:
            dma("sync", dram["dbg_sbf"], sb["sbf"][:], ["sbf"], [])

        for s in range(NSLAB):
            for kc in range(8):
                dma("sync", sb[f"xsb{kc}"][:], dram["xin"][kc, s],
                    [f"xsb{kc}"], [f"xsb{kc}"])
                vop(lambda e, kc=kc: e.tensor_copy(sb[f"xbf{kc}"][:],
                                                   sb[f"xsb{kc}"][:]),
                    [f"xsb{kc}"], [f"xbf{kc}"], eng="gpsimd")
            favor_fm(s, "Wa", "bq", "qf", False)
            if DEBUG and s == 0:
                for dd in range(8):
                    dma("sync", dram["dbg_qf"][dd], sb[f"qf{dd}"][:], [f"qf{dd}"], [])
            for tcl in range(4):
                c = 4 * s + tcl
                if c > 0:
                    buf = f"schb{c % 2}"
                    dma("sync", sb[buf][0:64, :], dram["sch_in"][c - 1], [buf], [buf])
                    dma("sync", sb[buf][64:128, :], dram["sch_in"][c - 1], [buf], [buf])
                    vop(lambda e, buf=buf: e.tensor_tensor(
                        sb["srun"][:], sb["srun"][:], sb[buf][:], OP.add),
                        [buf, "srun"], ["srun"])
                    vop(lambda e: e.tensor_copy(sb["sbf"][:], sb["srun"][:]),
                        ["srun"], ["sbf"])
                if not SCAN_BATCHED:
                    A2COL = 128 if SCAN_SER_COLS else 0
                    for h in range(16):
                        d, base = h // 2, 64 * (h % 2)
                        acol = A2COL * h if SCAN_SER_COLS else 0
                        mm(ps[3][0:128, 0:128],
                           sb[f"kf{d}"][base:base + 64, 128 * c:128 * c + 128],
                           sb[f"qf{d}"][base:base + 64, 128 * tcl:128 * tcl + 128],
                           True, True, [f"kf{d}", f"qf{d}", "ps3"], ["ps3"])
                        vop(lambda e, acol=acol: e.tensor_tensor(
                            sb["a2m"][:, acol:acol + 128],
                            ps[3][0:128, 0:128],
                            sb["triu"][:], OP.mult),
                            ["ps3", "triu"], ["am0"])
                        mm(ps[4][0:128, 0:65], sb["a2m"][:, acol:acol + 128],
                           sb[f"vt{s}"][:, 1040 * tcl + 65 * h:1040 * tcl + 65 * h + 65],
                           True, False, ["am0", f"vt{s}"], ["nm0", "ps4"])
                        mm(ps[4][0:128, 0:65],
                           sb[f"qf{d}"][base:base + 64, 128 * tcl:128 * tcl + 128],
                           sb["sbf"][base:base + 64, 65 * h:65 * h + 65],
                           False, True, [f"qf{d}", "sbf", "nm0"], ["nm0", "ps4"])
                        vop(lambda e: e.activation(sb["dcol"][:, 0:1],
                                                   ps[4][0:128, 64:65], AF.Identity,
                                                   bias=sb["cb"][:, 2:3], scale=1.0),
                            ["nm0", "cb", "ps4"], ["dc0"], eng="scalar", big=False)
                        vop(lambda e: e.reciprocal(sb["dcol"][:, 1:2],
                                                   sb["dcol"][:, 0:1]),
                            ["dc0"], ["dr0"], big=False)
                        vop(lambda e, h=h: e.activation(
                            sb["atm"][:, 64 * h:64 * h + 64], ps[4][0:128, 0:64],
                            AF.Identity, bias=0.0, scale=sb["dcol"][:, 1:2]),
                            ["nm0", "dr0", "ps4"], ["atm"], eng="scalar", big=False)
                if SCAN_BATCHED == 4:
                    banks4 = ["ps2", "ps3", "ps5", "ps6"]
                    for h in range(16):
                        d, base = h // 2, 64 * (h % 2)
                        bk = banks4[h % 4]
                        pb = ps[int(bk[2])]
                        mm(pb[0:128, 0:128],
                           sb[f"kf{d}"][base:base + 64, 128 * c:128 * c + 128],
                           sb[f"qf{d}"][base:base + 64, 128 * tcl:128 * tcl + 128],
                           True, True, [f"kf{d}", f"qf{d}", bk], [bk])
                        vop(lambda e, h=h, pb=pb: e.tensor_tensor(
                            sb["a2m"][:, 128 * h:128 * h + 128],
                            pb[0:128, 0:128], sb["triu"][:], OP.mult),
                            [bk, "triu"], [f"am{h}"])
                    for h in range(16):
                        d, base = h // 2, 64 * (h % 2)
                        nb = "ps4" if h % 2 == 0 else "ps7"
                        nap = ps[int(nb[2])][0:128, 0:65]
                        mm(nap, sb["a2m"][:, 128 * h:128 * h + 128],
                           sb[f"vt{s}"][:, 1040 * tcl + 65 * h:1040 * tcl + 65 * h + 65],
                           True, False, [f"am{h}", f"vt{s}"], [f"nm{h}", nb])
                        mm(nap,
                           sb[f"qf{d}"][base:base + 64, 128 * tcl:128 * tcl + 128],
                           sb["sbf"][base:base + 64, 65 * h:65 * h + 65],
                           False, True, [f"qf{d}", "sbf", f"nm{h}"], [f"nm{h}", nb])
                        vop(lambda e, h=h, nb=nb: e.activation(
                            sb["dcol"][:, 2 * h:2 * h + 1],
                            ps[int(nb[2])][0:128, 64:65], AF.Identity,
                            bias=sb["cb"][:, 2:3], scale=1.0),
                            [f"nm{h}", "cb", nb], [f"dc{h}"], eng="scalar", big=False)
                        vop(lambda e, h=h: e.reciprocal(
                            sb["dcol"][:, 2 * h + 1:2 * h + 2],
                            sb["dcol"][:, 2 * h:2 * h + 1]),
                            [f"dc{h}"], [f"dr{h}"], big=False)
                        vop(lambda e, h=h, nb=nb: e.activation(
                            sb["atm"][:, 64 * h:64 * h + 64],
                            ps[int(nb[2])][0:128, 0:64],
                            AF.Identity, bias=0.0,
                            scale=sb["dcol"][:, 2 * h + 1:2 * h + 2]),
                            [f"nm{h}", f"dr{h}", nb], ["atm"], eng="scalar", big=False)
                A2B = {0: "ps2", 1: "ps3", 2: "ps5", 3: "ps6"}
                if SCAN_BATCHED != 2:
                    A2B = None
                NB = {}
                for h in range(16):
                    if h < 7:
                        NB[h] = ("ps4", 66 * h)
                    elif h < 14:
                        NB[h] = ("ps7", 66 * (h - 7))
                    else:
                        NB[h] = ("ps2", 66 * (h - 14))
                a2ap = {}
                for h in (range(16) if SCAN_BATCHED == 2 else []):
                    d, base = h // 2, 64 * (h % 2)
                    bk = A2B[h // 4]
                    a2ap[h] = ps[int(bk[2])][0:128, 128 * (h % 4):128 * (h % 4) + 128]
                    mm(a2ap[h],
                       sb[f"kf{d}"][base:base + 64, 128 * c:128 * c + 128],
                       sb[f"qf{d}"][base:base + 64, 128 * tcl:128 * tcl + 128],
                       True, True, [f"kf{d}", f"qf{d}", bk], [bk])
                for h in (range(16) if SCAN_BATCHED == 2 else []):
                    bk = A2B[h // 4]
                    vop(lambda e, h=h, ap=a2ap[h]: e.tensor_tensor(
                        sb["a2m"][:, 128 * h:128 * h + 128], ap,
                        sb["triu"][:], OP.mult),
                        [bk, "triu"], [f"am{h}"])
                if SCAN_BATCHED == 1:
                    for h in range(16):
                        d, base = h // 2, 64 * (h % 2)
                        mm(ps[4][0:128, 0:65], sb["a2m"][:, 128 * h:128 * h + 128],
                           sb[f"vt{s}"][:, 1040 * tcl + 65 * h:1040 * tcl + 65 * h + 65],
                           True, False, [f"am{h}", f"vt{s}"], ["nm0", "ps4"])
                        mm(ps[4][0:128, 0:65],
                           sb[f"qf{d}"][base:base + 64, 128 * tcl:128 * tcl + 128],
                           sb["sbf"][base:base + 64, 65 * h:65 * h + 65],
                           False, True, [f"qf{d}", "sbf", "nm0"], ["nm0", "ps4"])
                        vop(lambda e: e.activation(sb["dcol"][:, 0:1],
                                                   ps[4][0:128, 64:65], AF.Identity,
                                                   bias=sb["cb"][:, 2:3], scale=1.0),
                            ["nm0", "cb", "ps4"], ["dc0"], eng="scalar", big=False)
                        vop(lambda e: e.reciprocal(sb["dcol"][:, 1:2],
                                                   sb["dcol"][:, 0:1]),
                            ["dc0"], ["dr0"], big=False)
                        vop(lambda e, h=h: e.activation(
                            sb["atm"][:, 64 * h:64 * h + 64], ps[4][0:128, 0:64],
                            AF.Identity, bias=0.0, scale=sb["dcol"][:, 1:2]),
                            ["nm0", "dr0", "ps4"], ["atm"], eng="scalar", big=False)
                for h in (range(16) if SCAN_BATCHED == 2 else []):
                    d, base = h // 2, 64 * (h % 2)
                    nb, ncol = NB[h]
                    nap = ps[int(nb[2])][0:128, ncol:ncol + 65]
                    mm(nap, sb["a2m"][:, 128 * h:128 * h + 128],
                       sb[f"vt{s}"][:, 1040 * tcl + 65 * h:1040 * tcl + 65 * h + 65],
                       True, False, [f"am{h}", f"vt{s}"], [f"nm{h}", nb])
                    mm(nap,
                       sb[f"qf{d}"][base:base + 64, 128 * tcl:128 * tcl + 128],
                       sb["sbf"][base:base + 64, 65 * h:65 * h + 65],
                       False, True, [f"qf{d}", "sbf", f"nm{h}"], [f"nm{h}", nb])
                for h in (range(16) if SCAN_BATCHED == 2 else []):
                    nb, ncol = NB[h]
                    vop(lambda e, h=h, nb=nb, ncol=ncol: e.activation(
                        sb["dcol"][:, 2 * h:2 * h + 1],
                        ps[int(nb[2])][0:128, ncol + 64:ncol + 65], AF.Identity,
                        bias=sb["cb"][:, 2:3], scale=1.0),
                        [f"nm{h}", "cb", nb], [f"dc{h}"], eng="scalar", big=False)
                    vop(lambda e, h=h: e.reciprocal(
                        sb["dcol"][:, 2 * h + 1:2 * h + 2],
                        sb["dcol"][:, 2 * h:2 * h + 1]),
                        [f"dc{h}"], [f"dr{h}"], big=False)
                for h in (range(16) if SCAN_BATCHED else []):
                    nb, ncol = NB[h]
                    vop(lambda e, h=h, nb=nb, ncol=ncol: e.activation(
                        sb["atm"][:, 64 * h:64 * h + 64],
                        ps[int(nb[2])][0:128, ncol:ncol + 64],
                        AF.Identity, bias=0.0,
                        scale=sb["dcol"][:, 2 * h + 1:2 * h + 2]),
                        [f"nm{h}", f"dr{h}", nb], ["atm"], eng="scalar", big=False)
                if DEBUG and s == 0 and tcl == 0:
                    dma("sync", dram["dbg_atm"], sb["atm"][:], ["atm"], [])
                for j in range(8):
                    bk = "ps3" if j % 2 == 0 else "ps5"
                    pb = ps[int(bk[2])]
                    r.op("tensor", lambda e, j=j, pb=pb: e.transpose(
                        pb[0:128, 0:128], sb["atm"][:, 128 * j:128 * j + 128],
                        sb["idf"][:]),
                        reads=["atm", "idf", bk] + [f"am{h}" for h in range(16)] if j == 0 else ["atm", "idf", bk],
                        writes=[bk])
                    vop(lambda e, j=j, tcl=tcl, pb=pb: e.tensor_copy(
                        sb[f"qs{j}"][:, 128 * tcl:128 * tcl + 128],
                        pb[0:128, 0:128]),
                        [bk], [f"qs{j}"])
            for mc in range(8):
                pb = ps[mc % 2]
                for kc in range(8):
                    mm(pb[0:128, 0:512],
                       sb["Wb"][:, 1024 * kc + 128 * mc:1024 * kc + 128 * mc + 128],
                       sb[f"qs{kc}"][:, 0:512], kc == 0, kc == 7,
                       ["Wb", f"qs{kc}", f"ps{mc % 2}"], [f"ps{mc % 2}"])
                vop(lambda e, pb=pb, mc=mc: e.scalar_tensor_tensor(
                    sb[f"hy{mc}"][:], pb[0:128, 0:512], sb["bo"][:, mc:mc + 1],
                    sb[f"xsb{mc}"][:], OP.add, OP.add),
                    [f"ps{mc % 2}", "bo", f"xsb{mc}"], [f"hy{mc}"])
                dma("scalar", dram["rmid"][s, mc], sb[f"hy{mc}"][:],
                    [f"hy{mc}"], ["rmid"])
                if DEBUG and s == 0:
                    dma("sync", dram["dbg_hy"][mc], sb[f"hy{mc}"][:],
                        [f"hy{mc}"], [])

        def ln_stats(src_tensors, src_tokens):
            for kc in range(8):
                mm(ps[6][0:1, 0:512], sb["onesc"][:, 0:1],
                   src_tensors[kc][:, 0:512], kc == 0, kc == 7,
                   ["onesc", src_tokens[kc], "ps6"], ["ps6"])
            for kc in range(8):
                vop(lambda e, kc=kc: e.activation(
                    sb["t1"][:, 0:512], src_tensors[kc][:, 0:512], AF.Square),
                    [src_tokens[kc]], ["t1"], eng="scalar")
                mm(ps[7][0:1, 0:512], sb["onesc"][:, 0:1],
                   sb["t1"][:, 0:512], kc == 0, kc == 7,
                   ["onesc", "t1", "ps7"], ["ps7"])
            vop(lambda e: e.tensor_scalar(sb["rowb"][0:1, :], ps[6][0:1, 0:512],
                                          1.0 / 1024.0, None, OP.mult),
                ["ps6"], ["rowb"])
            vop(lambda e: e.tensor_scalar(sb["rowb2"][0:1, :], ps[7][0:1, 0:512],
                                          1.0 / 1024.0, None, OP.mult),
                ["ps7"], ["rowb2"])
            vop(lambda e: e.tensor_tensor(sb["Ftile"][0:1, 0:512],
                                          sb["rowb"][0:1, :], sb["rowb"][0:1, :],
                                          OP.mult),
                ["rowb"], ["Ftile"])
            vop(lambda e: e.tensor_tensor(sb["rowb2"][0:1, :], sb["rowb2"][0:1, :],
                                          sb["Ftile"][0:1, 0:512], OP.subtract),
                ["rowb2", "Ftile"], ["rowb2"])
            vop(lambda e: e.activation(sb["rowb2"][0:1, :], sb["rowb2"][0:1, :],
                                       AF.Sqrt, bias=sb["cb"][0:1, 1:2], scale=1.0),
                ["rowb2", "cb"], ["rowb2"], eng="scalar")
            vop(lambda e: e.reciprocal(sb["rowb2"][0:1, :], sb["rowb2"][0:1, :]),
                ["rowb2"], ["rowb2"])
            vop(lambda e: e.tensor_tensor(sb["rowb"][0:1, :], sb["rowb"][0:1, :],
                                          sb["rowb2"][0:1, :], OP.mult),
                ["rowb", "rowb2"], ["rowb"])
            mm(ps[6][0:128, 0:512], sb["ones1f"][0:1, 0:128], sb["rowb2"][0:1, :],
               True, True, ["ones1f", "rowb2", "ps6"], ["ps6"])
            mm(ps[7][0:128, 0:512], sb["ones1f"][0:1, 0:128], sb["rowb"][0:1, :],
               True, True, ["ones1f", "rowb", "ps7"], ["ps7"])

        for s in range(NSLAB):
            for kc in range(8):
                dma("sync", sb[f"xsb{kc}"][:], dram["rmid"][s, kc],
                    [f"xsb{kc}", "rmid"], [f"xsb{kc}"])
            ln_stats([sb[f"xsb{kc}"] for kc in range(8)],
                     [f"xsb{kc}" for kc in range(8)])
            for kc in range(8):
                vop(lambda e, kc=kc: e.tensor_tensor(sb["t1"][:], sb[f"xsb{kc}"][:],
                                                     ps[6][0:128, 0:512], OP.mult),
                    [f"xsb{kc}", "ps6"], ["t1"])
                vop(lambda e: e.tensor_tensor(sb["t1"][:], sb["t1"][:],
                                              ps[7][0:128, 0:512], OP.subtract),
                    ["t1", "ps7"], ["t1"])
                vop(lambda e, kc=kc, s=s: e.tensor_scalar(
                    sb[f"kf{kc}"][:, 512 * s:512 * s + 512], sb["t1"][:],
                    sb["l1g"][:, kc:kc + 1], sb["l1b"][:, kc:kc + 1],
                    OP.mult, OP.add),
                    ["t1", "l1g", "l1b"], [f"kf{kc}"])
        for g in range(4):
            load_w_slice("Wa", "w1_b", 0, 1024, 1024 * g, 1024 * g + 1024)
            load_w_slice("Wb", "w2_b", 1024 * g, 1024 * g + 1024, 0, 1024)
            for s in range(NSLAB):
                for mc in range(8):
                    pb = ps[mc % 2]
                    for kc in range(8):
                        mm(pb[0:128, 0:512],
                           sb["Wa"][:, 1024 * kc + 128 * mc:1024 * kc + 128 * mc + 128],
                           sb[f"kf{kc}"][:, 512 * s:512 * s + 512], kc == 0, kc == 7,
                           ["Wa", f"kf{kc}", f"ps{mc % 2}"], [f"ps{mc % 2}"])
                    vop(lambda e, pb=pb, g=g, mc=mc: e.activation(
                        sb[f"qs{mc}"][:, 0:512], pb[0:128, 0:512], AF.Relu,
                        bias=sb["bf1"][:, 8 * g + mc:8 * g + mc + 1], scale=1.0),
                        [f"ps{mc % 2}", "bf1"], [f"qs{mc}"], eng="scalar")
                for mc2 in range(8):
                    pb = ps[2 + mc2 % 2]
                    for mc in range(8):
                        mm(pb[0:128, 0:512],
                           sb["Wb"][:, 1024 * mc + 128 * mc2:1024 * mc + 128 * mc2 + 128],
                           sb[f"qs{mc}"][:, 0:512], mc == 0, mc == 7,
                           ["Wb", f"qs{mc}", f"ps{2 + mc2 % 2}"], [f"ps{2 + mc2 % 2}"])
                    yt = f"vt{mc2 // 2}"
                    ycols = slice(2080 * (mc2 % 2) + 520 * 0 + 512 * s,
                                  2080 * (mc2 % 2) + 512 * s + 512)
                    if g == 0:
                        vop(lambda e, pb=pb, yt=yt, yc=ycols: e.tensor_copy(
                            sb[yt][:, yc], pb[0:128, 0:512]),
                            [f"ps{2 + mc2 % 2}"], [yt])
                    else:
                        vop(lambda e, pb=pb, yt=yt, yc=ycols: e.tensor_tensor(
                            sb[yt][:, yc], sb[yt][:, yc], pb[0:128, 0:512], OP.add),
                            [f"ps{2 + mc2 % 2}", yt], [yt])
        for s in range(NSLAB):
            for kc in range(8):
                dma("sync", sb[f"xsb{kc}"][:], dram["rmid"][s, kc],
                    [f"xsb{kc}", "rmid"], [f"xsb{kc}"])
            ln_stats([sb[f"xsb{kc}"] for kc in range(8)],
                     [f"xsb{kc}" for kc in range(8)])
            for kc in range(8):
                vop(lambda e, kc=kc: e.tensor_tensor(sb["t1"][:], sb[f"xsb{kc}"][:],
                                                     ps[6][0:128, 0:512], OP.mult),
                    [f"xsb{kc}", "ps6"], ["t1"])
                vop(lambda e: e.tensor_tensor(sb["t1"][:], sb["t1"][:],
                                              ps[7][0:128, 0:512], OP.subtract),
                    ["t1", "ps7"], ["t1"])
                vop(lambda e, kc=kc: e.tensor_scalar(
                    sb["t1"][:], sb["t1"][:], sb["l1g"][:, kc:kc + 1],
                    sb["bsum"][:, kc:kc + 1], OP.mult, OP.add),
                    ["t1", "l1g", "bsum"], ["t1"])
                yt = f"vt{kc // 2}"
                ycols = slice(2080 * (kc % 2) + 512 * s, 2080 * (kc % 2) + 512 * s + 512)
                vop(lambda e, yt=yt, yc=ycols, kc=kc: e.tensor_tensor(
                    sb[f"hy{kc}"][:], sb["t1"][:], sb[yt][:, yc], OP.add),
                    ["t1", yt], [f"hy{kc}"])
            ln_stats([sb[f"hy{kc}"] for kc in range(8)],
                     [f"hy{kc}" for kc in range(8)])
            for kc in range(8):
                vop(lambda e, kc=kc: e.tensor_tensor(sb["t1"][:], sb[f"hy{kc}"][:],
                                                     ps[6][0:128, 0:512], OP.mult),
                    [f"hy{kc}", "ps6"], ["t1"])
                vop(lambda e: e.tensor_tensor(sb["t1"][:], sb["t1"][:],
                                              ps[7][0:128, 0:512], OP.subtract),
                    ["t1", "ps7"], ["t1"])
                vop(lambda e, kc=kc: e.tensor_scalar(
                    sb[f"xsb{kc}"][:], sb["t1"][:], sb["l2g"][:, kc:kc + 1],
                    sb["l2b"][:, kc:kc + 1], OP.mult, OP.add),
                    ["t1", "l2g", "l2b"], [f"xsb{kc}"])
                dma("scalar", dram["xmid"][s, kc], sb[f"xsb{kc}"][:],
                    [f"xsb{kc}"], ["xmid"])
                if has_a:
                    dma("scalar", dram["xout"][kc, s], sb[f"xsb{kc}"][:],
                        [f"xsb{kc}"], [])

    def rhs_prep():
        for kc in range(8):
            dma("sync", sb[f"rhsf{kc}"][:], dram["rhs"][128 * kc:128 * kc + 128, :],
                [f"kf{kc}"], [f"kf{kc}"])
        for nc2 in range(8):
            for kc in range(8):
                r.op("tensor", lambda e, kc=kc, nc2=nc2: e.transpose(
                    ps[5][0:128, 0:128],
                    sb[f"rhsf{kc}"][:, 128 * nc2:128 * nc2 + 128], sb["idf"][:]),
                    reads=[f"kf{kc}", "idf", "ps5"], writes=["ps5"])
                vop(lambda e, kc=kc: e.tensor_copy(
                    sb["etm"][:, 128 * kc:128 * kc + 128], ps[5][0:128, 0:128]),
                    ["ps5"], ["etm"])
            vop(lambda e: e.tensor_reduce(sb["st8"][:, 0:1], sb["etm"][:],
                                          axis=AX.X, op=OP.max,
                                          apply_absolute_value=True),
                ["etm"], ["st8"], big=False)
            vop(lambda e: e.tensor_scalar(sb["st8"][:, 1:2], sb["st8"][:, 0:1],
                                          1e-12, INV127, OP.max, OP.mult),
                ["st8"], ["st8"], big=False)
            vop(lambda e, nc2=nc2: e.tensor_copy(sb["scall"][:, nc2:nc2 + 1],
                                                 sb["st8"][:, 1:2]),
                ["st8"], ["scall"], big=False)
            vop(lambda e, nc2=nc2: e.reciprocal(sb["rcall"][:, nc2:nc2 + 1],
                                                sb["st8"][:, 1:2]),
                ["st8"], ["rcall"], big=False)
        for which, src, dst in (("s", "scall", "scolbc"), ("r", "rcall", "rcolbc")):
            for nc2 in range(8):
                mm(ps[5][0:1, 0:128], sb[src][:, nc2:nc2 + 1], sb["idf"][:],
                   True, True, [src, "idf", "ps5"], ["ps5"])
                vop(lambda e: e.tensor_copy(sb["rowb"][0:1, 0:128], ps[5][0:1, 0:128]),
                    ["ps5"], ["rowb"])
                mm(ps[6][0:128, 0:128], sb["ones1f"][0:1, 0:128],
                   sb["rowb"][0:1, 0:128], True, True,
                   ["ones1f", "rowb", "ps6"], ["ps6"])
                vop(lambda e, dst=dst, nc2=nc2: e.tensor_copy(
                    sb[dst][:, 128 * nc2:128 * nc2 + 128], ps[6][0:128, 0:128]),
                    ["ps6"], [dst])
        for kc in range(8):
            vop(lambda e, kc=kc: e.tensor_tensor(sb["atm"][:], sb[f"rhsf{kc}"][:],
                                                 sb["rcolbc"][:], OP.mult),
                [f"kf{kc}", "rcolbc"], ["atm"])
            vop(lambda e: e.tensor_copy(sb["qi"][:], sb["atm"][:]), ["atm"], ["qi"])
            vop(lambda e, kc=kc: e.tensor_copy(sb[f"qrhs{kc}"][:], sb["qi"][:]),
                ["qi"], [f"qrhs{kc}"])

    def ln_plain_tm(src_key, out_key, out_tensor, src_tensor):
        vop(lambda e: e.tensor_reduce(sb["st8"][:, 2:3], src_tensor[:],
                                      axis=AX.X, op=OP.add),
            [src_key], ["st8"])
        vop(lambda e: e.activation(sb["atm"][:], src_tensor[:], AF.Square),
            [src_key], ["atm"], eng="scalar")
        vop(lambda e: e.tensor_reduce(sb["st8"][:, 3:4], sb["atm"][:],
                                      axis=AX.X, op=OP.add),
            ["atm"], ["st8"], big=False)
        vop(lambda e: e.tensor_scalar(sb["st8"][:, 4:5], sb["st8"][:, 2:3],
                                      1.0 / 1024.0, None, OP.mult),
            ["st8"], ["st8"], big=False)
        vop(lambda e: e.tensor_scalar(sb["st8"][:, 5:6], sb["st8"][:, 3:4],
                                      1.0 / 1024.0, None, OP.mult),
            ["st8"], ["st8"], big=False)
        vop(lambda e: e.tensor_tensor(sb["st8"][:, 6:7], sb["st8"][:, 4:5],
                                      sb["st8"][:, 4:5], OP.mult),
            ["st8"], ["st8"], big=False)
        vop(lambda e: e.tensor_tensor(sb["st8"][:, 6:7], sb["st8"][:, 5:6],
                                      sb["st8"][:, 6:7], OP.subtract),
            ["st8"], ["st8"], big=False)
        vop(lambda e: e.activation(sb["st8"][:, 7:8], sb["st8"][:, 6:7],
                                   AF.Sqrt, bias=sb["cb"][:, 1:2], scale=1.0),
            ["st8", "cb"], ["st8"], eng="scalar", big=False)
        vop(lambda e: e.reciprocal(sb["st8"][:, 8:9], sb["st8"][:, 7:8]),
            ["st8"], ["st8"], big=False)
        vop(lambda e: e.tensor_scalar(out_tensor[:], src_tensor[:],
                                      sb["st8"][:, 4:5], sb["st8"][:, 8:9],
                                      OP.subtract, OP.mult),
            [src_key, "st8"], [out_key])

    def final_block():
        for tc in range(16):
            s, tcl = tc // 4, tc % 4
            for kc in range(8):
                dma("sync", sb["atm"][:, 128 * kc:128 * kc + 128],
                    dram["xmid"][s, kc][:, 128 * tcl:128 * tcl + 128],
                    ["atm", "xmid"], ["atm"])
            dma("sync", sb["x0t"][:], dram["x0"][128 * tc:128 * tc + 128, :],
                ["x0t"], ["x0t"])
            for kc in range(8):
                r.op("tensor", lambda e, kc=kc: e.transpose(
                    ps[5][0:128, 0:128], sb["atm"][:, 128 * kc:128 * kc + 128],
                    sb["idf"][:]),
                    reads=["atm", "idf", "ps5"], writes=["ps5"])
                vop(lambda e, kc=kc: e.tensor_tensor(
                    sb["etm"][:, 128 * kc:128 * kc + 128], ps[5][0:128, 0:128],
                    sb["x0t"][:, 128 * kc:128 * kc + 128], OP.add),
                    ["ps5", "x0t"], ["etm"])
            ln_plain_tm("etm", "x1t", sb["x1t"], sb["etm"])
            vop(lambda e: e.tensor_reduce(sb["st8"][:, 0:1], sb["x1t"][:],
                                          axis=AX.X, op=OP.max,
                                          apply_absolute_value=True),
                ["x1t"], ["st8"], big=False)
            vop(lambda e: e.tensor_scalar(sb["st8"][:, 1:2], sb["st8"][:, 0:1],
                                          1e-12, INV127, OP.max, OP.mult),
                ["st8"], ["st8"], big=False)
            vop(lambda e: e.reciprocal(sb["st8"][:, 9:10], sb["st8"][:, 1:2]),
                ["st8"], ["st8"], big=False)
            vop(lambda e: e.tensor_scalar(sb["atm"][:], sb["x1t"][:],
                                          sb["st8"][:, 9:10], None, OP.mult),
                ["x1t", "st8"], ["atm"])
            vop(lambda e: e.tensor_copy(sb["qi"][:], sb["atm"][:]), ["atm"], ["qi"])
            vop(lambda e: e.tensor_copy(sb["atm"][:], sb["qi"][:]), ["qi"], ["atm"])
            for j in range(8):
                r.op("tensor", lambda e, j=j: e.transpose(
                    ps[5][0:128, 0:128], sb["atm"][:, 128 * j:128 * j + 128],
                    sb["idf"][:]),
                    reads=["atm", "idf", "ps5"], writes=["ps5"])
                vop(lambda e, j=j: e.tensor_copy(
                    sb["qlf"][:, 128 * j:128 * j + 128], ps[5][0:128, 0:128]),
                    ["ps5"], ["qlf"])
            for n2 in range(2):
                pb = ps[n2]
                for kc in range(8):
                    mm(pb[0:128, 0:512], sb["qlf"][:, 128 * kc:128 * kc + 128],
                       sb[f"qrhs{kc}"][:, 512 * n2:512 * n2 + 512], kc == 0, kc == 7,
                       ["qlf", f"qrhs{kc}", f"ps{n2}"], [f"ps{n2}"])
                vop(lambda e, pb=pb, n2=n2: e.scalar_tensor_tensor(
                    sb[f"hy{n2}"][:], pb[0:128, 0:512], sb["st8"][:, 1:2],
                    sb["scolbc"][:, 512 * n2:512 * n2 + 512], OP.mult, OP.mult),
                    [f"ps{n2}", "st8", "scolbc"], [f"hy{n2}"])
                vop(lambda e, n2=n2: e.tensor_tensor(
                    sb[f"hy{n2}"][:], sb[f"hy{n2}"][:],
                    sb["x1t"][:, 512 * n2:512 * n2 + 512], OP.add),
                    [f"hy{n2}", "x1t"], [f"hy{n2}"])
            vop(lambda e: e.tensor_reduce(sb["st8"][:, 2:3], sb["hy0"][:],
                                          axis=AX.X, op=OP.add),
                ["hy0"], ["st8"], big=False)
            vop(lambda e: e.tensor_reduce(sb["st8"][:, 3:4], sb["hy1"][:],
                                          axis=AX.X, op=OP.add),
                ["hy1"], ["st8"], big=False)
            vop(lambda e: e.tensor_tensor(sb["st8"][:, 2:3], sb["st8"][:, 2:3],
                                          sb["st8"][:, 3:4], OP.add),
                ["st8"], ["st8"], big=False)
            vop(lambda e: e.activation(sb["atm"][:, 0:512], sb["hy0"][:], AF.Square),
                ["hy0"], ["atm"], eng="scalar")
            vop(lambda e: e.activation(sb["atm"][:, 512:1024], sb["hy1"][:], AF.Square),
                ["hy1"], ["atm"], eng="scalar")
            vop(lambda e: e.tensor_reduce(sb["st8"][:, 3:4], sb["atm"][:],
                                          axis=AX.X, op=OP.add),
                ["atm"], ["st8"], big=False)
            vop(lambda e: e.tensor_scalar(sb["st8"][:, 4:5], sb["st8"][:, 2:3],
                                          1.0 / 1024.0, None, OP.mult),
                ["st8"], ["st8"], big=False)
            vop(lambda e: e.tensor_scalar(sb["st8"][:, 5:6], sb["st8"][:, 3:4],
                                          1.0 / 1024.0, None, OP.mult),
                ["st8"], ["st8"], big=False)
            vop(lambda e: e.tensor_tensor(sb["st8"][:, 6:7], sb["st8"][:, 4:5],
                                          sb["st8"][:, 4:5], OP.mult),
                ["st8"], ["st8"], big=False)
            vop(lambda e: e.tensor_tensor(sb["st8"][:, 6:7], sb["st8"][:, 5:6],
                                          sb["st8"][:, 6:7], OP.subtract),
                ["st8"], ["st8"], big=False)
            vop(lambda e: e.activation(sb["st8"][:, 7:8], sb["st8"][:, 6:7],
                                       AF.Sqrt, bias=sb["cb"][:, 1:2], scale=1.0),
                ["st8", "cb"], ["st8"], eng="scalar", big=False)
            vop(lambda e: e.reciprocal(sb["st8"][:, 8:9], sb["st8"][:, 7:8]),
                ["st8"], ["st8"], big=False)
            for n2 in range(2):
                vop(lambda e, n2=n2: e.tensor_scalar(
                    sb["x1t"][:, 512 * n2:512 * n2 + 512], sb[f"hy{n2}"][:],
                    sb["st8"][:, 4:5], sb["st8"][:, 8:9], OP.subtract, OP.mult),
                    [f"hy{n2}", "st8"], ["x1t"])
            dma("scalar", dram["out"][128 * tc:128 * tc + 128, :], sb["x1t"][:],
                ["x1t"], [])

    if mode == "DBGC":
        vop(lambda e: e.memset(sb["srun"][:], 0.0), [], ["srun"])
        for j in range(8):
            dma("sync", sb[f"kf{j}"][0:64, 0:1040], dram["gst"][j],
                [f"kf{j}"], [f"kf{j}"])
            dma("sync", sb[f"kf{j}"][64:128, 0:1040], dram["gst"][j],
                [f"kf{j}"], [f"kf{j}"])
        for j in range(8):
            vop(lambda e, j=j: e.scalar_tensor_tensor(
                sb["srun"][:], sb[f"kf{j}"][:, 0:1040], sb["pm"][:, j:j + 1],
                sb["srun"][:], OP.mult, OP.add),
                [f"kf{j}", "pm", "srun"], ["srun"])
        vop(lambda e: e.tensor_copy(sb["sbf"][:], sb["srun"][:]), ["srun"], ["sbf"])
        dma("sync", dram["dbgc_sbf"], sb["sbf"][:], ["sbf"], [])
        dma("sync", dram["dbgc_kf0"], sb["kf0"][:, 0:1040], ["kf0"], [])
    if has_b and mode != "DBGC":
        phase_b()
    if has_a:
        phase_a(la)
    if has_f:
        r.barrier()
        rhs_prep()
        final_block()

    sems = {}
    from contextlib import ExitStack
    stk = ExitStack()
    for e in ENGINES:
        sems[e] = stk.enter_context(nc.semaphore(f"sem_{e}"))
    block = stk.enter_context(nc.Block())
    r.emit(block, sems)
    stk.close()
    nc._rec_counts = dict(r.cnt)
    return nc


def _consts():
    idf = np.eye(128, dtype=np.float32)
    idb = np.eye(128).astype(ml_dtypes.bfloat16)
    tk, tq = np.meshgrid(np.arange(128), np.arange(128), indexing="ij")
    triu = (tq >= tk).astype(np.float32)
    e16 = np.zeros((128, 128), np.float32)
    for c in range(8):
        e16[0:64, 16 * c + 2 * c] = 1.0
        e16[64:128, 16 * c + 2 * c + 1] = 1.0
    sel = np.zeros((16, 1024), np.float32)
    for d in range(8):
        sel[2 * d, 128 * d:128 * d + 64] = 1.0
        sel[2 * d + 1, 128 * d + 64:128 * d + 128] = 1.0
    ones1f = np.ones((1, 128), np.float32)
    return dict(idf=idf, idb=idb, triu=triu,
                e16=e16.astype(ml_dtypes.bfloat16),
                sel=sel.astype(ml_dtypes.bfloat16), ones1f=ones1f)


def _fm8(v):
    return np.ascontiguousarray(np.asarray(v).reshape(8, 128).T)


def _fm32(v):
    return np.ascontiguousarray(np.asarray(v).reshape(32, 128).T)


def kernel(L0_attn_x, L0_attn_Wq, L0_attn_bq, L0_attn_Wk, L0_attn_bk,
           L0_attn_Wv, L0_attn_bv, L0_attn_Wo, L0_attn_bo, L0_attn_omega,
           L0_attn_ln1_g, L0_attn_ln1_b, L0_attn_ln2_g, L0_attn_ln2_b,
           L0_attn_W1, L0_attn_bf1, L0_attn_W2, L0_attn_bf2, L0_ffn_rhs):
    x = np.ascontiguousarray(np.asarray(L0_attn_x, dtype=np.float32))
    Wq = np.asarray(L0_attn_Wq, np.float32)
    Wk = np.asarray(L0_attn_Wk, np.float32)
    Wv = np.asarray(L0_attn_Wv, np.float32)
    Wo = np.asarray(L0_attn_Wo, np.float32)
    W1 = np.asarray(L0_attn_W1, np.float32)
    W2 = np.asarray(L0_attn_W2, np.float32)
    bq = np.asarray(L0_attn_bq, np.float32)
    bk = np.asarray(L0_attn_bk, np.float32)
    bv = np.asarray(L0_attn_bv, np.float32)
    bo = np.asarray(L0_attn_bo, np.float32)
    bf1 = np.asarray(L0_attn_bf1, np.float32)
    bf2 = np.asarray(L0_attn_bf2, np.float32)
    l1g = np.asarray(L0_attn_ln1_g, np.float32)
    l1b = np.asarray(L0_attn_ln1_b, np.float32)
    l2g = np.asarray(L0_attn_ln2_g, np.float32)
    l2b = np.asarray(L0_attn_ln2_b, np.float32)
    om = np.asarray(L0_attn_omega, np.float32)
    rhs = np.ascontiguousarray(np.asarray(L0_ffn_rhs, np.float32))

    consts = _consts()
    x0s = [np.ascontiguousarray(x[c // 4, 2048 * (c % 4):2048 * (c % 4) + 2048, :])
           for c in range(NCORE)]
    pmasks = []
    for c in range(NCORE):
        g0 = (c // 4) * 4
        m = np.zeros(8, np.float32)
        m[g0:c] = 1.0
        pmasks.append(np.broadcast_to(m, (128, 8)).copy())

    def layer_vecs(l):
        return dict(
            bq_b=_fm8(bq[l]), bo_b=_fm8(bo[l]), ln1g_b=_fm8(l1g[l]),
            ln1b_b=_fm8(l1b[l]), ln2g_b=_fm8(l2g[l]), ln2b_b=_fm8(l2b[l]),
            bf1_b=_fm32(bf1[l]), bf2_b=_fm8(bf2[l]),
            wq_b=np.ascontiguousarray(Wq[l]), wo_b=np.ascontiguousarray(Wo[l]),
            w1_b=np.ascontiguousarray(W1[l]), w2_b=np.ascontiguousarray(W2[l]),
            om_b=np.ascontiguousarray(np.vstack([om[l], om[l]])))

    def layer_a_in(l):
        return dict(
            wk_a=np.ascontiguousarray(Wk[l]), wv_a=np.ascontiguousarray(Wv[l]),
            bk_a=_fm8(bk[l]), bv_a=np.ascontiguousarray(bv[l]).reshape(1, D),
            om_a=np.ascontiguousarray(np.vstack([om[l], om[l]])))

    state = [dict() for _ in range(NCORE)]

    nc0 = build_launch("A", 0, None)
    in_maps = []
    for c in range(NCORE):
        m = dict(consts)
        m.update(layer_a_in(0))
        m["x0"] = x0s[c]
        in_maps.append(m)
    res = run_bass_kernel_spmd(nc0, in_maps, core_ids=list(range(NCORE)))
    for c in range(NCORE):
        state[c] = {k: res.results[c][k]
                    for k in ("xout", "kf_out", "vt_out", "sch_out", "stot")}

    for k in (1, 2, 3):
        lb = k - 1
        la = k if k < 3 else None
        mode = "BA" if k < 3 else "BF"
        nck = build_launch(mode, la, lb)
        gst = np.stack([state[c]["stot"] for c in range(NCORE)])
        in_maps = []
        for c in range(NCORE):
            m = dict(consts)
            m.update(layer_vecs(lb))
            m["xin"] = state[c]["xout"]
            m["kf_in"] = state[c]["kf_out"]
            m["vt_in"] = state[c]["vt_out"]
            m["sch_in"] = state[c]["sch_out"]
            m["gst"] = gst
            m["pmask"] = pmasks[c]
            if mode == "BA":
                m.update(layer_a_in(la))
            else:
                m["x0"] = x0s[c]
                m["rhs"] = rhs
            in_maps.append(m)
        res = run_bass_kernel_spmd(nck, in_maps, core_ids=list(range(NCORE)))
        if mode == "BA":
            ns = []
            for c in range(NCORE):
                ns.append({kk: res.results[c][kk]
                           for kk in ("xout", "kf_out", "vt_out", "sch_out", "stot")})
            state = ns
        else:
            outs = [res.results[c]["out"] for c in range(NCORE)]

    full = np.empty((2, 8192, D), np.float32)
    for c in range(NCORE):
        full[c // 4, 2048 * (c % 4):2048 * (c % 4) + 2048, :] = outs[c]
    return full


# version 07
# speedup vs baseline: 1.1654x; 1.0606x over previous
import numpy as np
import ml_dtypes
import concourse.bass as bass
import concourse.mybir as mybir
from concourse.bass_utils import run_bass_kernel_spmd

F32 = mybir.dt.float32
BF16 = mybir.dt.bfloat16
I32 = mybir.dt.int32
AF = mybir.ActivationFunctionType
OP = mybir.AluOpType
AX = mybir.AxisListType

NCORE = 8
T = 2048
D = 1024
H = 16
FF = 4096
NL = 3
C = 128
NCH = 16
SLAB = 512
NSLAB = 4
CQ = 0.0625
LN8 = 2.0794415416798357
CS = 0.35355339059327373
LN_EPS = 1e-5
ATTN_EPS = 1e-6
INV127 = 1.0 / 127.0

ENGINES = ("tensor", "vector", "scalar", "gpsimd", "sync", "gdma", "sdma")
QUEUE_LABELS = {"tensor": ("tensor",), "vector": ("vector",),
                "scalar": ("scalar", "sdma"), "gpsimd": ("gpsimd", "gdma"),
                "sync": ("sync",)}
DEBUG = False
SCAN_BATCHED = 4
SCAN_SER_COLS = False


class Rec:

    SPACED = ("vector", "gpsimd", "scalar")

    def __init__(self):
        self.items = []
        self.cnt = {e: 0 for e in ENGINES}
        self.opi = {e: 0 for e in ENGINES}
        self.writer = {}
        self.readers = {}

    def barrier(self):
        snap = dict(self.cnt)
        self._pend = {e: {d: v for d, v in snap.items() if d != e and v > 0}
                      for e in ENGINES}

    def op(self, eng, fn, reads=(), writes=(), dma=False, big=False):
        inc = 16 if dma else 1
        waits = {}
        pend = getattr(self, "_pend", None)
        if pend and pend.get(eng):
            waits.update(pend.pop(eng))

        def need(dep):
            deng, dcnt = dep[0], dep[1]
            dep_dma = dep[3] if len(dep) > 3 else False
            if deng == eng and not dep_dma and eng in ("tensor", "sync"):
                return
            if (eng == "gdma" and deng == "gpsimd") or (eng == "sdma" and deng == "scalar"):
                return
            if waits.get(deng, 0) < dcnt:
                waits[deng] = dcnt

        for t in reads:
            w = self.writer.get(t)
            if w:
                need(w)
        for t in writes:
            w = self.writer.get(t)
            if w:
                need(w)
            for rd in self.readers.get(t, ()):
                need(rd)
        self.cnt[eng] += inc
        self.opi[eng] += 1
        here = (eng, self.cnt[eng], self.opi[eng], dma, big or dma)
        for t in reads:
            self.readers.setdefault(t, []).append(here)
        for t in writes:
            self.writer[t] = here
            self.readers[t] = []
        self.items.append((eng, fn, waits, inc))

    def emit(self, block, sems):
        finals = dict(self.cnt)
        fence = {e: set() for e in ENGINES}
        for eng, fn, waits, inc in self.items:
            for d, v in waits.items():
                fence[d].add(v)

        def mk(qname, items):
            cnt = {l: 0 for l in QUEUE_LABELS[qname]}
            selfhw = {l: 0 for l in QUEUE_LABELS[qname]}
            fi = {l: 0 for l in QUEUE_LABELS[qname]}
            fv = {l: sorted(fence[l]) for l in QUEUE_LABELS[qname]}
            dma_lbl = [l for l in QUEUE_LABELS[qname] if l in ("sync", "gdma", "sdma")]
            hwq = {d: 0 for d in ENGINES}

            def body(eng):
                for lbl, fn, waits, inc in items:
                    if lbl in dma_lbl or inc == 16:
                        fmax = 0
                        while fi[lbl] < len(fv[lbl]) and fv[lbl][fi[lbl]] <= cnt[lbl]:
                            fmax = fv[lbl][fi[lbl]]
                            fi[lbl] += 1
                        if fmax > selfhw[lbl]:
                            eng.wait_ge(sems[lbl], fmax)
                            selfhw[lbl] = fmax
                    for d, v in sorted(waits.items()):
                        if hwq[d] < v:
                            eng.wait_ge(sems[d], v)
                            hwq[d] = v
                    fn(eng).then_inc(sems[lbl], inc)
                    cnt[lbl] += inc
                for d in ENGINES:
                    if d not in QUEUE_LABELS[qname] and finals[d] > 0 and hwq[d] < finals[d]:
                        eng.wait_ge(sems[d], finals[d])
                for l in QUEUE_LABELS[qname]:
                    if finals[l] > cnt[l] or (finals[l] > selfhw[l] and l in dma_lbl):
                        eng.wait_ge(sems[l], finals[l])
            return body

        for qname in QUEUE_LABELS:
            qitems = [(l, fn, w, inc) for (l, fn, w, inc) in
                      [(e, f, w, i) for (e, f, w, i) in self.items]
                      if l in QUEUE_LABELS[qname]]
            getattr(block, qname)(mk(qname, qitems))


SBUF_BASE = 16512


class Alloc:
    def __init__(self, nc):
        self.nc = nc
        self.off = SBUF_BASE
        self.offs = {}

    def at(self, name, shape, dtype, offset):
        return self.nc.alloc_sbuf_tensor_at(name, shape, dtype, offset=offset)

    def buf(self, name, shape, dtype):
        per = int(np.prod(shape[1:])) * mybir.dt.size(dtype)
        per = (per + 31) // 32 * 32
        t = self.nc.alloc_sbuf_tensor_at(name, shape, dtype, offset=self.off)
        self.offs[name] = self.off
        self.off += per
        return t

    def mark(self):
        return self.off


def ceil32(x):
    return (x + 31) // 32 * 32


def build_launch(mode, la, lb):
    nc = bass.Bass("TRN2", target_bir_lowering=False, debug=False)
    r = Rec()

    dram = {}

    def din(name, shape, dtype):
        dram[name] = nc.dram_tensor(name, shape, dtype, kind="ExternalInput").ap()
        return dram[name]

    def dout(name, shape, dtype):
        dram[name] = nc.dram_tensor(name, shape, dtype, kind="ExternalOutput").ap()
        return dram[name]

    din("idf", [128, 128], F32)
    din("idb", [128, 128], BF16)
    din("triu", [128, 128], F32)
    din("e16", [128, 128], BF16)
    din("sel", [16, 1024], BF16)
    din("ones1f", [1, 128], F32)

    has_a = mode in ("A", "BA")
    has_b = mode in ("BA", "BF", "DBGC")
    has_f = mode == "BF"
    if mode == "DBGC":
        has_a = False

    if mode == "A":
        din("x0", [T, D], F32)
    else:
        din("xin", [8, NSLAB, 128, SLAB], F32)
    if mode == "DBGC":
        dout("dbgc_sbf", [128, 1040], BF16)
        dout("dbgc_kf0", [128, 1040], BF16)
    if has_a:
        din("wk_a", [D, D], F32)
        din("wv_a", [D, D], F32)
        din("bk_a", [128, 8], F32)
        din("bv_a", [1, D], F32)
        din("om_a", [128, 32], F32)
        dout("xout", [8, NSLAB, 128, SLAB], F32)
        dout("kf_out", [8, 128, T], BF16)
        dout("vt_out", [NSLAB, 128, 4160], BF16)
        dout("sch_out", [NCH, 64, 1040], BF16)
        dout("stot", [64, 1040], BF16)
    if has_b:
        din("kf_in", [8, 128, T], BF16)
        din("vt_in", [NSLAB, 128, 4160], BF16)
        din("sch_in", [NCH, 64, 1040], BF16)
        din("gst", [8, 64, 1040], BF16)
        din("pmask", [128, 8], F32)
        din("wq_b", [D, D], F32)
        din("wo_b", [D, D], F32)
        din("w1_b", [D, FF], F32)
        din("w2_b", [FF, D], F32)
        din("bq_b", [128, 8], F32)
        din("bo_b", [128, 8], F32)
        din("ln1g_b", [128, 8], F32)
        din("ln1b_b", [128, 8], F32)
        din("ln2g_b", [128, 8], F32)
        din("ln2b_b", [128, 8], F32)
        din("bf1_b", [128, 32], F32)
        din("bf2_b", [128, 8], F32)
        din("om_b", [128, 32], F32)
        dram["rmid"] = nc.dram_tensor("rmid", [NSLAB, 8, 128, SLAB], F32).ap()
        dram["xmid"] = nc.dram_tensor("xmid", [NSLAB, 8, 128, SLAB], F32).ap()
    if has_f:
        din("x0", [T, D], F32)
        din("rhs", [D, D], F32)
        dout("out", [T, D], F32)
    if DEBUG and has_b:
        dout("dbg_qf", [8, 128, SLAB], BF16)
        dout("dbg_sbf", [128, 1040], BF16)
        dout("dbg_atm", [128, 1024], F32)
        dout("dbg_hy", [8, 128, SLAB], F32)
        dout("dbg_num", [2, 128, 65], F32)
        dout("dbg_a2m", [2, 128, 128], BF16)

    al = Alloc(nc)
    sb = {}

    def B(name, shape, dtype):
        sb[name] = al.buf(name, shape, dtype)
        return sb[name]

    for i in range(8):
        B(f"xsb{i}", [128, SLAB], F32)
    for i in range(8):
        B(f"xbf{i}", [128, SLAB], BF16)
    B("Wa", [128, 8192], BF16)
    B("Wb", [128, 8192], BF16)
    kf_off = al.mark()
    for i in range(8):
        B(f"kf{i}", [128, T], BF16)
    vt_off = al.mark()
    for i in range(NSLAB):
        B(f"vt{i}", [128, 4160], BF16)
    for i in range(8):
        B(f"qs{i}", [128, SLAB], BF16)
    for i in range(8):
        B(f"qf{i}", [128, SLAB], BF16)
    B("srun", [128, 1040], F32)
    B("sbf", [128, 1040], BF16)
    B("schb0", [128, 1040], BF16)
    B("schb1", [128, 1040], BF16)
    B("atm", [128, 1024], F32)
    B("etm", [128, 1024], F32)
    B("escr", [128, SLAB], F32)
    B("sq0", [128, SLAB], BF16)
    B("sq1", [128, SLAB], BF16)
    B("t1", [128, SLAB], F32)
    for i in range(8):
        B(f"hy{i}", [128, SLAB], F32)
    B("a2m", [128, 2048], BF16)
    B("cb", [128, 4], F32)
    r._junk = B("junk", [128, 64], F32)
    B("onesc", [128, 1], F32)
    B("oq", [128, 128], BF16)
    B("omsb", [128, 32], F32)
    B("dcol", [128, 32], F32)
    B("st8", [128, 12], F32)
    B("rowb", [1, SLAB], F32)
    B("rowb2", [1, SLAB], F32)
    if has_a:
        B("bvbc", [128, 1024], F32)
        B("stotsb", [128, 1040], F32)
        B("bvrow", [1, 1024], F32)
    B("bsum", [128, 8], F32)
    B("idf", [128, 128], F32)
    B("idb", [128, 128], BF16)
    B("triu", [128, 128], F32)
    B("e16", [128, 128], BF16)
    B("sel", [16, 1024], BF16)
    B("ones1f", [1, 128], F32)
    B("pm", [128, 8], F32)
    B("Ftile", [16, SLAB], F32)
    B("Fbf", [16, SLAB], BF16)
    for nm in ("bq", "bo", "l1g", "l1b", "l2g", "l2b", "bf2", "bk"):
        B(nm, [128, 8], F32)
    B("bf1", [128, 32], F32)
    if has_f:
        for i in range(8):
            sb[f"qrhs{i}"] = al.at(f"qrhs{i}", [128, 1024], BF16,
                                   offset=al.offs[f"xsb{i}"])
        sb["scolbc"] = al.at("scolbc", [128, 1024], F32, offset=al.offs["xbf0"])
        sb["rcolbc"] = al.at("rcolbc", [128, 1024], F32, offset=al.offs["xbf4"])
        B("scall", [128, 8], F32)
        B("rcall", [128, 8], F32)
        for i in range(8):
            sb[f"rhsf{i}"] = al.at(f"rhsf{i}", [128, 1024], F32,
                                   offset=kf_off + i * ceil32(T * 2))
        sb["x1t"] = al.at("x1t", [128, 1024], F32, offset=al.offs["qf0"])
        sb["qi"] = al.at("qi", [128, 1024], I32, offset=al.offs["qf4"])
        sb["qlf"] = al.at("qlf", [128, 1024], BF16, offset=al.offs["escr"])
        sb["x0t"] = al.at("x0t", [128, 1024], F32, offset=al.offs["srun"])

    assert al.off <= SBUF_BASE + 208 * 1024, f"SBUF over budget: {al.off}"

    ps = [nc.alloc_psum_tensor(f"ps{i}", [128, 512], F32) for i in range(8)]

    def dma(q, out_ap, in_ap, reads, writes):
        lbl = {"gpsimd": "gdma", "scalar": "sdma"}.get(q, q)
        r.op(lbl, lambda e, o=out_ap, i=in_ap: e.dma_start(out=o, in_=i),
             reads=reads, writes=writes, dma=True)

    def mm(out_ap, lh, rh, st, sp, reads, writes, tile_position=None):
        r.op("tensor",
             lambda e, o=out_ap, a=lh, b=rh, s=st, p=sp, tp=tile_position:
             e.matmul(o, a, b, start=s, stop=p, tile_position=tp),
             reads=reads, writes=writes)

    def vop(fn, reads, writes, eng="vector", big=True):
        r.op(eng, fn, reads=reads, writes=writes, big=big)

    for nm, key in (("idf", "idf"), ("idb", "idb"), ("triu", "triu"),
                    ("e16", "e16"), ("sel", "sel"), ("ones1f", "ones1f")):
        dma("sync", sb[nm][:], dram[key], [], [nm])
    if has_b:
        dma("sync", sb["pm"][:], dram["pmask"], [], ["pm"])
    vop(lambda e: e.memset(sb["onesc"][:], 1.0), [], ["onesc"])
    vop(lambda e: e.memset(sb["cb"][:, 0:1], -LN8), [], ["cb"])
    vop(lambda e: e.memset(sb["cb"][:, 1:2], LN_EPS), [], ["cb"])
    vop(lambda e: e.memset(sb["cb"][:, 2:3], ATTN_EPS), [], ["cb"])

    def load_layer_vecs(which):
        if which == "B":
            for nm, key in (("bq", "bq_b"), ("bo", "bo_b"), ("l1g", "ln1g_b"),
                            ("l1b", "ln1b_b"), ("l2g", "ln2g_b"), ("l2b", "ln2b_b"),
                            ("bf2", "bf2_b")):
                dma("sync", sb[nm][:], dram[key], [], [nm])
            dma("sync", sb["bf1"][:], dram["bf1_b"], [], ["bf1"])
            vop(lambda e: e.tensor_tensor(sb["bsum"][:], sb["l1b"][:], sb["bf2"][:], OP.add),
                ["l1b", "bf2"], ["bsum"])
        else:
            dma("sync", sb["bk"][:], dram["bk_a"], [], ["bk"])
            dma("sync", sb["bvrow"][:], dram["bv_a"], [], ["bvrow"])

    def build_oq(omkey):
        dma("sync", sb["omsb"][:], dram[omkey], [], ["om"])
        vop(lambda e: e.memset(sb["oq"][:], 0.0), [], ["oq"])
        for (r0, c0, sc) in ((0, 0, CS), (0, 32, -CS), (64, 64, CS), (64, 96, -CS)):
            vop(lambda e, r0=r0, c0=c0, sc=sc:
                e.tensor_scalar(sb["oq"][r0:r0 + 64, c0:c0 + 32],
                                sb["omsb"][r0:r0 + 64, 0:32], sc, None, OP.mult),
                ["om", "oq"], ["oq"], big=False)

    def build_bvbc():
        for n2 in range(2):
            mm(ps[2][0:128, 0:512], sb["ones1f"][0:1, 0:128],
               sb["bvrow"][0:1, 512 * n2:512 * n2 + 512], True, True,
               ["ones1f", "bvrow", "ps2"], ["ps2"])
            vop(lambda e, n2=n2: e.tensor_copy(sb["bvbc"][:, 512 * n2:512 * n2 + 512],
                                               ps[2][0:128, 0:512]),
                ["ps2"], ["bvbc"])

    def load_w(dst, key, ncols=D):
        for kc in range(8):
            dma("gpsimd", sb[dst][:, ncols * kc:ncols * kc + ncols],
                dram[key][128 * kc:128 * kc + 128, :], [dst], [dst])

    def load_w_slice(dst, key, r0, r1c, c0, c1):
        n = c1 - c0
        for kc in range((r1c - r0) // 128):
            dma("gpsimd", sb[dst][:, n * kc:n * kc + n],
                dram[key][r0 + 128 * kc:r0 + 128 * kc + 128, c0:c1], [dst], [dst])

    def favor_fm(s, wkey, biaskey, dst_pref, store_whole):
        col = 512 * s if store_whole else 0
        for mc in range(8):
            pb = ps[mc % 2]
            for kc in range(8):
                mm(pb[0:128, 0:512],
                   sb[wkey][:, 1024 * kc + 128 * mc:1024 * kc + 128 * mc + 128],
                   sb[f"xbf{kc}"][:, 0:512], kc == 0, kc == 7,
                   [wkey, f"xbf{kc}", f"ps{mc % 2}"], [f"ps{mc % 2}"])
            vop(lambda e, pb=pb, mc=mc: e.activation(
                sb[f"qs{mc}"][:, 0:512], pb[0:128, 0:512], AF.Identity,
                bias=sb[biaskey][:, mc:mc + 1], scale=1.0),
                [f"ps{mc % 2}", biaskey], [f"qs{mc}"], eng="scalar")
        for mc in range(8):
            sqb = f"sq{mc % 2}"
            vop(lambda e, mc=mc, sqb=sqb: e.activation(
                sb[sqb][:, 0:512], sb[f"qs{mc}"][:, 0:512], AF.Square),
                [f"qs{mc}"], [sqb], eng="scalar")
            mm(ps[3][0:16, 0:512], sb["e16"][:, 16 * mc:16 * mc + 16],
               sb[sqb][:, 0:512], mc == 0, mc == 7,
               ["e16", sqb, "ps3"], ["ps3"])
        vop(lambda e: e.activation(sb["Ftile"][:, 0:512], ps[3][0:16, 0:512],
                                   AF.Exp, bias=sb["cb"][0:16, 0:1], scale=-CQ),
            ["ps3", "cb"], ["Ftile"], eng="scalar")
        vop(lambda e: e.tensor_copy(sb["Fbf"][:, 0:512], sb["Ftile"][:, 0:512]),
            ["Ftile"], ["Fbf"])
        for d in range(8):
            mm(ps[4][0:128, 0:512], sb["oq"][:, 0:128], sb[f"qs{d}"][:, 0:512],
               True, True, ["oq", f"qs{d}", "ps4"], ["ps4"])
            mm(ps[5][0:128, 0:512], sb["sel"][0:16, 128 * d:128 * d + 128],
               sb["Fbf"][0:16, 0:512], True, True,
               ["sel", "Fbf", "ps5"], ["ps5"])
            vop(lambda e: e.activation(sb["escr"][:, 0:512], ps[4][0:128, 0:512], AF.Exp),
                ["ps4"], ["escr"], eng="scalar")
            tgt = f"{dst_pref}{d}"
            vop(lambda e, tgt=tgt, col=col: e.tensor_tensor(
                sb[tgt][:, col:col + 512], sb["escr"][:, 0:512],
                ps[5][0:128, 0:512], OP.mult),
                ["escr", "ps5"], [tgt])

    def phase_a(layer_first):
        build_oq("om_a")
        load_layer_vecs("A")
        build_bvbc()
        load_w("Wa", "wk_a")
        load_w("Wb", "wv_a")
        for s in range(NSLAB):
            for tcl in range(4):
                vop(lambda e, s=s, tcl=tcl: e.memset(
                    sb[f"vt{s}"][:, 1040 * tcl:1040 * tcl + 1040]
                    .rearrange("p (h j) -> p h j", j=65)[:, :, 64:65], 1.0),
                    [f"vt{s}"], [f"vt{s}"])
        vop(lambda e: e.memset(sb["stotsb"][:], 0.0), [], ["stotsb"])
        for s in range(NSLAB):
            if mode == "A":
                for tcl in range(4):
                    c = 4 * s + tcl
                    dma("sync", sb["atm"][:], dram["x0"][128 * c:128 * c + 128, :],
                        ["atm"], ["atm"])
                    for kc in range(8):
                        r.op("tensor", lambda e, kc=kc: e.transpose(
                            ps[6][0:128, 0:128],
                            sb["atm"][:, 128 * kc:128 * kc + 128], sb["idf"][:]),
                            reads=["atm", "idf", "ps6"], writes=["ps6"])
                        vop(lambda e, kc=kc, tcl=tcl: e.tensor_copy(
                            sb[f"xsb{kc}"][:, 128 * tcl:128 * tcl + 128],
                            ps[6][0:128, 0:128]),
                            ["ps6"], [f"xsb{kc}"])
                for kc in range(8):
                    dma("scalar", dram["xout"][kc, s], sb[f"xsb{kc}"][:],
                        [f"xsb{kc}"], [])
            else:
                for kc in range(8):
                    dma("sync", sb[f"xsb{kc}"][:], dram["xmid"][s, kc],
                        [f"xsb{kc}", "xmid"], [f"xsb{kc}"])
            for kc in range(8):
                vop(lambda e, kc=kc: e.tensor_copy(sb[f"xbf{kc}"][:],
                                                   sb[f"xsb{kc}"][:]),
                    [f"xsb{kc}"], [f"xbf{kc}"], eng="gpsimd")
            for tcl in range(4):
                for n2 in range(2):
                    pb = ps[n2]
                    for kc in range(8):
                        mm(pb[0:128, 0:512],
                           sb[f"xbf{kc}"][:, 128 * tcl:128 * tcl + 128],
                           sb["Wb"][:, 1024 * kc + 512 * n2:1024 * kc + 512 * n2 + 512],
                           kc == 0, kc == 7,
                           [f"xbf{kc}", "Wb", f"ps{n2}"], [f"ps{n2}"])
                    vop(lambda e, s=s, tcl=tcl, n2=n2, pb=pb: e.tensor_tensor(
                        sb[f"vt{s}"][:, 1040 * tcl + 520 * n2:1040 * tcl + 520 * n2 + 520]
                        .rearrange("p (h j) -> p h j", j=65)[:, :, 0:64],
                        pb[0:128, 0:512].rearrange("p (h j) -> p h j", j=64),
                        sb["bvbc"][:, 512 * n2:512 * n2 + 512]
                        .rearrange("p (h j) -> p h j", j=64), OP.add),
                        [f"ps{n2}", "bvbc", f"vt{s}"], [f"vt{s}"])
            favor_fm(s, "Wa", "bk", "kf", True)
            for tcl in range(4):
                c = 4 * s + tcl
                for d in range(8):
                    pb = ps[2 + d // 4]
                    mm(pb[0:128, 128 * (d % 4):128 * (d % 4) + 128],
                       sb[f"qs{d}"][:, 128 * tcl:128 * tcl + 128],
                       sb["oq"][:, 0:128], True, True,
                       [f"qs{d}", "oq", f"ps{2 + d // 4}"], [f"ps{2 + d // 4}"])
                r.op("tensor", lambda e, tcl=tcl: e.transpose(
                    ps[7][0:128, 0:16],
                    sb["Ftile"][0:16, 128 * tcl:128 * tcl + 128],
                    sb["idf"][0:16, 0:16]),
                    reads=["Ftile", "idf", "ps7"], writes=["ps7"])
                for half in range(2):
                    vop(lambda e, half=half: e.activation(
                        sb["etm"][:, 512 * half:512 * half + 512],
                        ps[2 + half][0:128, 0:512], AF.Exp),
                        [f"ps{2 + half}"], ["etm"], eng="scalar")
                vop(lambda e: e.tensor_copy(sb["atm"][:, 0:16], ps[7][0:128, 0:16]),
                    ["ps7"], ["atm"], big=False)
                vop(lambda e: e.tensor_tensor(
                    sb["schb1"][:, 0:1024].rearrange("p (h j) -> p h j", j=64),
                    sb["etm"][:, 0:1024].rearrange("p (h j) -> p h j", j=64),
                    sb["atm"][:, 0:16].rearrange("p (h j) -> p h j", j=1)
                    .broadcast_to((128, 16, 64)), OP.mult),
                    ["etm", "atm"], ["schb1"])
                for h in range(16):
                    bk, col = ("ps4", 65 * h) if h < 7 else (
                        ("ps5", 65 * (h - 7)) if h < 14 else ("ps6", 65 * (h - 14)))
                    mm(ps[int(bk[2])][0:64, col:col + 65],
                       sb["schb1"][:, 64 * h:64 * h + 64],
                       sb[f"vt{s}"][:, 1040 * tcl + 65 * h:1040 * tcl + 65 * h + 65],
                       True, True, ["schb1", f"vt{s}", bk], [bk])
                for (bk, c0, w) in (("ps4", 0, 455), ("ps5", 455, 455), ("ps6", 910, 130)):
                    vop(lambda e, bk=bk, c0=c0, w=w: e.tensor_copy(
                        sb["schb0"][0:64, c0:c0 + w], ps[int(bk[2])][0:64, 0:w]),
                        [bk], ["schb0"])
                vop(lambda e: e.tensor_tensor(sb["stotsb"][0:64, :],
                                              sb["stotsb"][0:64, :],
                                              sb["schb0"][0:64, :], OP.add),
                    ["schb0", "stotsb"], ["stotsb"])
                dma("scalar", dram["sch_out"][c], sb["schb0"][0:64, :], ["schb0"], [])
        for d in range(8):
            dma("scalar", dram["kf_out"][d], sb[f"kf{d}"][:], [f"kf{d}"], [])
        for s in range(NSLAB):
            dma("scalar", dram["vt_out"][s], sb[f"vt{s}"][:], [f"vt{s}"], [])
        vop(lambda e: e.tensor_copy(sb["schb0"][0:64, :], sb["stotsb"][0:64, :]),
            ["stotsb"], ["schb0"])
        dma("scalar", dram["stot"], sb["schb0"][0:64, :], ["schb0"], [])

    def phase_b():
        build_oq("om_b")
        load_layer_vecs("B")
        vop(lambda e: e.memset(sb["srun"][:], 0.0), [], ["srun"])
        for j in range(8):
            dma("sync", sb[f"kf{j}"][0:64, 0:1040], dram["gst"][j],
                [f"kf{j}"], [f"kf{j}"])
            dma("sync", sb[f"kf{j}"][64:128, 0:1040], dram["gst"][j],
                [f"kf{j}"], [f"kf{j}"])
        for j in range(8):
            vop(lambda e, j=j: e.scalar_tensor_tensor(
                sb["srun"][:], sb[f"kf{j}"][:, 0:1040], sb["pm"][:, j:j + 1],
                sb["srun"][:], OP.mult, OP.add),
                [f"kf{j}", "pm", "srun"], ["srun"])
        vop(lambda e: e.tensor_copy(sb["sbf"][:], sb["srun"][:]), ["srun"], ["sbf"])
        load_w("Wa", "wq_b")
        load_w("Wb", "wo_b")
        for d in range(8):
            dma("sync", sb[f"kf{d}"][:], dram["kf_in"][d], [f"kf{d}"], [f"kf{d}"])
        for s in range(NSLAB):
            dma("sync", sb[f"vt{s}"][:], dram["vt_in"][s], [f"vt{s}"], [f"vt{s}"])
        if DEBUG:
            dma("sync", dram["dbg_sbf"], sb["sbf"][:], ["sbf"], [])

        for s in range(NSLAB):
            for kc in range(8):
                dma("sync", sb[f"xsb{kc}"][:], dram["xin"][kc, s],
                    [f"xsb{kc}"], [f"xsb{kc}"])
                vop(lambda e, kc=kc: e.tensor_copy(sb[f"xbf{kc}"][:],
                                                   sb[f"xsb{kc}"][:]),
                    [f"xsb{kc}"], [f"xbf{kc}"], eng="gpsimd")
            favor_fm(s, "Wa", "bq", "qf", False)
            if DEBUG and s == 0:
                for dd in range(8):
                    dma("sync", dram["dbg_qf"][dd], sb[f"qf{dd}"][:], [f"qf{dd}"], [])
            for tcl in range(4):
                c = 4 * s + tcl
                if c > 0:
                    buf = f"schb{c % 2}"
                    dma("sync", sb[buf][0:64, :], dram["sch_in"][c - 1], [buf], [buf])
                    dma("sync", sb[buf][64:128, :], dram["sch_in"][c - 1], [buf], [buf])
                    vop(lambda e, buf=buf: e.tensor_tensor(
                        sb["srun"][:], sb["srun"][:], sb[buf][:], OP.add),
                        [buf, "srun"], ["srun"])
                    vop(lambda e: e.tensor_copy(sb["sbf"][:], sb["srun"][:]),
                        ["srun"], ["sbf"])
                if not SCAN_BATCHED:
                    A2COL = 128 if SCAN_SER_COLS else 0
                    for h in range(16):
                        d, base = h // 2, 64 * (h % 2)
                        acol = A2COL * h if SCAN_SER_COLS else 0
                        mm(ps[3][0:128, 0:128],
                           sb[f"kf{d}"][base:base + 64, 128 * c:128 * c + 128],
                           sb[f"qf{d}"][base:base + 64, 128 * tcl:128 * tcl + 128],
                           True, True, [f"kf{d}", f"qf{d}", "ps3"], ["ps3"])
                        vop(lambda e, acol=acol: e.tensor_tensor(
                            sb["a2m"][:, acol:acol + 128],
                            ps[3][0:128, 0:128],
                            sb["triu"][:], OP.mult),
                            ["ps3", "triu"], ["am0"])
                        mm(ps[4][0:128, 0:65], sb["a2m"][:, acol:acol + 128],
                           sb[f"vt{s}"][:, 1040 * tcl + 65 * h:1040 * tcl + 65 * h + 65],
                           True, False, ["am0", f"vt{s}"], ["nm0", "ps4"])
                        mm(ps[4][0:128, 0:65],
                           sb[f"qf{d}"][base:base + 64, 128 * tcl:128 * tcl + 128],
                           sb["sbf"][base:base + 64, 65 * h:65 * h + 65],
                           False, True, [f"qf{d}", "sbf", "nm0"], ["nm0", "ps4"])
                        vop(lambda e: e.activation(sb["dcol"][:, 0:1],
                                                   ps[4][0:128, 64:65], AF.Identity,
                                                   bias=sb["cb"][:, 2:3], scale=1.0),
                            ["nm0", "cb", "ps4"], ["dc0"], eng="scalar", big=False)
                        vop(lambda e: e.reciprocal(sb["dcol"][:, 1:2],
                                                   sb["dcol"][:, 0:1]),
                            ["dc0"], ["dr0"], big=False)
                        vop(lambda e, h=h: e.activation(
                            sb["atm"][:, 64 * h:64 * h + 64], ps[4][0:128, 0:64],
                            AF.Identity, bias=0.0, scale=sb["dcol"][:, 1:2]),
                            ["nm0", "dr0", "ps4"], ["atm"], eng="scalar", big=False)
                if SCAN_BATCHED == 4:
                    banks4 = ["ps2", "ps3", "ps5", "ps6"]
                    for h in range(16):
                        d, base = h // 2, 64 * (h % 2)
                        bk = banks4[h % 4]
                        pb = ps[int(bk[2])]
                        mm(pb[0:128, 0:128],
                           sb[f"kf{d}"][base:base + 64, 128 * c:128 * c + 128],
                           sb[f"qf{d}"][base:base + 64, 128 * tcl:128 * tcl + 128],
                           True, True, [f"kf{d}", f"qf{d}", bk], [bk])
                        vop(lambda e, h=h, pb=pb: e.tensor_tensor(
                            sb["a2m"][:, 128 * h:128 * h + 128],
                            pb[0:128, 0:128], sb["triu"][:], OP.mult),
                            [bk, "triu"], [f"am{h}"])
                    for h in range(16):
                        d, base = h // 2, 64 * (h % 2)
                        nb = "ps4" if h % 2 == 0 else "ps7"
                        nap = ps[int(nb[2])][0:128, 0:65]
                        mm(nap, sb["a2m"][:, 128 * h:128 * h + 128],
                           sb[f"vt{s}"][:, 1040 * tcl + 65 * h:1040 * tcl + 65 * h + 65],
                           True, False, [f"am{h}", f"vt{s}"], [f"nm{h}", nb])
                        mm(nap,
                           sb[f"qf{d}"][base:base + 64, 128 * tcl:128 * tcl + 128],
                           sb["sbf"][base:base + 64, 65 * h:65 * h + 65],
                           False, True, [f"qf{d}", "sbf", f"nm{h}"], [f"nm{h}", nb])
                        vop(lambda e, h=h, nb=nb: e.activation(
                            sb["dcol"][:, 2 * h:2 * h + 1],
                            ps[int(nb[2])][0:128, 64:65], AF.Identity,
                            bias=sb["cb"][:, 2:3], scale=1.0),
                            [f"nm{h}", "cb", nb], [f"dc{h}"], eng="scalar", big=False)
                        vop(lambda e, h=h: e.reciprocal(
                            sb["dcol"][:, 2 * h + 1:2 * h + 2],
                            sb["dcol"][:, 2 * h:2 * h + 1]),
                            [f"dc{h}"], [f"dr{h}"], big=False)
                        vop(lambda e, h=h, nb=nb: e.activation(
                            sb["atm"][:, 64 * h:64 * h + 64],
                            ps[int(nb[2])][0:128, 0:64],
                            AF.Identity, bias=0.0,
                            scale=sb["dcol"][:, 2 * h + 1:2 * h + 2]),
                            [f"nm{h}", f"dr{h}", nb], ["atm"], eng="scalar", big=False)
                A2B = {0: "ps2", 1: "ps3", 2: "ps5", 3: "ps6"}
                if SCAN_BATCHED != 2:
                    A2B = None
                NB = {}
                for h in range(16):
                    if h < 7:
                        NB[h] = ("ps4", 66 * h)
                    elif h < 14:
                        NB[h] = ("ps7", 66 * (h - 7))
                    else:
                        NB[h] = ("ps2", 66 * (h - 14))
                a2ap = {}
                for h in (range(16) if SCAN_BATCHED == 2 else []):
                    d, base = h // 2, 64 * (h % 2)
                    bk = A2B[h // 4]
                    a2ap[h] = ps[int(bk[2])][0:128, 128 * (h % 4):128 * (h % 4) + 128]
                    mm(a2ap[h],
                       sb[f"kf{d}"][base:base + 64, 128 * c:128 * c + 128],
                       sb[f"qf{d}"][base:base + 64, 128 * tcl:128 * tcl + 128],
                       True, True, [f"kf{d}", f"qf{d}", bk], [bk])
                for h in (range(16) if SCAN_BATCHED == 2 else []):
                    bk = A2B[h // 4]
                    vop(lambda e, h=h, ap=a2ap[h]: e.tensor_tensor(
                        sb["a2m"][:, 128 * h:128 * h + 128], ap,
                        sb["triu"][:], OP.mult),
                        [bk, "triu"], [f"am{h}"])
                if SCAN_BATCHED == 1:
                    for h in range(16):
                        d, base = h // 2, 64 * (h % 2)
                        mm(ps[4][0:128, 0:65], sb["a2m"][:, 128 * h:128 * h + 128],
                           sb[f"vt{s}"][:, 1040 * tcl + 65 * h:1040 * tcl + 65 * h + 65],
                           True, False, [f"am{h}", f"vt{s}"], ["nm0", "ps4"])
                        mm(ps[4][0:128, 0:65],
                           sb[f"qf{d}"][base:base + 64, 128 * tcl:128 * tcl + 128],
                           sb["sbf"][base:base + 64, 65 * h:65 * h + 65],
                           False, True, [f"qf{d}", "sbf", "nm0"], ["nm0", "ps4"])
                        vop(lambda e: e.activation(sb["dcol"][:, 0:1],
                                                   ps[4][0:128, 64:65], AF.Identity,
                                                   bias=sb["cb"][:, 2:3], scale=1.0),
                            ["nm0", "cb", "ps4"], ["dc0"], eng="scalar", big=False)
                        vop(lambda e: e.reciprocal(sb["dcol"][:, 1:2],
                                                   sb["dcol"][:, 0:1]),
                            ["dc0"], ["dr0"], big=False)
                        vop(lambda e, h=h: e.activation(
                            sb["atm"][:, 64 * h:64 * h + 64], ps[4][0:128, 0:64],
                            AF.Identity, bias=0.0, scale=sb["dcol"][:, 1:2]),
                            ["nm0", "dr0", "ps4"], ["atm"], eng="scalar", big=False)
                for h in (range(16) if SCAN_BATCHED == 2 else []):
                    d, base = h // 2, 64 * (h % 2)
                    nb, ncol = NB[h]
                    nap = ps[int(nb[2])][0:128, ncol:ncol + 65]
                    mm(nap, sb["a2m"][:, 128 * h:128 * h + 128],
                       sb[f"vt{s}"][:, 1040 * tcl + 65 * h:1040 * tcl + 65 * h + 65],
                       True, False, [f"am{h}", f"vt{s}"], [f"nm{h}", nb])
                    mm(nap,
                       sb[f"qf{d}"][base:base + 64, 128 * tcl:128 * tcl + 128],
                       sb["sbf"][base:base + 64, 65 * h:65 * h + 65],
                       False, True, [f"qf{d}", "sbf", f"nm{h}"], [f"nm{h}", nb])
                for h in (range(16) if SCAN_BATCHED == 2 else []):
                    nb, ncol = NB[h]
                    vop(lambda e, h=h, nb=nb, ncol=ncol: e.activation(
                        sb["dcol"][:, 2 * h:2 * h + 1],
                        ps[int(nb[2])][0:128, ncol + 64:ncol + 65], AF.Identity,
                        bias=sb["cb"][:, 2:3], scale=1.0),
                        [f"nm{h}", "cb", nb], [f"dc{h}"], eng="scalar", big=False)
                    vop(lambda e, h=h: e.reciprocal(
                        sb["dcol"][:, 2 * h + 1:2 * h + 2],
                        sb["dcol"][:, 2 * h:2 * h + 1]),
                        [f"dc{h}"], [f"dr{h}"], big=False)
                for h in (range(16) if SCAN_BATCHED == 2 else []):
                    nb, ncol = NB[h]
                    vop(lambda e, h=h, nb=nb, ncol=ncol: e.activation(
                        sb["atm"][:, 64 * h:64 * h + 64],
                        ps[int(nb[2])][0:128, ncol:ncol + 64],
                        AF.Identity, bias=0.0,
                        scale=sb["dcol"][:, 2 * h + 1:2 * h + 2]),
                        [f"nm{h}", f"dr{h}", nb], ["atm"], eng="scalar", big=False)
                if DEBUG and s == 0 and tcl == 0:
                    dma("sync", dram["dbg_atm"], sb["atm"][:], ["atm"], [])
                for j in range(8):
                    bk = "ps3" if j % 2 == 0 else "ps5"
                    pb = ps[int(bk[2])]
                    r.op("tensor", lambda e, j=j, pb=pb: e.transpose(
                        pb[0:128, 0:128], sb["atm"][:, 128 * j:128 * j + 128],
                        sb["idf"][:]),
                        reads=["atm", "idf", bk] + [f"am{h}" for h in range(16)] if j == 0 else ["atm", "idf", bk],
                        writes=[bk])
                    vop(lambda e, j=j, tcl=tcl, pb=pb: e.tensor_copy(
                        sb[f"qs{j}"][:, 128 * tcl:128 * tcl + 128],
                        pb[0:128, 0:128]),
                        [bk], [f"qs{j}"])
            for mc in range(8):
                pb = ps[mc % 2]
                for kc in range(8):
                    mm(pb[0:128, 0:512],
                       sb["Wb"][:, 1024 * kc + 128 * mc:1024 * kc + 128 * mc + 128],
                       sb[f"qs{kc}"][:, 0:512], kc == 0, kc == 7,
                       ["Wb", f"qs{kc}", f"ps{mc % 2}"], [f"ps{mc % 2}"])
                vop(lambda e, pb=pb, mc=mc: e.scalar_tensor_tensor(
                    sb[f"hy{mc}"][:], pb[0:128, 0:512], sb["bo"][:, mc:mc + 1],
                    sb[f"xsb{mc}"][:], OP.add, OP.add),
                    [f"ps{mc % 2}", "bo", f"xsb{mc}"], [f"hy{mc}"])
                dma("scalar", dram["rmid"][s, mc], sb[f"hy{mc}"][:],
                    [f"hy{mc}"], ["rmid"])
                if DEBUG and s == 0:
                    dma("sync", dram["dbg_hy"][mc], sb[f"hy{mc}"][:],
                        [f"hy{mc}"], [])

        def ln_stats(src_tensors, src_tokens):
            for kc in range(8):
                mm(ps[6][0:1, 0:512], sb["onesc"][:, 0:1],
                   src_tensors[kc][:, 0:512], kc == 0, kc == 7,
                   ["onesc", src_tokens[kc], "ps6"], ["ps6"])
            for kc in range(8):
                vop(lambda e, kc=kc: e.activation(
                    sb["t1"][:, 0:512], src_tensors[kc][:, 0:512], AF.Square),
                    [src_tokens[kc]], ["t1"], eng="scalar")
                mm(ps[7][0:1, 0:512], sb["onesc"][:, 0:1],
                   sb["t1"][:, 0:512], kc == 0, kc == 7,
                   ["onesc", "t1", "ps7"], ["ps7"])
            vop(lambda e: e.tensor_scalar(sb["rowb"][0:1, :], ps[6][0:1, 0:512],
                                          1.0 / 1024.0, None, OP.mult),
                ["ps6"], ["rowb"])
            vop(lambda e: e.tensor_scalar(sb["rowb2"][0:1, :], ps[7][0:1, 0:512],
                                          1.0 / 1024.0, None, OP.mult),
                ["ps7"], ["rowb2"])
            vop(lambda e: e.tensor_tensor(sb["Ftile"][0:1, 0:512],
                                          sb["rowb"][0:1, :], sb["rowb"][0:1, :],
                                          OP.mult),
                ["rowb"], ["Ftile"])
            vop(lambda e: e.tensor_tensor(sb["rowb2"][0:1, :], sb["rowb2"][0:1, :],
                                          sb["Ftile"][0:1, 0:512], OP.subtract),
                ["rowb2", "Ftile"], ["rowb2"])
            vop(lambda e: e.activation(sb["rowb2"][0:1, :], sb["rowb2"][0:1, :],
                                       AF.Sqrt, bias=sb["cb"][0:1, 1:2], scale=1.0),
                ["rowb2", "cb"], ["rowb2"], eng="scalar")
            vop(lambda e: e.reciprocal(sb["rowb2"][0:1, :], sb["rowb2"][0:1, :]),
                ["rowb2"], ["rowb2"])
            vop(lambda e: e.tensor_tensor(sb["rowb"][0:1, :], sb["rowb"][0:1, :],
                                          sb["rowb2"][0:1, :], OP.mult),
                ["rowb", "rowb2"], ["rowb"])
            mm(ps[6][0:128, 0:512], sb["ones1f"][0:1, 0:128], sb["rowb2"][0:1, :],
               True, True, ["ones1f", "rowb2", "ps6"], ["ps6"])
            mm(ps[7][0:128, 0:512], sb["ones1f"][0:1, 0:128], sb["rowb"][0:1, :],
               True, True, ["ones1f", "rowb", "ps7"], ["ps7"])

        for s in range(NSLAB):
            for kc in range(8):
                dma("sync", sb[f"xsb{kc}"][:], dram["rmid"][s, kc],
                    [f"xsb{kc}", "rmid"], [f"xsb{kc}"])
            ln_stats([sb[f"xsb{kc}"] for kc in range(8)],
                     [f"xsb{kc}" for kc in range(8)])
            for kc in range(8):
                vop(lambda e, kc=kc: e.tensor_tensor(sb["t1"][:], sb[f"xsb{kc}"][:],
                                                     ps[6][0:128, 0:512], OP.mult),
                    [f"xsb{kc}", "ps6"], ["t1"])
                vop(lambda e: e.tensor_tensor(sb["t1"][:], sb["t1"][:],
                                              ps[7][0:128, 0:512], OP.subtract),
                    ["t1", "ps7"], ["t1"])
                vop(lambda e, kc=kc, s=s: e.tensor_scalar(
                    sb[f"kf{kc}"][:, 512 * s:512 * s + 512], sb["t1"][:],
                    sb["l1g"][:, kc:kc + 1], sb["l1b"][:, kc:kc + 1],
                    OP.mult, OP.add),
                    ["t1", "l1g", "l1b"], [f"kf{kc}"])
        for g in range(4):
            load_w_slice("Wa", "w1_b", 0, 1024, 1024 * g, 1024 * g + 1024)
            load_w_slice("Wb", "w2_b", 1024 * g, 1024 * g + 1024, 0, 1024)
            for s in range(NSLAB):
                for mc in range(8):
                    pb = ps[mc % 2]
                    for kc in range(8):
                        mm(pb[0:128, 0:512],
                           sb["Wa"][:, 1024 * kc + 128 * mc:1024 * kc + 128 * mc + 128],
                           sb[f"kf{kc}"][:, 512 * s:512 * s + 512], kc == 0, kc == 7,
                           ["Wa", f"kf{kc}", f"ps{mc % 2}"], [f"ps{mc % 2}"])
                    vop(lambda e, pb=pb, g=g, mc=mc: e.activation(
                        sb[f"qs{mc}"][:, 0:512], pb[0:128, 0:512], AF.Relu,
                        bias=sb["bf1"][:, 8 * g + mc:8 * g + mc + 1], scale=1.0),
                        [f"ps{mc % 2}", "bf1"], [f"qs{mc}"], eng="scalar")
                for mc2 in range(8):
                    pb = ps[2 + mc2 % 2]
                    for mc in range(8):
                        mm(pb[0:128, 0:512],
                           sb["Wb"][:, 1024 * mc + 128 * mc2:1024 * mc + 128 * mc2 + 128],
                           sb[f"qs{mc}"][:, 0:512], mc == 0, mc == 7,
                           ["Wb", f"qs{mc}", f"ps{2 + mc2 % 2}"], [f"ps{2 + mc2 % 2}"])
                    yt = f"vt{mc2 // 2}"
                    ycols = slice(2080 * (mc2 % 2) + 520 * 0 + 512 * s,
                                  2080 * (mc2 % 2) + 512 * s + 512)
                    if g == 0:
                        vop(lambda e, pb=pb, yt=yt, yc=ycols: e.tensor_copy(
                            sb[yt][:, yc], pb[0:128, 0:512]),
                            [f"ps{2 + mc2 % 2}"], [yt])
                    else:
                        vop(lambda e, pb=pb, yt=yt, yc=ycols: e.tensor_tensor(
                            sb[yt][:, yc], sb[yt][:, yc], pb[0:128, 0:512], OP.add),
                            [f"ps{2 + mc2 % 2}", yt], [yt])
        for s in range(NSLAB):
            for kc in range(8):
                dma("sync", sb[f"xsb{kc}"][:], dram["rmid"][s, kc],
                    [f"xsb{kc}", "rmid"], [f"xsb{kc}"])
            ln_stats([sb[f"xsb{kc}"] for kc in range(8)],
                     [f"xsb{kc}" for kc in range(8)])
            for kc in range(8):
                vop(lambda e, kc=kc: e.tensor_tensor(sb["t1"][:], sb[f"xsb{kc}"][:],
                                                     ps[6][0:128, 0:512], OP.mult),
                    [f"xsb{kc}", "ps6"], ["t1"])
                vop(lambda e: e.tensor_tensor(sb["t1"][:], sb["t1"][:],
                                              ps[7][0:128, 0:512], OP.subtract),
                    ["t1", "ps7"], ["t1"])
                vop(lambda e, kc=kc: e.tensor_scalar(
                    sb["t1"][:], sb["t1"][:], sb["l1g"][:, kc:kc + 1],
                    sb["bsum"][:, kc:kc + 1], OP.mult, OP.add),
                    ["t1", "l1g", "bsum"], ["t1"])
                yt = f"vt{kc // 2}"
                ycols = slice(2080 * (kc % 2) + 512 * s, 2080 * (kc % 2) + 512 * s + 512)
                vop(lambda e, yt=yt, yc=ycols, kc=kc: e.tensor_tensor(
                    sb[f"hy{kc}"][:], sb["t1"][:], sb[yt][:, yc], OP.add),
                    ["t1", yt], [f"hy{kc}"])
            ln_stats([sb[f"hy{kc}"] for kc in range(8)],
                     [f"hy{kc}" for kc in range(8)])
            for kc in range(8):
                vop(lambda e, kc=kc: e.tensor_tensor(sb["t1"][:], sb[f"hy{kc}"][:],
                                                     ps[6][0:128, 0:512], OP.mult),
                    [f"hy{kc}", "ps6"], ["t1"])
                vop(lambda e: e.tensor_tensor(sb["t1"][:], sb["t1"][:],
                                              ps[7][0:128, 0:512], OP.subtract),
                    ["t1", "ps7"], ["t1"])
                vop(lambda e, kc=kc: e.tensor_scalar(
                    sb[f"xsb{kc}"][:], sb["t1"][:], sb["l2g"][:, kc:kc + 1],
                    sb["l2b"][:, kc:kc + 1], OP.mult, OP.add),
                    ["t1", "l2g", "l2b"], [f"xsb{kc}"])
                dma("scalar", dram["xmid"][s, kc], sb[f"xsb{kc}"][:],
                    [f"xsb{kc}"], ["xmid"])
                if has_a:
                    dma("scalar", dram["xout"][kc, s], sb[f"xsb{kc}"][:],
                        [f"xsb{kc}"], [])

    def rhs_prep():
        for kc in range(8):
            dma("sync", sb[f"rhsf{kc}"][:], dram["rhs"][128 * kc:128 * kc + 128, :],
                [f"kf{kc}"], [f"kf{kc}"])
        for nc2 in range(8):
            for kc in range(8):
                r.op("tensor", lambda e, kc=kc, nc2=nc2: e.transpose(
                    ps[5][0:128, 0:128],
                    sb[f"rhsf{kc}"][:, 128 * nc2:128 * nc2 + 128], sb["idf"][:]),
                    reads=[f"kf{kc}", "idf", "ps5"], writes=["ps5"])
                vop(lambda e, kc=kc: e.tensor_copy(
                    sb["etm"][:, 128 * kc:128 * kc + 128], ps[5][0:128, 0:128]),
                    ["ps5"], ["etm"])
            vop(lambda e: e.tensor_reduce(sb["st8"][:, 0:1], sb["etm"][:],
                                          axis=AX.X, op=OP.max,
                                          apply_absolute_value=True),
                ["etm"], ["st8"], big=False)
            vop(lambda e: e.tensor_scalar(sb["st8"][:, 1:2], sb["st8"][:, 0:1],
                                          1e-12, INV127, OP.max, OP.mult),
                ["st8"], ["st8"], big=False)
            vop(lambda e, nc2=nc2: e.tensor_copy(sb["scall"][:, nc2:nc2 + 1],
                                                 sb["st8"][:, 1:2]),
                ["st8"], ["scall"], big=False)
            vop(lambda e, nc2=nc2: e.reciprocal(sb["rcall"][:, nc2:nc2 + 1],
                                                sb["st8"][:, 1:2]),
                ["st8"], ["rcall"], big=False)
        for which, src, dst in (("s", "scall", "scolbc"), ("r", "rcall", "rcolbc")):
            for nc2 in range(8):
                mm(ps[5][0:1, 0:128], sb[src][:, nc2:nc2 + 1], sb["idf"][:],
                   True, True, [src, "idf", "ps5"], ["ps5"])
                vop(lambda e: e.tensor_copy(sb["rowb"][0:1, 0:128], ps[5][0:1, 0:128]),
                    ["ps5"], ["rowb"])
                mm(ps[6][0:128, 0:128], sb["ones1f"][0:1, 0:128],
                   sb["rowb"][0:1, 0:128], True, True,
                   ["ones1f", "rowb", "ps6"], ["ps6"])
                vop(lambda e, dst=dst, nc2=nc2: e.tensor_copy(
                    sb[dst][:, 128 * nc2:128 * nc2 + 128], ps[6][0:128, 0:128]),
                    ["ps6"], [dst])
        for kc in range(8):
            vop(lambda e, kc=kc: e.tensor_tensor(sb["atm"][:], sb[f"rhsf{kc}"][:],
                                                 sb["rcolbc"][:], OP.mult),
                [f"kf{kc}", "rcolbc"], ["atm"])
            vop(lambda e: e.tensor_copy(sb["qi"][:], sb["atm"][:]), ["atm"], ["qi"])
            vop(lambda e, kc=kc: e.tensor_copy(sb[f"qrhs{kc}"][:], sb["qi"][:]),
                ["qi"], [f"qrhs{kc}"])

    def ln_plain_tm(src_key, out_key, out_tensor, src_tensor):
        vop(lambda e: e.tensor_reduce(sb["st8"][:, 2:3], src_tensor[:],
                                      axis=AX.X, op=OP.add),
            [src_key], ["st8"])
        vop(lambda e: e.activation(sb["atm"][:], src_tensor[:], AF.Square),
            [src_key], ["atm"], eng="scalar")
        vop(lambda e: e.tensor_reduce(sb["st8"][:, 3:4], sb["atm"][:],
                                      axis=AX.X, op=OP.add),
            ["atm"], ["st8"], big=False)
        vop(lambda e: e.tensor_scalar(sb["st8"][:, 4:5], sb["st8"][:, 2:3],
                                      1.0 / 1024.0, None, OP.mult),
            ["st8"], ["st8"], big=False)
        vop(lambda e: e.tensor_scalar(sb["st8"][:, 5:6], sb["st8"][:, 3:4],
                                      1.0 / 1024.0, None, OP.mult),
            ["st8"], ["st8"], big=False)
        vop(lambda e: e.tensor_tensor(sb["st8"][:, 6:7], sb["st8"][:, 4:5],
                                      sb["st8"][:, 4:5], OP.mult),
            ["st8"], ["st8"], big=False)
        vop(lambda e: e.tensor_tensor(sb["st8"][:, 6:7], sb["st8"][:, 5:6],
                                      sb["st8"][:, 6:7], OP.subtract),
            ["st8"], ["st8"], big=False)
        vop(lambda e: e.activation(sb["st8"][:, 7:8], sb["st8"][:, 6:7],
                                   AF.Sqrt, bias=sb["cb"][:, 1:2], scale=1.0),
            ["st8", "cb"], ["st8"], eng="scalar", big=False)
        vop(lambda e: e.reciprocal(sb["st8"][:, 8:9], sb["st8"][:, 7:8]),
            ["st8"], ["st8"], big=False)
        vop(lambda e: e.tensor_scalar(out_tensor[:], src_tensor[:],
                                      sb["st8"][:, 4:5], sb["st8"][:, 8:9],
                                      OP.subtract, OP.mult),
            [src_key, "st8"], [out_key])

    def final_block():
        for tc in range(16):
            s, tcl = tc // 4, tc % 4
            for kc in range(8):
                dma("sync", sb["atm"][:, 128 * kc:128 * kc + 128],
                    dram["xmid"][s, kc][:, 128 * tcl:128 * tcl + 128],
                    ["atm", "xmid"], ["atm"])
            dma("sync", sb["x0t"][:], dram["x0"][128 * tc:128 * tc + 128, :],
                ["x0t"], ["x0t"])
            for kc in range(8):
                r.op("tensor", lambda e, kc=kc: e.transpose(
                    ps[5][0:128, 0:128], sb["atm"][:, 128 * kc:128 * kc + 128],
                    sb["idf"][:]),
                    reads=["atm", "idf", "ps5"], writes=["ps5"])
                vop(lambda e, kc=kc: e.tensor_tensor(
                    sb["etm"][:, 128 * kc:128 * kc + 128], ps[5][0:128, 0:128],
                    sb["x0t"][:, 128 * kc:128 * kc + 128], OP.add),
                    ["ps5", "x0t"], ["etm"])
            ln_plain_tm("etm", "x1t", sb["x1t"], sb["etm"])
            vop(lambda e: e.tensor_reduce(sb["st8"][:, 0:1], sb["x1t"][:],
                                          axis=AX.X, op=OP.max,
                                          apply_absolute_value=True),
                ["x1t"], ["st8"], big=False)
            vop(lambda e: e.tensor_scalar(sb["st8"][:, 1:2], sb["st8"][:, 0:1],
                                          1e-12, INV127, OP.max, OP.mult),
                ["st8"], ["st8"], big=False)
            vop(lambda e: e.reciprocal(sb["st8"][:, 9:10], sb["st8"][:, 1:2]),
                ["st8"], ["st8"], big=False)
            vop(lambda e: e.tensor_scalar(sb["atm"][:], sb["x1t"][:],
                                          sb["st8"][:, 9:10], None, OP.mult),
                ["x1t", "st8"], ["atm"])
            vop(lambda e: e.tensor_copy(sb["qi"][:], sb["atm"][:]), ["atm"], ["qi"])
            vop(lambda e: e.tensor_copy(sb["atm"][:], sb["qi"][:]), ["qi"], ["atm"])
            for j in range(8):
                r.op("tensor", lambda e, j=j: e.transpose(
                    ps[5][0:128, 0:128], sb["atm"][:, 128 * j:128 * j + 128],
                    sb["idf"][:]),
                    reads=["atm", "idf", "ps5"], writes=["ps5"])
                vop(lambda e, j=j: e.tensor_copy(
                    sb["qlf"][:, 128 * j:128 * j + 128], ps[5][0:128, 0:128]),
                    ["ps5"], ["qlf"])
            for n2 in range(2):
                pb = ps[n2]
                for kc in range(8):
                    mm(pb[0:128, 0:512], sb["qlf"][:, 128 * kc:128 * kc + 128],
                       sb[f"qrhs{kc}"][:, 512 * n2:512 * n2 + 512], kc == 0, kc == 7,
                       ["qlf", f"qrhs{kc}", f"ps{n2}"], [f"ps{n2}"])
                vop(lambda e, pb=pb, n2=n2: e.scalar_tensor_tensor(
                    sb[f"hy{n2}"][:], pb[0:128, 0:512], sb["st8"][:, 1:2],
                    sb["scolbc"][:, 512 * n2:512 * n2 + 512], OP.mult, OP.mult),
                    [f"ps{n2}", "st8", "scolbc"], [f"hy{n2}"])
                vop(lambda e, n2=n2: e.tensor_tensor(
                    sb[f"hy{n2}"][:], sb[f"hy{n2}"][:],
                    sb["x1t"][:, 512 * n2:512 * n2 + 512], OP.add),
                    [f"hy{n2}", "x1t"], [f"hy{n2}"])
            vop(lambda e: e.tensor_reduce(sb["st8"][:, 2:3], sb["hy0"][:],
                                          axis=AX.X, op=OP.add),
                ["hy0"], ["st8"], big=False)
            vop(lambda e: e.tensor_reduce(sb["st8"][:, 3:4], sb["hy1"][:],
                                          axis=AX.X, op=OP.add),
                ["hy1"], ["st8"], big=False)
            vop(lambda e: e.tensor_tensor(sb["st8"][:, 2:3], sb["st8"][:, 2:3],
                                          sb["st8"][:, 3:4], OP.add),
                ["st8"], ["st8"], big=False)
            vop(lambda e: e.activation(sb["atm"][:, 0:512], sb["hy0"][:], AF.Square),
                ["hy0"], ["atm"], eng="scalar")
            vop(lambda e: e.activation(sb["atm"][:, 512:1024], sb["hy1"][:], AF.Square),
                ["hy1"], ["atm"], eng="scalar")
            vop(lambda e: e.tensor_reduce(sb["st8"][:, 3:4], sb["atm"][:],
                                          axis=AX.X, op=OP.add),
                ["atm"], ["st8"], big=False)
            vop(lambda e: e.tensor_scalar(sb["st8"][:, 4:5], sb["st8"][:, 2:3],
                                          1.0 / 1024.0, None, OP.mult),
                ["st8"], ["st8"], big=False)
            vop(lambda e: e.tensor_scalar(sb["st8"][:, 5:6], sb["st8"][:, 3:4],
                                          1.0 / 1024.0, None, OP.mult),
                ["st8"], ["st8"], big=False)
            vop(lambda e: e.tensor_tensor(sb["st8"][:, 6:7], sb["st8"][:, 4:5],
                                          sb["st8"][:, 4:5], OP.mult),
                ["st8"], ["st8"], big=False)
            vop(lambda e: e.tensor_tensor(sb["st8"][:, 6:7], sb["st8"][:, 5:6],
                                          sb["st8"][:, 6:7], OP.subtract),
                ["st8"], ["st8"], big=False)
            vop(lambda e: e.activation(sb["st8"][:, 7:8], sb["st8"][:, 6:7],
                                       AF.Sqrt, bias=sb["cb"][:, 1:2], scale=1.0),
                ["st8", "cb"], ["st8"], eng="scalar", big=False)
            vop(lambda e: e.reciprocal(sb["st8"][:, 8:9], sb["st8"][:, 7:8]),
                ["st8"], ["st8"], big=False)
            for n2 in range(2):
                vop(lambda e, n2=n2: e.tensor_scalar(
                    sb["x1t"][:, 512 * n2:512 * n2 + 512], sb[f"hy{n2}"][:],
                    sb["st8"][:, 4:5], sb["st8"][:, 8:9], OP.subtract, OP.mult),
                    [f"hy{n2}", "st8"], ["x1t"])
            dma("scalar", dram["out"][128 * tc:128 * tc + 128, :], sb["x1t"][:],
                ["x1t"], [])

    if mode == "DBGC":
        vop(lambda e: e.memset(sb["srun"][:], 0.0), [], ["srun"])
        for j in range(8):
            dma("sync", sb[f"kf{j}"][0:64, 0:1040], dram["gst"][j],
                [f"kf{j}"], [f"kf{j}"])
            dma("sync", sb[f"kf{j}"][64:128, 0:1040], dram["gst"][j],
                [f"kf{j}"], [f"kf{j}"])
        for j in range(8):
            vop(lambda e, j=j: e.scalar_tensor_tensor(
                sb["srun"][:], sb[f"kf{j}"][:, 0:1040], sb["pm"][:, j:j + 1],
                sb["srun"][:], OP.mult, OP.add),
                [f"kf{j}", "pm", "srun"], ["srun"])
        vop(lambda e: e.tensor_copy(sb["sbf"][:], sb["srun"][:]), ["srun"], ["sbf"])
        dma("sync", dram["dbgc_sbf"], sb["sbf"][:], ["sbf"], [])
        dma("sync", dram["dbgc_kf0"], sb["kf0"][:, 0:1040], ["kf0"], [])
    if has_b and mode != "DBGC":
        phase_b()
    if has_a:
        phase_a(la)
    if has_f:
        r.barrier()
        rhs_prep()
        final_block()

    sems = {}
    from contextlib import ExitStack
    stk = ExitStack()
    for e in ENGINES:
        sems[e] = stk.enter_context(nc.semaphore(f"sem_{e}"))
    block = stk.enter_context(nc.Block())
    r.emit(block, sems)
    stk.close()
    nc._rec_counts = dict(r.cnt)
    return nc


def _consts():
    idf = np.eye(128, dtype=np.float32)
    idb = np.eye(128).astype(ml_dtypes.bfloat16)
    tk, tq = np.meshgrid(np.arange(128), np.arange(128), indexing="ij")
    triu = (tq >= tk).astype(np.float32)
    e16 = np.zeros((128, 128), np.float32)
    for c in range(8):
        e16[0:64, 16 * c + 2 * c] = 1.0
        e16[64:128, 16 * c + 2 * c + 1] = 1.0
    sel = np.zeros((16, 1024), np.float32)
    for d in range(8):
        sel[2 * d, 128 * d:128 * d + 64] = 1.0
        sel[2 * d + 1, 128 * d + 64:128 * d + 128] = 1.0
    ones1f = np.ones((1, 128), np.float32)
    return dict(idf=idf, idb=idb, triu=triu,
                e16=e16.astype(ml_dtypes.bfloat16),
                sel=sel.astype(ml_dtypes.bfloat16), ones1f=ones1f)


def _fm8(v):
    return np.ascontiguousarray(np.asarray(v).reshape(8, 128).T)


def _fm32(v):
    return np.ascontiguousarray(np.asarray(v).reshape(32, 128).T)


def kernel(L0_attn_x, L0_attn_Wq, L0_attn_bq, L0_attn_Wk, L0_attn_bk,
           L0_attn_Wv, L0_attn_bv, L0_attn_Wo, L0_attn_bo, L0_attn_omega,
           L0_attn_ln1_g, L0_attn_ln1_b, L0_attn_ln2_g, L0_attn_ln2_b,
           L0_attn_W1, L0_attn_bf1, L0_attn_W2, L0_attn_bf2, L0_ffn_rhs):
    x = np.ascontiguousarray(np.asarray(L0_attn_x, dtype=np.float32))
    Wq = np.asarray(L0_attn_Wq, np.float32)
    Wk = np.asarray(L0_attn_Wk, np.float32)
    Wv = np.asarray(L0_attn_Wv, np.float32)
    Wo = np.asarray(L0_attn_Wo, np.float32)
    W1 = np.asarray(L0_attn_W1, np.float32)
    W2 = np.asarray(L0_attn_W2, np.float32)
    bq = np.asarray(L0_attn_bq, np.float32)
    bk = np.asarray(L0_attn_bk, np.float32)
    bv = np.asarray(L0_attn_bv, np.float32)
    bo = np.asarray(L0_attn_bo, np.float32)
    bf1 = np.asarray(L0_attn_bf1, np.float32)
    bf2 = np.asarray(L0_attn_bf2, np.float32)
    l1g = np.asarray(L0_attn_ln1_g, np.float32)
    l1b = np.asarray(L0_attn_ln1_b, np.float32)
    l2g = np.asarray(L0_attn_ln2_g, np.float32)
    l2b = np.asarray(L0_attn_ln2_b, np.float32)
    om = np.asarray(L0_attn_omega, np.float32)
    rhs = np.ascontiguousarray(np.asarray(L0_ffn_rhs, np.float32))

    consts = _consts()
    x0s = [np.ascontiguousarray(x[c // 4, 2048 * (c % 4):2048 * (c % 4) + 2048, :])
           for c in range(NCORE)]
    pmasks = []
    for c in range(NCORE):
        g0 = (c // 4) * 4
        m = np.zeros(8, np.float32)
        m[g0:c] = 1.0
        pmasks.append(np.broadcast_to(m, (128, 8)).copy())

    def layer_vecs(l):
        return dict(
            bq_b=_fm8(bq[l]), bo_b=_fm8(bo[l]), ln1g_b=_fm8(l1g[l]),
            ln1b_b=_fm8(l1b[l]), ln2g_b=_fm8(l2g[l]), ln2b_b=_fm8(l2b[l]),
            bf1_b=_fm32(bf1[l]), bf2_b=_fm8(bf2[l]),
            wq_b=np.ascontiguousarray(Wq[l]), wo_b=np.ascontiguousarray(Wo[l]),
            w1_b=np.ascontiguousarray(W1[l]), w2_b=np.ascontiguousarray(W2[l]),
            om_b=np.ascontiguousarray(np.vstack([om[l], om[l]])))

    def layer_a_in(l):
        return dict(
            wk_a=np.ascontiguousarray(Wk[l]), wv_a=np.ascontiguousarray(Wv[l]),
            bk_a=_fm8(bk[l]), bv_a=np.ascontiguousarray(bv[l]).reshape(1, D),
            om_a=np.ascontiguousarray(np.vstack([om[l], om[l]])))

    state = [dict() for _ in range(NCORE)]

    nc0 = build_launch("A", 0, None)
    in_maps = []
    for c in range(NCORE):
        m = dict(consts)
        m.update(layer_a_in(0))
        m["x0"] = x0s[c]
        in_maps.append(m)
    res = run_bass_kernel_spmd(nc0, in_maps, core_ids=list(range(NCORE)))
    for c in range(NCORE):
        state[c] = {k: res.results[c][k]
                    for k in ("xout", "kf_out", "vt_out", "sch_out", "stot")}

    for k in (1, 2, 3):
        lb = k - 1
        la = k if k < 3 else None
        mode = "BA" if k < 3 else "BF"
        nck = build_launch(mode, la, lb)
        gst = np.stack([state[c]["stot"] for c in range(NCORE)])
        in_maps = []
        for c in range(NCORE):
            m = dict(consts)
            m.update(layer_vecs(lb))
            m["xin"] = state[c]["xout"]
            m["kf_in"] = state[c]["kf_out"]
            m["vt_in"] = state[c]["vt_out"]
            m["sch_in"] = state[c]["sch_out"]
            m["gst"] = gst
            m["pmask"] = pmasks[c]
            if mode == "BA":
                m.update(layer_a_in(la))
            else:
                m["x0"] = x0s[c]
                m["rhs"] = rhs
            in_maps.append(m)
        res = run_bass_kernel_spmd(nck, in_maps, core_ids=list(range(NCORE)))
        if mode == "BA":
            ns = []
            for c in range(NCORE):
                ns.append({kk: res.results[c][kk]
                           for kk in ("xout", "kf_out", "vt_out", "sch_out", "stot")})
            state = ns
        else:
            outs = [res.results[c]["out"] for c in range(NCORE)]

    full = np.empty((2, 8192, D), np.float32)
    for c in range(NCORE):
        full[c // 4, 2048 * (c % 4):2048 * (c % 4) + 2048, :] = outs[c]
    return full


# version 08
# speedup vs baseline: 1.2172x; 1.0444x over previous
import numpy as np
import ml_dtypes
import concourse.bass as bass
import concourse.mybir as mybir
from concourse.bass_utils import run_bass_kernel_spmd

F32 = mybir.dt.float32
BF16 = mybir.dt.bfloat16
I32 = mybir.dt.int32
AF = mybir.ActivationFunctionType
OP = mybir.AluOpType
AX = mybir.AxisListType

NCORE = 8
T = 2048
D = 1024
H = 16
FF = 4096
NL = 3
C = 128
NCH = 16
SLAB = 512
NSLAB = 4
CQ = 0.0625
LN8 = 2.0794415416798357
CS = 0.35355339059327373
LN_EPS = 1e-5
ATTN_EPS = 1e-6
INV127 = 1.0 / 127.0

ENGINES = ("tensor", "vector", "scalar", "gpsimd", "sync", "gdma", "sdma")
QUEUE_LABELS = {"tensor": ("tensor",), "vector": ("vector",),
                "scalar": ("scalar", "sdma"), "gpsimd": ("gpsimd", "gdma"),
                "sync": ("sync",)}
DEBUG = False
SCAN_BATCHED = 4
SCAN_SER_COLS = False


class Rec:

    SPACED = ("vector", "gpsimd", "scalar")

    def __init__(self):
        self.items = []
        self.cnt = {e: 0 for e in ENGINES}
        self.opi = {e: 0 for e in ENGINES}
        self.writer = {}
        self.readers = {}

    def barrier(self):
        snap = dict(self.cnt)
        self._pend = {e: {d: v for d, v in snap.items() if d != e and v > 0}
                      for e in ENGINES}

    def op(self, eng, fn, reads=(), writes=(), dma=False, big=False):
        inc = 16 if dma else 1
        waits = {}
        pend = getattr(self, "_pend", None)
        if pend and pend.get(eng):
            waits.update(pend.pop(eng))

        def need(dep):
            deng, dcnt = dep[0], dep[1]
            dep_dma = dep[3] if len(dep) > 3 else False
            if deng == eng and not dep_dma and eng in ("tensor", "sync"):
                return
            if (eng == "gdma" and deng == "gpsimd") or (eng == "sdma" and deng == "scalar"):
                return
            if waits.get(deng, 0) < dcnt:
                waits[deng] = dcnt

        for t in reads:
            w = self.writer.get(t)
            if w:
                need(w)
        for t in writes:
            w = self.writer.get(t)
            if w:
                need(w)
            for rd in self.readers.get(t, ()):
                need(rd)
        self.cnt[eng] += inc
        self.opi[eng] += 1
        here = (eng, self.cnt[eng], self.opi[eng], dma, big or dma)
        for t in reads:
            self.readers.setdefault(t, []).append(here)
        for t in writes:
            self.writer[t] = here
            self.readers[t] = []
        self.items.append((eng, fn, waits, inc))

    def emit(self, block, sems):
        finals = dict(self.cnt)
        fence = {e: set() for e in ENGINES}
        for eng, fn, waits, inc in self.items:
            for d, v in waits.items():
                fence[d].add(v)

        def mk(qname, items):
            cnt = {l: 0 for l in QUEUE_LABELS[qname]}
            selfhw = {l: 0 for l in QUEUE_LABELS[qname]}
            fi = {l: 0 for l in QUEUE_LABELS[qname]}
            fv = {l: sorted(fence[l]) for l in QUEUE_LABELS[qname]}
            dma_lbl = [l for l in QUEUE_LABELS[qname] if l in ("sync", "gdma", "sdma")]
            hwq = {d: 0 for d in ENGINES}

            def body(eng):
                for lbl, fn, waits, inc in items:
                    if lbl in dma_lbl or inc == 16:
                        fmax = 0
                        while fi[lbl] < len(fv[lbl]) and fv[lbl][fi[lbl]] <= cnt[lbl]:
                            fmax = fv[lbl][fi[lbl]]
                            fi[lbl] += 1
                        if fmax > selfhw[lbl]:
                            eng.wait_ge(sems[lbl], fmax)
                            selfhw[lbl] = fmax
                    for d, v in sorted(waits.items()):
                        if hwq[d] < v:
                            eng.wait_ge(sems[d], v)
                            hwq[d] = v
                    fn(eng).then_inc(sems[lbl], inc)
                    cnt[lbl] += inc
                for d in ENGINES:
                    if d not in QUEUE_LABELS[qname] and finals[d] > 0 and hwq[d] < finals[d]:
                        eng.wait_ge(sems[d], finals[d])
                for l in QUEUE_LABELS[qname]:
                    if finals[l] > cnt[l] or (finals[l] > selfhw[l] and l in dma_lbl):
                        eng.wait_ge(sems[l], finals[l])
            return body

        for qname in QUEUE_LABELS:
            qitems = [(l, fn, w, inc) for (l, fn, w, inc) in
                      [(e, f, w, i) for (e, f, w, i) in self.items]
                      if l in QUEUE_LABELS[qname]]
            getattr(block, qname)(mk(qname, qitems))


SBUF_BASE = 16512


class Alloc:
    def __init__(self, nc):
        self.nc = nc
        self.off = SBUF_BASE
        self.offs = {}

    def at(self, name, shape, dtype, offset):
        return self.nc.alloc_sbuf_tensor_at(name, shape, dtype, offset=offset)

    def buf(self, name, shape, dtype):
        per = int(np.prod(shape[1:])) * mybir.dt.size(dtype)
        per = (per + 31) // 32 * 32
        t = self.nc.alloc_sbuf_tensor_at(name, shape, dtype, offset=self.off)
        self.offs[name] = self.off
        self.off += per
        return t

    def mark(self):
        return self.off


def ceil32(x):
    return (x + 31) // 32 * 32


def build_launch(mode, la, lb):
    nc = bass.Bass("TRN2", target_bir_lowering=False, debug=False)
    r = Rec()

    dram = {}

    def din(name, shape, dtype):
        dram[name] = nc.dram_tensor(name, shape, dtype, kind="ExternalInput").ap()
        return dram[name]

    def dout(name, shape, dtype):
        dram[name] = nc.dram_tensor(name, shape, dtype, kind="ExternalOutput").ap()
        return dram[name]

    din("idf", [128, 128], F32)
    din("idb", [128, 128], BF16)
    din("triu", [128, 128], F32)
    din("e16", [128, 128], BF16)
    din("sel", [16, 1024], BF16)
    din("ones1f", [1, 128], F32)

    has_a = mode in ("A", "BA")
    has_b = mode in ("BA", "BF", "DBGC")
    has_f = mode == "BF"
    if mode == "DBGC":
        has_a = False

    if mode == "A":
        din("x0", [T, D], F32)
    else:
        din("xin", [8, NSLAB, 128, SLAB], F32)
    if mode == "DBGC":
        dout("dbgc_sbf", [128, 1040], BF16)
        dout("dbgc_kf0", [128, 1040], BF16)
    if has_a:
        din("wk_a", [D, D], F32)
        din("wv_a", [D, D], F32)
        din("bk_a", [128, 8], F32)
        din("bv_a", [1, D], F32)
        din("om_a", [128, 32], F32)
        dout("xout", [8, NSLAB, 128, SLAB], F32)
        dout("kf_out", [8, 128, T], BF16)
        dout("vt_out", [NSLAB, 128, 4160], BF16)
        dout("sch_out", [NCH, 64, 1040], BF16)
        dout("stot", [64, 1040], BF16)
    if has_b:
        din("kf_in", [8, 128, T], BF16)
        din("vt_in", [NSLAB, 128, 4160], BF16)
        din("sch_in", [NCH, 64, 1040], BF16)
        din("gst", [8, 64, 1040], BF16)
        din("pmask", [128, 8], F32)
        din("wq_b", [D, D], F32)
        din("wo_b", [D, D], F32)
        din("w1_b", [D, FF], F32)
        din("w2_b", [FF, D], F32)
        din("bq_b", [128, 8], F32)
        din("bo_b", [128, 8], F32)
        din("ln1g_b", [128, 8], F32)
        din("ln1b_b", [128, 8], F32)
        din("ln2g_b", [128, 8], F32)
        din("ln2b_b", [128, 8], F32)
        din("bf1_b", [128, 32], F32)
        din("bf2_b", [128, 8], F32)
        din("om_b", [128, 32], F32)
        dram["rmid"] = nc.dram_tensor("rmid", [NSLAB, 8, 128, SLAB], F32).ap()
        dram["xmid"] = nc.dram_tensor("xmid", [NSLAB, 8, 128, SLAB], F32).ap()
    if has_f:
        din("x0", [T, D], F32)
        din("rhs", [D, D], F32)
        dout("out", [T, D], F32)
    if DEBUG and has_b:
        dout("dbg_qf", [8, 128, SLAB], BF16)
        dout("dbg_sbf", [128, 1040], BF16)
        dout("dbg_atm", [128, 1024], F32)
        dout("dbg_hy", [8, 128, SLAB], F32)
        dout("dbg_num", [2, 128, 65], F32)
        dout("dbg_a2m", [2, 128, 128], BF16)

    al = Alloc(nc)
    sb = {}

    def B(name, shape, dtype):
        sb[name] = al.buf(name, shape, dtype)
        return sb[name]

    for i in range(8):
        B(f"xsb{i}", [128, SLAB], F32)
    for i in range(8):
        B(f"xbf{i}", [128, SLAB], BF16)
    B("Wa", [128, 8192], BF16)
    B("Wb", [128, 8192], BF16)
    kf_off = al.mark()
    for i in range(8):
        B(f"kf{i}", [128, T], BF16)
    vt_off = al.mark()
    for i in range(NSLAB):
        B(f"vt{i}", [128, 4160], BF16)
    for i in range(8):
        B(f"qs{i}", [128, SLAB], BF16)
    for i in range(8):
        B(f"qf{i}", [128, SLAB], BF16)
    B("srun", [128, 1040], F32)
    B("sbf", [128, 1040], BF16)
    B("schb0", [128, 1040], BF16)
    B("schb1", [128, 1040], BF16)
    B("atm", [128, 1024], F32)
    B("etm", [128, 1024], F32)
    B("escr", [128, SLAB], F32)
    B("sq0", [128, SLAB], BF16)
    B("sq1", [128, SLAB], BF16)
    B("t1", [128, SLAB], F32)
    for i in range(8):
        B(f"hy{i}", [128, SLAB], F32)
    B("a2m", [128, 2048], BF16)
    B("cb", [128, 4], F32)
    r._junk = B("junk", [128, 64], F32)
    B("onesc", [128, 1], F32)
    B("onescb", [128, 1], BF16)
    B("oq", [128, 128], BF16)
    B("omsb", [128, 32], F32)
    B("dcol", [128, 32], F32)
    B("st8", [128, 12], F32)
    B("rowb", [1, SLAB], F32)
    B("rowb2", [1, SLAB], F32)
    if has_a:
        B("bvbc", [128, 1024], F32)
        B("stotsb", [128, 1040], F32)
        B("bvrow", [1, 1024], F32)
    B("bsum", [128, 8], F32)
    B("idf", [128, 128], F32)
    B("idb", [128, 128], BF16)
    B("triu", [128, 128], F32)
    B("e16", [128, 128], BF16)
    B("sel", [16, 1024], BF16)
    B("ones1f", [1, 128], F32)
    B("pm", [128, 8], F32)
    B("Ftile", [16, SLAB], F32)
    B("Fbf", [16, SLAB], BF16)
    for nm in ("bq", "bo", "l1g", "l1b", "l2g", "l2b", "bf2", "bk"):
        B(nm, [128, 8], F32)
    B("bf1", [128, 32], F32)
    if has_f:
        for i in range(8):
            sb[f"qrhs{i}"] = al.at(f"qrhs{i}", [128, 1024], BF16,
                                   offset=al.offs[f"xsb{i}"])
        sb["scolbc"] = al.at("scolbc", [128, 1024], F32, offset=al.offs["xbf0"])
        sb["rcolbc"] = al.at("rcolbc", [128, 1024], F32, offset=al.offs["xbf4"])
        B("scall", [128, 8], F32)
        B("rcall", [128, 8], F32)
        for i in range(8):
            sb[f"rhsf{i}"] = al.at(f"rhsf{i}", [128, 1024], F32,
                                   offset=kf_off + i * ceil32(T * 2))
        sb["x1t"] = al.at("x1t", [128, 1024], F32, offset=al.offs["qf0"])
        sb["qi"] = al.at("qi", [128, 1024], I32, offset=al.offs["qf4"])
        sb["qlf"] = al.at("qlf", [128, 1024], BF16, offset=al.offs["escr"])
        sb["x0t"] = al.at("x0t", [128, 1024], F32, offset=al.offs["srun"])

    assert al.off <= SBUF_BASE + 208 * 1024, f"SBUF over budget: {al.off}"

    ps = [nc.alloc_psum_tensor(f"ps{i}", [128, 512], F32) for i in range(8)]

    def dma(q, out_ap, in_ap, reads, writes):
        lbl = {"gpsimd": "gdma", "scalar": "sdma"}.get(q, q)
        r.op(lbl, lambda e, o=out_ap, i=in_ap: e.dma_start(out=o, in_=i),
             reads=reads, writes=writes, dma=True)

    def mm(out_ap, lh, rh, st, sp, reads, writes, tile_position=None):
        r.op("tensor",
             lambda e, o=out_ap, a=lh, b=rh, s=st, p=sp, tp=tile_position:
             e.matmul(o, a, b, start=s, stop=p, tile_position=tp),
             reads=reads, writes=writes)

    def vop(fn, reads, writes, eng="vector", big=True):
        r.op(eng, fn, reads=reads, writes=writes, big=big)

    for nm, key in (("idf", "idf"), ("idb", "idb"), ("triu", "triu"),
                    ("e16", "e16"), ("sel", "sel"), ("ones1f", "ones1f")):
        dma("sync", sb[nm][:], dram[key], [], [nm])
    if has_b:
        dma("sync", sb["pm"][:], dram["pmask"], [], ["pm"])
    vop(lambda e: e.memset(sb["onesc"][:], 1.0), [], ["onesc"])
    vop(lambda e: e.memset(sb["onescb"][:], 1.0), [], ["onescb"])
    vop(lambda e: e.memset(sb["cb"][:, 0:1], -LN8), [], ["cb"])
    vop(lambda e: e.memset(sb["cb"][:, 1:2], LN_EPS), [], ["cb"])
    vop(lambda e: e.memset(sb["cb"][:, 2:3], ATTN_EPS), [], ["cb"])

    def load_layer_vecs(which):
        if which == "B":
            for nm, key in (("bq", "bq_b"), ("bo", "bo_b"), ("l1g", "ln1g_b"),
                            ("l1b", "ln1b_b"), ("l2g", "ln2g_b"), ("l2b", "ln2b_b"),
                            ("bf2", "bf2_b")):
                dma("sync", sb[nm][:], dram[key], [], [nm])
            dma("sync", sb["bf1"][:], dram["bf1_b"], [], ["bf1"])
            vop(lambda e: e.tensor_tensor(sb["bsum"][:], sb["l1b"][:], sb["bf2"][:], OP.add),
                ["l1b", "bf2"], ["bsum"])
        else:
            dma("sync", sb["bk"][:], dram["bk_a"], [], ["bk"])
            dma("sync", sb["bvrow"][:], dram["bv_a"], [], ["bvrow"])

    def build_oq(omkey):
        dma("sync", sb["omsb"][:], dram[omkey], [], ["om"])
        vop(lambda e: e.memset(sb["oq"][:], 0.0), [], ["oq"])
        for (r0, c0, sc) in ((0, 0, CS), (0, 32, -CS), (64, 64, CS), (64, 96, -CS)):
            vop(lambda e, r0=r0, c0=c0, sc=sc:
                e.tensor_scalar(sb["oq"][r0:r0 + 64, c0:c0 + 32],
                                sb["omsb"][r0:r0 + 64, 0:32], sc, None, OP.mult),
                ["om", "oq"], ["oq"], big=False)

    def build_bvbc():
        for n2 in range(2):
            mm(ps[2][0:128, 0:512], sb["ones1f"][0:1, 0:128],
               sb["bvrow"][0:1, 512 * n2:512 * n2 + 512], True, True,
               ["ones1f", "bvrow", "ps2"], ["ps2"])
            vop(lambda e, n2=n2: e.tensor_copy(sb["bvbc"][:, 512 * n2:512 * n2 + 512],
                                               ps[2][0:128, 0:512]),
                ["ps2"], ["bvbc"])

    def load_w(dst, key, ncols=D):
        for kc in range(8):
            dma("gpsimd", sb[dst][:, ncols * kc:ncols * kc + ncols],
                dram[key][128 * kc:128 * kc + 128, :], [dst], [dst])

    def load_w_slice(dst, key, r0, r1c, c0, c1):
        n = c1 - c0
        for kc in range((r1c - r0) // 128):
            dma("gpsimd", sb[dst][:, n * kc:n * kc + n],
                dram[key][r0 + 128 * kc:r0 + 128 * kc + 128, c0:c1], [dst], [dst])

    def favor_fm(s, wkey, biaskey, dst_pref, store_whole):
        col = 512 * s if store_whole else 0
        for mc in range(8):
            pb = ps[mc % 2]
            for kc in range(8):
                mm(pb[0:128, 0:512],
                   sb[wkey][:, 1024 * kc + 128 * mc:1024 * kc + 128 * mc + 128],
                   sb[f"xbf{kc}"][:, 0:512], kc == 0, kc == 7,
                   [wkey, f"xbf{kc}", f"ps{mc % 2}"], [f"ps{mc % 2}"])
            vop(lambda e, pb=pb, mc=mc: e.activation(
                sb[f"qs{mc}"][:, 0:512], pb[0:128, 0:512], AF.Identity,
                bias=sb[biaskey][:, mc:mc + 1], scale=1.0),
                [f"ps{mc % 2}", biaskey], [f"qs{mc}"], eng="scalar")
        for mc in range(8):
            sqb = f"sq{mc % 2}"
            vop(lambda e, mc=mc, sqb=sqb: e.activation(
                sb[sqb][:, 0:512], sb[f"qs{mc}"][:, 0:512], AF.Square),
                [f"qs{mc}"], [sqb], eng="scalar")
            mm(ps[3][0:16, 0:512], sb["e16"][:, 16 * mc:16 * mc + 16],
               sb[sqb][:, 0:512], mc == 0, mc == 7,
               ["e16", sqb, "ps3"], ["ps3"])
        vop(lambda e: e.activation(sb["Ftile"][:, 0:512], ps[3][0:16, 0:512],
                                   AF.Exp, bias=sb["cb"][0:16, 0:1], scale=-CQ),
            ["ps3", "cb"], ["Ftile"], eng="scalar")
        vop(lambda e: e.tensor_copy(sb["Fbf"][:, 0:512], sb["Ftile"][:, 0:512]),
            ["Ftile"], ["Fbf"])
        for d in range(8):
            mm(ps[4][0:128, 0:512], sb["oq"][:, 0:128], sb[f"qs{d}"][:, 0:512],
               True, True, ["oq", f"qs{d}", "ps4"], ["ps4"])
            mm(ps[5][0:128, 0:512], sb["sel"][0:16, 128 * d:128 * d + 128],
               sb["Fbf"][0:16, 0:512], True, True,
               ["sel", "Fbf", "ps5"], ["ps5"])
            vop(lambda e: e.activation(sb["escr"][:, 0:512], ps[4][0:128, 0:512], AF.Exp),
                ["ps4"], ["escr"], eng="scalar")
            tgt = f"{dst_pref}{d}"
            vop(lambda e, tgt=tgt, col=col: e.tensor_tensor(
                sb[tgt][:, col:col + 512], sb["escr"][:, 0:512],
                ps[5][0:128, 0:512], OP.mult),
                ["escr", "ps5"], [tgt])

    def phase_a(layer_first):
        build_oq("om_a")
        load_layer_vecs("A")
        build_bvbc()
        load_w("Wa", "wk_a")
        load_w("Wb", "wv_a")
        for s in range(NSLAB):
            for tcl in range(4):
                vop(lambda e, s=s, tcl=tcl: e.memset(
                    sb[f"vt{s}"][:, 1040 * tcl:1040 * tcl + 1040]
                    .rearrange("p (h j) -> p h j", j=65)[:, :, 64:65], 1.0),
                    [f"vt{s}"], [f"vt{s}"])
        vop(lambda e: e.memset(sb["stotsb"][:], 0.0), [], ["stotsb"])
        for s in range(NSLAB):
            if mode == "A":
                for tcl in range(4):
                    c = 4 * s + tcl
                    dma("sync", sb["atm"][:], dram["x0"][128 * c:128 * c + 128, :],
                        ["atm"], ["atm"])
                    for kc in range(8):
                        r.op("tensor", lambda e, kc=kc: e.transpose(
                            ps[6][0:128, 0:128],
                            sb["atm"][:, 128 * kc:128 * kc + 128], sb["idf"][:]),
                            reads=["atm", "idf", "ps6"], writes=["ps6"])
                        vop(lambda e, kc=kc, tcl=tcl: e.tensor_copy(
                            sb[f"xsb{kc}"][:, 128 * tcl:128 * tcl + 128],
                            ps[6][0:128, 0:128]),
                            ["ps6"], [f"xsb{kc}"])
                for kc in range(8):
                    dma("scalar", dram["xout"][kc, s], sb[f"xsb{kc}"][:],
                        [f"xsb{kc}"], [])
            else:
                for kc in range(8):
                    dma("sync", sb[f"xsb{kc}"][:], dram["xmid"][s, kc],
                        [f"xsb{kc}", "xmid"], [f"xsb{kc}"])
            for kc in range(8):
                vop(lambda e, kc=kc: e.tensor_copy(sb[f"xbf{kc}"][:],
                                                   sb[f"xsb{kc}"][:]),
                    [f"xsb{kc}"], [f"xbf{kc}"], eng="gpsimd")
            for tcl in range(4):
                for n2 in range(2):
                    pb = ps[n2]
                    for kc in range(8):
                        mm(pb[0:128, 0:512],
                           sb[f"xbf{kc}"][:, 128 * tcl:128 * tcl + 128],
                           sb["Wb"][:, 1024 * kc + 512 * n2:1024 * kc + 512 * n2 + 512],
                           kc == 0, kc == 7,
                           [f"xbf{kc}", "Wb", f"ps{n2}"], [f"ps{n2}"])
                    vop(lambda e, s=s, tcl=tcl, n2=n2, pb=pb: e.tensor_tensor(
                        sb[f"vt{s}"][:, 1040 * tcl + 520 * n2:1040 * tcl + 520 * n2 + 520]
                        .rearrange("p (h j) -> p h j", j=65)[:, :, 0:64],
                        pb[0:128, 0:512].rearrange("p (h j) -> p h j", j=64),
                        sb["bvbc"][:, 512 * n2:512 * n2 + 512]
                        .rearrange("p (h j) -> p h j", j=64), OP.add),
                        [f"ps{n2}", "bvbc", f"vt{s}"], [f"vt{s}"])
            favor_fm(s, "Wa", "bk", "kf", True)
            for tcl in range(4):
                c = 4 * s + tcl
                for d in range(8):
                    pb = ps[2 + d // 4]
                    mm(pb[0:128, 128 * (d % 4):128 * (d % 4) + 128],
                       sb[f"qs{d}"][:, 128 * tcl:128 * tcl + 128],
                       sb["oq"][:, 0:128], True, True,
                       [f"qs{d}", "oq", f"ps{2 + d // 4}"], [f"ps{2 + d // 4}"])
                r.op("tensor", lambda e, tcl=tcl: e.transpose(
                    ps[7][0:128, 0:16],
                    sb["Ftile"][0:16, 128 * tcl:128 * tcl + 128],
                    sb["idf"][0:16, 0:16]),
                    reads=["Ftile", "idf", "ps7"], writes=["ps7"])
                for half in range(2):
                    vop(lambda e, half=half: e.activation(
                        sb["etm"][:, 512 * half:512 * half + 512],
                        ps[2 + half][0:128, 0:512], AF.Exp),
                        [f"ps{2 + half}"], ["etm"], eng="scalar")
                vop(lambda e: e.tensor_copy(sb["atm"][:, 0:16], ps[7][0:128, 0:16]),
                    ["ps7"], ["atm"], big=False)
                vop(lambda e: e.tensor_tensor(
                    sb["schb1"][:, 0:1024].rearrange("p (h j) -> p h j", j=64),
                    sb["etm"][:, 0:1024].rearrange("p (h j) -> p h j", j=64),
                    sb["atm"][:, 0:16].rearrange("p (h j) -> p h j", j=1)
                    .broadcast_to((128, 16, 64)), OP.mult),
                    ["etm", "atm"], ["schb1"])
                for h in range(16):
                    bk, col = ("ps4", 65 * h) if h < 7 else (
                        ("ps5", 65 * (h - 7)) if h < 14 else ("ps6", 65 * (h - 14)))
                    mm(ps[int(bk[2])][0:64, col:col + 65],
                       sb["schb1"][:, 64 * h:64 * h + 64],
                       sb[f"vt{s}"][:, 1040 * tcl + 65 * h:1040 * tcl + 65 * h + 65],
                       True, True, ["schb1", f"vt{s}", bk], [bk])
                for (bk, c0, w) in (("ps4", 0, 455), ("ps5", 455, 455), ("ps6", 910, 130)):
                    vop(lambda e, bk=bk, c0=c0, w=w: e.tensor_copy(
                        sb["schb0"][0:64, c0:c0 + w], ps[int(bk[2])][0:64, 0:w]),
                        [bk], ["schb0"])
                vop(lambda e: e.tensor_tensor(sb["stotsb"][0:64, :],
                                              sb["stotsb"][0:64, :],
                                              sb["schb0"][0:64, :], OP.add),
                    ["schb0", "stotsb"], ["stotsb"])
                dma("scalar", dram["sch_out"][c], sb["schb0"][0:64, :], ["schb0"], [])
        for d in range(8):
            dma("scalar", dram["kf_out"][d], sb[f"kf{d}"][:], [f"kf{d}"], [])
        for s in range(NSLAB):
            dma("scalar", dram["vt_out"][s], sb[f"vt{s}"][:], [f"vt{s}"], [])
        vop(lambda e: e.tensor_copy(sb["schb0"][0:64, :], sb["stotsb"][0:64, :]),
            ["stotsb"], ["schb0"])
        dma("scalar", dram["stot"], sb["schb0"][0:64, :], ["schb0"], [])

    def phase_b():
        build_oq("om_b")
        load_layer_vecs("B")
        vop(lambda e: e.memset(sb["srun"][:], 0.0), [], ["srun"])
        for j in range(8):
            dma("sync", sb[f"kf{j}"][0:64, 0:1040], dram["gst"][j],
                [f"kf{j}"], [f"kf{j}"])
            dma("sync", sb[f"kf{j}"][64:128, 0:1040], dram["gst"][j],
                [f"kf{j}"], [f"kf{j}"])
        for j in range(8):
            vop(lambda e, j=j: e.scalar_tensor_tensor(
                sb["srun"][:], sb[f"kf{j}"][:, 0:1040], sb["pm"][:, j:j + 1],
                sb["srun"][:], OP.mult, OP.add),
                [f"kf{j}", "pm", "srun"], ["srun"])
        vop(lambda e: e.tensor_copy(sb["sbf"][:], sb["srun"][:]), ["srun"], ["sbf"])
        load_w("Wa", "wq_b")
        load_w("Wb", "wo_b")
        for d in range(8):
            dma("sync", sb[f"kf{d}"][:], dram["kf_in"][d], [f"kf{d}"], [f"kf{d}"])
        for s in range(NSLAB):
            dma("sync", sb[f"vt{s}"][:], dram["vt_in"][s], [f"vt{s}"], [f"vt{s}"])
        if DEBUG:
            dma("sync", dram["dbg_sbf"], sb["sbf"][:], ["sbf"], [])

        for s in range(NSLAB):
            for kc in range(8):
                dma("sync", sb[f"xsb{kc}"][:], dram["xin"][kc, s],
                    [f"xsb{kc}"], [f"xsb{kc}"])
                vop(lambda e, kc=kc: e.tensor_copy(sb[f"xbf{kc}"][:],
                                                   sb[f"xsb{kc}"][:]),
                    [f"xsb{kc}"], [f"xbf{kc}"], eng="gpsimd")
            favor_fm(s, "Wa", "bq", "qf", False)
            if DEBUG and s == 0:
                for dd in range(8):
                    dma("sync", dram["dbg_qf"][dd], sb[f"qf{dd}"][:], [f"qf{dd}"], [])
            for tcl in range(4):
                c = 4 * s + tcl
                if c > 0:
                    buf = f"schb{c % 2}"
                    dma("sync", sb[buf][0:64, :], dram["sch_in"][c - 1], [buf], [buf])
                    dma("sync", sb[buf][64:128, :], dram["sch_in"][c - 1], [buf], [buf])
                    vop(lambda e, buf=buf: e.tensor_tensor(
                        sb["srun"][:], sb["srun"][:], sb[buf][:], OP.add),
                        [buf, "srun"], ["srun"])
                    vop(lambda e: e.tensor_copy(sb["sbf"][:], sb["srun"][:]),
                        ["srun"], ["sbf"])
                if not SCAN_BATCHED:
                    A2COL = 128 if SCAN_SER_COLS else 0
                    for h in range(16):
                        d, base = h // 2, 64 * (h % 2)
                        acol = A2COL * h if SCAN_SER_COLS else 0
                        mm(ps[3][0:128, 0:128],
                           sb[f"kf{d}"][base:base + 64, 128 * c:128 * c + 128],
                           sb[f"qf{d}"][base:base + 64, 128 * tcl:128 * tcl + 128],
                           True, True, [f"kf{d}", f"qf{d}", "ps3"], ["ps3"])
                        vop(lambda e, acol=acol: e.tensor_tensor(
                            sb["a2m"][:, acol:acol + 128],
                            ps[3][0:128, 0:128],
                            sb["triu"][:], OP.mult),
                            ["ps3", "triu"], ["am0"])
                        mm(ps[4][0:128, 0:65], sb["a2m"][:, acol:acol + 128],
                           sb[f"vt{s}"][:, 1040 * tcl + 65 * h:1040 * tcl + 65 * h + 65],
                           True, False, ["am0", f"vt{s}"], ["nm0", "ps4"])
                        mm(ps[4][0:128, 0:65],
                           sb[f"qf{d}"][base:base + 64, 128 * tcl:128 * tcl + 128],
                           sb["sbf"][base:base + 64, 65 * h:65 * h + 65],
                           False, True, [f"qf{d}", "sbf", "nm0"], ["nm0", "ps4"])
                        vop(lambda e: e.activation(sb["dcol"][:, 0:1],
                                                   ps[4][0:128, 64:65], AF.Identity,
                                                   bias=sb["cb"][:, 2:3], scale=1.0),
                            ["nm0", "cb", "ps4"], ["dc0"], eng="scalar", big=False)
                        vop(lambda e: e.reciprocal(sb["dcol"][:, 1:2],
                                                   sb["dcol"][:, 0:1]),
                            ["dc0"], ["dr0"], big=False)
                        vop(lambda e, h=h: e.activation(
                            sb["atm"][:, 64 * h:64 * h + 64], ps[4][0:128, 0:64],
                            AF.Identity, bias=0.0, scale=sb["dcol"][:, 1:2]),
                            ["nm0", "dr0", "ps4"], ["atm"], eng="scalar", big=False)
                if SCAN_BATCHED == 4:
                    banks4 = ["ps2", "ps3", "ps5", "ps6"]
                    for h in range(16):
                        d, base = h // 2, 64 * (h % 2)
                        bk = banks4[h % 4]
                        pb = ps[int(bk[2])]
                        mm(pb[0:128, 0:128],
                           sb[f"kf{d}"][base:base + 64, 128 * c:128 * c + 128],
                           sb[f"qf{d}"][base:base + 64, 128 * tcl:128 * tcl + 128],
                           True, True, [f"kf{d}", f"qf{d}", bk], [bk])
                        vop(lambda e, h=h, pb=pb: e.tensor_tensor(
                            sb["a2m"][:, 128 * h:128 * h + 128],
                            pb[0:128, 0:128], sb["triu"][:], OP.mult),
                            [bk, "triu"], [f"am{h}"])
                    for h in range(16):
                        d, base = h // 2, 64 * (h % 2)
                        nb = "ps4" if h % 2 == 0 else "ps7"
                        nap = ps[int(nb[2])][0:128, 0:65]
                        mm(nap, sb["a2m"][:, 128 * h:128 * h + 128],
                           sb[f"vt{s}"][:, 1040 * tcl + 65 * h:1040 * tcl + 65 * h + 65],
                           True, False, [f"am{h}", f"vt{s}"], [f"nm{h}", nb])
                        mm(nap,
                           sb[f"qf{d}"][base:base + 64, 128 * tcl:128 * tcl + 128],
                           sb["sbf"][base:base + 64, 65 * h:65 * h + 65],
                           False, True, [f"qf{d}", "sbf", f"nm{h}"], [f"nm{h}", nb])
                        vop(lambda e, h=h, nb=nb: e.activation(
                            sb["dcol"][:, 2 * h:2 * h + 1],
                            ps[int(nb[2])][0:128, 64:65], AF.Identity,
                            bias=sb["cb"][:, 2:3], scale=1.0),
                            [f"nm{h}", "cb", nb], [f"dc{h}"], eng="scalar", big=False)
                        vop(lambda e, h=h: e.reciprocal(
                            sb["dcol"][:, 2 * h + 1:2 * h + 2],
                            sb["dcol"][:, 2 * h:2 * h + 1]),
                            [f"dc{h}"], [f"dr{h}"], big=False)
                        vop(lambda e, h=h, nb=nb: e.activation(
                            sb["atm"][:, 64 * h:64 * h + 64],
                            ps[int(nb[2])][0:128, 0:64],
                            AF.Identity, bias=0.0,
                            scale=sb["dcol"][:, 2 * h + 1:2 * h + 2]),
                            [f"nm{h}", f"dr{h}", nb], ["atm"], eng="scalar", big=False)
                A2B = {0: "ps2", 1: "ps3", 2: "ps5", 3: "ps6"}
                if SCAN_BATCHED != 2:
                    A2B = None
                NB = {}
                for h in range(16):
                    if h < 7:
                        NB[h] = ("ps4", 66 * h)
                    elif h < 14:
                        NB[h] = ("ps7", 66 * (h - 7))
                    else:
                        NB[h] = ("ps2", 66 * (h - 14))
                a2ap = {}
                for h in (range(16) if SCAN_BATCHED == 2 else []):
                    d, base = h // 2, 64 * (h % 2)
                    bk = A2B[h // 4]
                    a2ap[h] = ps[int(bk[2])][0:128, 128 * (h % 4):128 * (h % 4) + 128]
                    mm(a2ap[h],
                       sb[f"kf{d}"][base:base + 64, 128 * c:128 * c + 128],
                       sb[f"qf{d}"][base:base + 64, 128 * tcl:128 * tcl + 128],
                       True, True, [f"kf{d}", f"qf{d}", bk], [bk])
                for h in (range(16) if SCAN_BATCHED == 2 else []):
                    bk = A2B[h // 4]
                    vop(lambda e, h=h, ap=a2ap[h]: e.tensor_tensor(
                        sb["a2m"][:, 128 * h:128 * h + 128], ap,
                        sb["triu"][:], OP.mult),
                        [bk, "triu"], [f"am{h}"])
                if SCAN_BATCHED == 1:
                    for h in range(16):
                        d, base = h // 2, 64 * (h % 2)
                        mm(ps[4][0:128, 0:65], sb["a2m"][:, 128 * h:128 * h + 128],
                           sb[f"vt{s}"][:, 1040 * tcl + 65 * h:1040 * tcl + 65 * h + 65],
                           True, False, [f"am{h}", f"vt{s}"], ["nm0", "ps4"])
                        mm(ps[4][0:128, 0:65],
                           sb[f"qf{d}"][base:base + 64, 128 * tcl:128 * tcl + 128],
                           sb["sbf"][base:base + 64, 65 * h:65 * h + 65],
                           False, True, [f"qf{d}", "sbf", "nm0"], ["nm0", "ps4"])
                        vop(lambda e: e.activation(sb["dcol"][:, 0:1],
                                                   ps[4][0:128, 64:65], AF.Identity,
                                                   bias=sb["cb"][:, 2:3], scale=1.0),
                            ["nm0", "cb", "ps4"], ["dc0"], eng="scalar", big=False)
                        vop(lambda e: e.reciprocal(sb["dcol"][:, 1:2],
                                                   sb["dcol"][:, 0:1]),
                            ["dc0"], ["dr0"], big=False)
                        vop(lambda e, h=h: e.activation(
                            sb["atm"][:, 64 * h:64 * h + 64], ps[4][0:128, 0:64],
                            AF.Identity, bias=0.0, scale=sb["dcol"][:, 1:2]),
                            ["nm0", "dr0", "ps4"], ["atm"], eng="scalar", big=False)
                for h in (range(16) if SCAN_BATCHED == 2 else []):
                    d, base = h // 2, 64 * (h % 2)
                    nb, ncol = NB[h]
                    nap = ps[int(nb[2])][0:128, ncol:ncol + 65]
                    mm(nap, sb["a2m"][:, 128 * h:128 * h + 128],
                       sb[f"vt{s}"][:, 1040 * tcl + 65 * h:1040 * tcl + 65 * h + 65],
                       True, False, [f"am{h}", f"vt{s}"], [f"nm{h}", nb])
                    mm(nap,
                       sb[f"qf{d}"][base:base + 64, 128 * tcl:128 * tcl + 128],
                       sb["sbf"][base:base + 64, 65 * h:65 * h + 65],
                       False, True, [f"qf{d}", "sbf", f"nm{h}"], [f"nm{h}", nb])
                for h in (range(16) if SCAN_BATCHED == 2 else []):
                    nb, ncol = NB[h]
                    vop(lambda e, h=h, nb=nb, ncol=ncol: e.activation(
                        sb["dcol"][:, 2 * h:2 * h + 1],
                        ps[int(nb[2])][0:128, ncol + 64:ncol + 65], AF.Identity,
                        bias=sb["cb"][:, 2:3], scale=1.0),
                        [f"nm{h}", "cb", nb], [f"dc{h}"], eng="scalar", big=False)
                    vop(lambda e, h=h: e.reciprocal(
                        sb["dcol"][:, 2 * h + 1:2 * h + 2],
                        sb["dcol"][:, 2 * h:2 * h + 1]),
                        [f"dc{h}"], [f"dr{h}"], big=False)
                for h in (range(16) if SCAN_BATCHED == 2 else []):
                    nb, ncol = NB[h]
                    vop(lambda e, h=h, nb=nb, ncol=ncol: e.activation(
                        sb["atm"][:, 64 * h:64 * h + 64],
                        ps[int(nb[2])][0:128, ncol:ncol + 64],
                        AF.Identity, bias=0.0,
                        scale=sb["dcol"][:, 2 * h + 1:2 * h + 2]),
                        [f"nm{h}", f"dr{h}", nb], ["atm"], eng="scalar", big=False)
                if DEBUG and s == 0 and tcl == 0:
                    dma("sync", dram["dbg_atm"], sb["atm"][:], ["atm"], [])
                for j in range(8):
                    bk = "ps3" if j % 2 == 0 else "ps5"
                    pb = ps[int(bk[2])]
                    r.op("tensor", lambda e, j=j, pb=pb: e.transpose(
                        pb[0:128, 0:128], sb["atm"][:, 128 * j:128 * j + 128],
                        sb["idf"][:]),
                        reads=["atm", "idf", bk] + [f"am{h}" for h in range(16)] if j == 0 else ["atm", "idf", bk],
                        writes=[bk])
                    vop(lambda e, j=j, tcl=tcl, pb=pb: e.tensor_copy(
                        sb[f"qs{j}"][:, 128 * tcl:128 * tcl + 128],
                        pb[0:128, 0:128]),
                        [bk], [f"qs{j}"])
            for mc in range(8):
                pb = ps[mc % 2]
                for kc in range(8):
                    mm(pb[0:128, 0:512],
                       sb["Wb"][:, 1024 * kc + 128 * mc:1024 * kc + 128 * mc + 128],
                       sb[f"qs{kc}"][:, 0:512], kc == 0, kc == 7,
                       ["Wb", f"qs{kc}", f"ps{mc % 2}"], [f"ps{mc % 2}"])
                vop(lambda e, pb=pb, mc=mc: e.scalar_tensor_tensor(
                    sb[f"hy{mc}"][:], pb[0:128, 0:512], sb["bo"][:, mc:mc + 1],
                    sb[f"xsb{mc}"][:], OP.add, OP.add),
                    [f"ps{mc % 2}", "bo", f"xsb{mc}"], [f"hy{mc}"])
                dma("scalar", dram["rmid"][s, mc], sb[f"hy{mc}"][:],
                    [f"hy{mc}"], ["rmid"])
                if DEBUG and s == 0:
                    dma("sync", dram["dbg_hy"][mc], sb[f"hy{mc}"][:],
                        [f"hy{mc}"], [])

        def ln_stats(src_tensors, src_tokens):
            for kc in range(8):
                vop(lambda e, kc=kc: e.tensor_copy(sb[f"xbf{kc}"][:, 0:512],
                                                   src_tensors[kc][:, 0:512]),
                    [src_tokens[kc]], [f"xbf{kc}"], eng="gpsimd")
            for kc in range(8):
                mm(ps[6][0:1, 0:512], sb["onescb"][:, 0:1],
                   sb[f"xbf{kc}"][:, 0:512], kc == 0, kc == 7,
                   ["onescb", f"xbf{kc}", "ps6"], ["ps6"])
            for kc in range(8):
                sq = f"sq{kc % 2}"
                vop(lambda e, kc=kc, sq=sq: e.activation(
                    sb[sq][:, 0:512], sb[f"xbf{kc}"][:, 0:512], AF.Square),
                    [f"xbf{kc}"], [sq], eng="scalar")
                mm(ps[7][0:1, 0:512], sb["onescb"][:, 0:1],
                   sb[sq][:, 0:512], kc == 0, kc == 7,
                   ["onescb", sq, "ps7"], ["ps7"])
            vop(lambda e: e.tensor_scalar(sb["rowb"][0:1, :], ps[6][0:1, 0:512],
                                          1.0 / 1024.0, None, OP.mult),
                ["ps6"], ["rowb"])
            vop(lambda e: e.tensor_scalar(sb["rowb2"][0:1, :], ps[7][0:1, 0:512],
                                          1.0 / 1024.0, None, OP.mult),
                ["ps7"], ["rowb2"])
            vop(lambda e: e.tensor_tensor(sb["Ftile"][0:1, 0:512],
                                          sb["rowb"][0:1, :], sb["rowb"][0:1, :],
                                          OP.mult),
                ["rowb"], ["Ftile"])
            vop(lambda e: e.tensor_tensor(sb["rowb2"][0:1, :], sb["rowb2"][0:1, :],
                                          sb["Ftile"][0:1, 0:512], OP.subtract),
                ["rowb2", "Ftile"], ["rowb2"])
            vop(lambda e: e.activation(sb["rowb2"][0:1, :], sb["rowb2"][0:1, :],
                                       AF.Sqrt, bias=sb["cb"][0:1, 1:2], scale=1.0),
                ["rowb2", "cb"], ["rowb2"], eng="scalar")
            vop(lambda e: e.reciprocal(sb["rowb2"][0:1, :], sb["rowb2"][0:1, :]),
                ["rowb2"], ["rowb2"])
            vop(lambda e: e.tensor_tensor(sb["rowb"][0:1, :], sb["rowb"][0:1, :],
                                          sb["rowb2"][0:1, :], OP.mult),
                ["rowb", "rowb2"], ["rowb"])
            mm(ps[6][0:128, 0:512], sb["ones1f"][0:1, 0:128], sb["rowb2"][0:1, :],
               True, True, ["ones1f", "rowb2", "ps6"], ["ps6"])
            mm(ps[7][0:128, 0:512], sb["ones1f"][0:1, 0:128], sb["rowb"][0:1, :],
               True, True, ["ones1f", "rowb", "ps7"], ["ps7"])

        for s in range(NSLAB):
            for kc in range(8):
                dma("sync", sb[f"xsb{kc}"][:], dram["rmid"][s, kc],
                    [f"xsb{kc}", "rmid"], [f"xsb{kc}"])
            ln_stats([sb[f"xsb{kc}"] for kc in range(8)],
                     [f"xsb{kc}" for kc in range(8)])
            for kc in range(8):
                vop(lambda e, kc=kc: e.tensor_tensor(sb["t1"][:], sb[f"xsb{kc}"][:],
                                                     ps[6][0:128, 0:512], OP.mult),
                    [f"xsb{kc}", "ps6"], ["t1"])
                vop(lambda e: e.tensor_tensor(sb["t1"][:], sb["t1"][:],
                                              ps[7][0:128, 0:512], OP.subtract),
                    ["t1", "ps7"], ["t1"])
                vop(lambda e, kc=kc, s=s: e.tensor_scalar(
                    sb[f"kf{kc}"][:, 512 * s:512 * s + 512], sb["t1"][:],
                    sb["l1g"][:, kc:kc + 1], sb["l1b"][:, kc:kc + 1],
                    OP.mult, OP.add),
                    ["t1", "l1g", "l1b"], [f"kf{kc}"])
        for g in range(4):
            load_w_slice("Wa", "w1_b", 0, 1024, 1024 * g, 1024 * g + 1024)
            load_w_slice("Wb", "w2_b", 1024 * g, 1024 * g + 1024, 0, 1024)
            for s in range(NSLAB):
                for mc in range(8):
                    pb = ps[mc % 2]
                    for kc in range(8):
                        mm(pb[0:128, 0:512],
                           sb["Wa"][:, 1024 * kc + 128 * mc:1024 * kc + 128 * mc + 128],
                           sb[f"kf{kc}"][:, 512 * s:512 * s + 512], kc == 0, kc == 7,
                           ["Wa", f"kf{kc}", f"ps{mc % 2}"], [f"ps{mc % 2}"])
                    vop(lambda e, pb=pb, g=g, mc=mc: e.activation(
                        sb[f"qs{mc}"][:, 0:512], pb[0:128, 0:512], AF.Relu,
                        bias=sb["bf1"][:, 8 * g + mc:8 * g + mc + 1], scale=1.0),
                        [f"ps{mc % 2}", "bf1"], [f"qs{mc}"], eng="scalar")
                for mc2 in range(8):
                    pb = ps[2 + mc2 % 2]
                    for mc in range(8):
                        mm(pb[0:128, 0:512],
                           sb["Wb"][:, 1024 * mc + 128 * mc2:1024 * mc + 128 * mc2 + 128],
                           sb[f"qs{mc}"][:, 0:512], mc == 0, mc == 7,
                           ["Wb", f"qs{mc}", f"ps{2 + mc2 % 2}"], [f"ps{2 + mc2 % 2}"])
                    yt = f"vt{mc2 // 2}"
                    ycols = slice(2080 * (mc2 % 2) + 520 * 0 + 512 * s,
                                  2080 * (mc2 % 2) + 512 * s + 512)
                    if g == 0:
                        vop(lambda e, pb=pb, yt=yt, yc=ycols: e.tensor_copy(
                            sb[yt][:, yc], pb[0:128, 0:512]),
                            [f"ps{2 + mc2 % 2}"], [yt])
                    else:
                        vop(lambda e, pb=pb, yt=yt, yc=ycols: e.tensor_tensor(
                            sb[yt][:, yc], sb[yt][:, yc], pb[0:128, 0:512], OP.add),
                            [f"ps{2 + mc2 % 2}", yt], [yt])
        for s in range(NSLAB):
            for kc in range(8):
                dma("sync", sb[f"xsb{kc}"][:], dram["rmid"][s, kc],
                    [f"xsb{kc}", "rmid"], [f"xsb{kc}"])
            ln_stats([sb[f"xsb{kc}"] for kc in range(8)],
                     [f"xsb{kc}" for kc in range(8)])
            for kc in range(8):
                vop(lambda e, kc=kc: e.tensor_tensor(sb["t1"][:], sb[f"xsb{kc}"][:],
                                                     ps[6][0:128, 0:512], OP.mult),
                    [f"xsb{kc}", "ps6"], ["t1"])
                vop(lambda e: e.tensor_tensor(sb["t1"][:], sb["t1"][:],
                                              ps[7][0:128, 0:512], OP.subtract),
                    ["t1", "ps7"], ["t1"])
                vop(lambda e, kc=kc: e.tensor_scalar(
                    sb["t1"][:], sb["t1"][:], sb["l1g"][:, kc:kc + 1],
                    sb["bsum"][:, kc:kc + 1], OP.mult, OP.add),
                    ["t1", "l1g", "bsum"], ["t1"])
                yt = f"vt{kc // 2}"
                ycols = slice(2080 * (kc % 2) + 512 * s, 2080 * (kc % 2) + 512 * s + 512)
                vop(lambda e, yt=yt, yc=ycols, kc=kc: e.tensor_tensor(
                    sb[f"hy{kc}"][:], sb["t1"][:], sb[yt][:, yc], OP.add),
                    ["t1", yt], [f"hy{kc}"])
            ln_stats([sb[f"hy{kc}"] for kc in range(8)],
                     [f"hy{kc}" for kc in range(8)])
            for kc in range(8):
                vop(lambda e, kc=kc: e.tensor_tensor(sb["t1"][:], sb[f"hy{kc}"][:],
                                                     ps[6][0:128, 0:512], OP.mult),
                    [f"hy{kc}", "ps6"], ["t1"])
                vop(lambda e: e.tensor_tensor(sb["t1"][:], sb["t1"][:],
                                              ps[7][0:128, 0:512], OP.subtract),
                    ["t1", "ps7"], ["t1"])
                vop(lambda e, kc=kc: e.tensor_scalar(
                    sb[f"xsb{kc}"][:], sb["t1"][:], sb["l2g"][:, kc:kc + 1],
                    sb["l2b"][:, kc:kc + 1], OP.mult, OP.add),
                    ["t1", "l2g", "l2b"], [f"xsb{kc}"])
                dma("scalar", dram["xmid"][s, kc], sb[f"xsb{kc}"][:],
                    [f"xsb{kc}"], ["xmid"])
                if has_a:
                    dma("scalar", dram["xout"][kc, s], sb[f"xsb{kc}"][:],
                        [f"xsb{kc}"], [])

    def rhs_prep():
        for kc in range(8):
            dma("sync", sb[f"rhsf{kc}"][:], dram["rhs"][128 * kc:128 * kc + 128, :],
                [f"kf{kc}"], [f"kf{kc}"])
        for nc2 in range(8):
            for kc in range(8):
                r.op("tensor", lambda e, kc=kc, nc2=nc2: e.transpose(
                    ps[5][0:128, 0:128],
                    sb[f"rhsf{kc}"][:, 128 * nc2:128 * nc2 + 128], sb["idf"][:]),
                    reads=[f"kf{kc}", "idf", "ps5"], writes=["ps5"])
                vop(lambda e, kc=kc: e.tensor_copy(
                    sb["etm"][:, 128 * kc:128 * kc + 128], ps[5][0:128, 0:128]),
                    ["ps5"], ["etm"])
            vop(lambda e: e.tensor_reduce(sb["st8"][:, 0:1], sb["etm"][:],
                                          axis=AX.X, op=OP.max,
                                          apply_absolute_value=True),
                ["etm"], ["st8"], big=False)
            vop(lambda e: e.tensor_scalar(sb["st8"][:, 1:2], sb["st8"][:, 0:1],
                                          1e-12, INV127, OP.max, OP.mult),
                ["st8"], ["st8"], big=False)
            vop(lambda e, nc2=nc2: e.tensor_copy(sb["scall"][:, nc2:nc2 + 1],
                                                 sb["st8"][:, 1:2]),
                ["st8"], ["scall"], big=False)
            vop(lambda e, nc2=nc2: e.reciprocal(sb["rcall"][:, nc2:nc2 + 1],
                                                sb["st8"][:, 1:2]),
                ["st8"], ["rcall"], big=False)
        for which, src, dst in (("s", "scall", "scolbc"), ("r", "rcall", "rcolbc")):
            for nc2 in range(8):
                mm(ps[5][0:1, 0:128], sb[src][:, nc2:nc2 + 1], sb["idf"][:],
                   True, True, [src, "idf", "ps5"], ["ps5"])
                vop(lambda e: e.tensor_copy(sb["rowb"][0:1, 0:128], ps[5][0:1, 0:128]),
                    ["ps5"], ["rowb"])
                mm(ps[6][0:128, 0:128], sb["ones1f"][0:1, 0:128],
                   sb["rowb"][0:1, 0:128], True, True,
                   ["ones1f", "rowb", "ps6"], ["ps6"])
                vop(lambda e, dst=dst, nc2=nc2: e.tensor_copy(
                    sb[dst][:, 128 * nc2:128 * nc2 + 128], ps[6][0:128, 0:128]),
                    ["ps6"], [dst])
        for kc in range(8):
            vop(lambda e, kc=kc: e.tensor_tensor(sb["atm"][:], sb[f"rhsf{kc}"][:],
                                                 sb["rcolbc"][:], OP.mult),
                [f"kf{kc}", "rcolbc"], ["atm"])
            vop(lambda e: e.tensor_copy(sb["qi"][:], sb["atm"][:]), ["atm"], ["qi"])
            vop(lambda e, kc=kc: e.tensor_copy(sb[f"qrhs{kc}"][:], sb["qi"][:]),
                ["qi"], [f"qrhs{kc}"])

    def ln_plain_tm(src_key, out_key, out_tensor, src_tensor):
        vop(lambda e: e.tensor_reduce(sb["st8"][:, 2:3], src_tensor[:],
                                      axis=AX.X, op=OP.add),
            [src_key], ["st8"])
        vop(lambda e: e.activation(sb["atm"][:], src_tensor[:], AF.Square),
            [src_key], ["atm"], eng="scalar")
        vop(lambda e: e.tensor_reduce(sb["st8"][:, 3:4], sb["atm"][:],
                                      axis=AX.X, op=OP.add),
            ["atm"], ["st8"], big=False)
        vop(lambda e: e.tensor_scalar(sb["st8"][:, 4:5], sb["st8"][:, 2:3],
                                      1.0 / 1024.0, None, OP.mult),
            ["st8"], ["st8"], big=False)
        vop(lambda e: e.tensor_scalar(sb["st8"][:, 5:6], sb["st8"][:, 3:4],
                                      1.0 / 1024.0, None, OP.mult),
            ["st8"], ["st8"], big=False)
        vop(lambda e: e.tensor_tensor(sb["st8"][:, 6:7], sb["st8"][:, 4:5],
                                      sb["st8"][:, 4:5], OP.mult),
            ["st8"], ["st8"], big=False)
        vop(lambda e: e.tensor_tensor(sb["st8"][:, 6:7], sb["st8"][:, 5:6],
                                      sb["st8"][:, 6:7], OP.subtract),
            ["st8"], ["st8"], big=False)
        vop(lambda e: e.activation(sb["st8"][:, 7:8], sb["st8"][:, 6:7],
                                   AF.Sqrt, bias=sb["cb"][:, 1:2], scale=1.0),
            ["st8", "cb"], ["st8"], eng="scalar", big=False)
        vop(lambda e: e.reciprocal(sb["st8"][:, 8:9], sb["st8"][:, 7:8]),
            ["st8"], ["st8"], big=False)
        vop(lambda e: e.tensor_scalar(out_tensor[:], src_tensor[:],
                                      sb["st8"][:, 4:5], sb["st8"][:, 8:9],
                                      OP.subtract, OP.mult),
            [src_key, "st8"], [out_key])

    def final_block():
        for tc in range(16):
            s, tcl = tc // 4, tc % 4
            for kc in range(8):
                dma("sync", sb["atm"][:, 128 * kc:128 * kc + 128],
                    dram["xmid"][s, kc][:, 128 * tcl:128 * tcl + 128],
                    ["atm", "xmid"], ["atm"])
            dma("sync", sb["x0t"][:], dram["x0"][128 * tc:128 * tc + 128, :],
                ["x0t"], ["x0t"])
            for kc in range(8):
                r.op("tensor", lambda e, kc=kc: e.transpose(
                    ps[5][0:128, 0:128], sb["atm"][:, 128 * kc:128 * kc + 128],
                    sb["idf"][:]),
                    reads=["atm", "idf", "ps5"], writes=["ps5"])
                vop(lambda e, kc=kc: e.tensor_tensor(
                    sb["etm"][:, 128 * kc:128 * kc + 128], ps[5][0:128, 0:128],
                    sb["x0t"][:, 128 * kc:128 * kc + 128], OP.add),
                    ["ps5", "x0t"], ["etm"])
            ln_plain_tm("etm", "x1t", sb["x1t"], sb["etm"])
            vop(lambda e: e.tensor_reduce(sb["st8"][:, 0:1], sb["x1t"][:],
                                          axis=AX.X, op=OP.max,
                                          apply_absolute_value=True),
                ["x1t"], ["st8"], big=False)
            vop(lambda e: e.tensor_scalar(sb["st8"][:, 1:2], sb["st8"][:, 0:1],
                                          1e-12, INV127, OP.max, OP.mult),
                ["st8"], ["st8"], big=False)
            vop(lambda e: e.reciprocal(sb["st8"][:, 9:10], sb["st8"][:, 1:2]),
                ["st8"], ["st8"], big=False)
            vop(lambda e: e.tensor_scalar(sb["atm"][:], sb["x1t"][:],
                                          sb["st8"][:, 9:10], None, OP.mult),
                ["x1t", "st8"], ["atm"])
            vop(lambda e: e.tensor_copy(sb["qi"][:], sb["atm"][:]), ["atm"], ["qi"])
            vop(lambda e: e.tensor_copy(sb["atm"][:], sb["qi"][:]), ["qi"], ["atm"])
            for j in range(8):
                r.op("tensor", lambda e, j=j: e.transpose(
                    ps[5][0:128, 0:128], sb["atm"][:, 128 * j:128 * j + 128],
                    sb["idf"][:]),
                    reads=["atm", "idf", "ps5"], writes=["ps5"])
                vop(lambda e, j=j: e.tensor_copy(
                    sb["qlf"][:, 128 * j:128 * j + 128], ps[5][0:128, 0:128]),
                    ["ps5"], ["qlf"])
            for n2 in range(2):
                pb = ps[n2]
                for kc in range(8):
                    mm(pb[0:128, 0:512], sb["qlf"][:, 128 * kc:128 * kc + 128],
                       sb[f"qrhs{kc}"][:, 512 * n2:512 * n2 + 512], kc == 0, kc == 7,
                       ["qlf", f"qrhs{kc}", f"ps{n2}"], [f"ps{n2}"])
                vop(lambda e, pb=pb, n2=n2: e.scalar_tensor_tensor(
                    sb[f"hy{n2}"][:], pb[0:128, 0:512], sb["st8"][:, 1:2],
                    sb["scolbc"][:, 512 * n2:512 * n2 + 512], OP.mult, OP.mult),
                    [f"ps{n2}", "st8", "scolbc"], [f"hy{n2}"])
                vop(lambda e, n2=n2: e.tensor_tensor(
                    sb[f"hy{n2}"][:], sb[f"hy{n2}"][:],
                    sb["x1t"][:, 512 * n2:512 * n2 + 512], OP.add),
                    [f"hy{n2}", "x1t"], [f"hy{n2}"])
            vop(lambda e: e.tensor_reduce(sb["st8"][:, 2:3], sb["hy0"][:],
                                          axis=AX.X, op=OP.add),
                ["hy0"], ["st8"], big=False)
            vop(lambda e: e.tensor_reduce(sb["st8"][:, 3:4], sb["hy1"][:],
                                          axis=AX.X, op=OP.add),
                ["hy1"], ["st8"], big=False)
            vop(lambda e: e.tensor_tensor(sb["st8"][:, 2:3], sb["st8"][:, 2:3],
                                          sb["st8"][:, 3:4], OP.add),
                ["st8"], ["st8"], big=False)
            vop(lambda e: e.activation(sb["atm"][:, 0:512], sb["hy0"][:], AF.Square),
                ["hy0"], ["atm"], eng="scalar")
            vop(lambda e: e.activation(sb["atm"][:, 512:1024], sb["hy1"][:], AF.Square),
                ["hy1"], ["atm"], eng="scalar")
            vop(lambda e: e.tensor_reduce(sb["st8"][:, 3:4], sb["atm"][:],
                                          axis=AX.X, op=OP.add),
                ["atm"], ["st8"], big=False)
            vop(lambda e: e.tensor_scalar(sb["st8"][:, 4:5], sb["st8"][:, 2:3],
                                          1.0 / 1024.0, None, OP.mult),
                ["st8"], ["st8"], big=False)
            vop(lambda e: e.tensor_scalar(sb["st8"][:, 5:6], sb["st8"][:, 3:4],
                                          1.0 / 1024.0, None, OP.mult),
                ["st8"], ["st8"], big=False)
            vop(lambda e: e.tensor_tensor(sb["st8"][:, 6:7], sb["st8"][:, 4:5],
                                          sb["st8"][:, 4:5], OP.mult),
                ["st8"], ["st8"], big=False)
            vop(lambda e: e.tensor_tensor(sb["st8"][:, 6:7], sb["st8"][:, 5:6],
                                          sb["st8"][:, 6:7], OP.subtract),
                ["st8"], ["st8"], big=False)
            vop(lambda e: e.activation(sb["st8"][:, 7:8], sb["st8"][:, 6:7],
                                       AF.Sqrt, bias=sb["cb"][:, 1:2], scale=1.0),
                ["st8", "cb"], ["st8"], eng="scalar", big=False)
            vop(lambda e: e.reciprocal(sb["st8"][:, 8:9], sb["st8"][:, 7:8]),
                ["st8"], ["st8"], big=False)
            for n2 in range(2):
                vop(lambda e, n2=n2: e.tensor_scalar(
                    sb["x1t"][:, 512 * n2:512 * n2 + 512], sb[f"hy{n2}"][:],
                    sb["st8"][:, 4:5], sb["st8"][:, 8:9], OP.subtract, OP.mult),
                    [f"hy{n2}", "st8"], ["x1t"])
            dma("scalar", dram["out"][128 * tc:128 * tc + 128, :], sb["x1t"][:],
                ["x1t"], [])

    if mode == "DBGC":
        vop(lambda e: e.memset(sb["srun"][:], 0.0), [], ["srun"])
        for j in range(8):
            dma("sync", sb[f"kf{j}"][0:64, 0:1040], dram["gst"][j],
                [f"kf{j}"], [f"kf{j}"])
            dma("sync", sb[f"kf{j}"][64:128, 0:1040], dram["gst"][j],
                [f"kf{j}"], [f"kf{j}"])
        for j in range(8):
            vop(lambda e, j=j: e.scalar_tensor_tensor(
                sb["srun"][:], sb[f"kf{j}"][:, 0:1040], sb["pm"][:, j:j + 1],
                sb["srun"][:], OP.mult, OP.add),
                [f"kf{j}", "pm", "srun"], ["srun"])
        vop(lambda e: e.tensor_copy(sb["sbf"][:], sb["srun"][:]), ["srun"], ["sbf"])
        dma("sync", dram["dbgc_sbf"], sb["sbf"][:], ["sbf"], [])
        dma("sync", dram["dbgc_kf0"], sb["kf0"][:, 0:1040], ["kf0"], [])
    if has_b and mode != "DBGC":
        phase_b()
    if has_a:
        phase_a(la)
    if has_f:
        r.barrier()
        rhs_prep()
        final_block()

    sems = {}
    from contextlib import ExitStack
    stk = ExitStack()
    for e in ENGINES:
        sems[e] = stk.enter_context(nc.semaphore(f"sem_{e}"))
    block = stk.enter_context(nc.Block())
    r.emit(block, sems)
    stk.close()
    nc._rec_counts = dict(r.cnt)
    return nc


def _consts():
    idf = np.eye(128, dtype=np.float32)
    idb = np.eye(128).astype(ml_dtypes.bfloat16)
    tk, tq = np.meshgrid(np.arange(128), np.arange(128), indexing="ij")
    triu = (tq >= tk).astype(np.float32)
    e16 = np.zeros((128, 128), np.float32)
    for c in range(8):
        e16[0:64, 16 * c + 2 * c] = 1.0
        e16[64:128, 16 * c + 2 * c + 1] = 1.0
    sel = np.zeros((16, 1024), np.float32)
    for d in range(8):
        sel[2 * d, 128 * d:128 * d + 64] = 1.0
        sel[2 * d + 1, 128 * d + 64:128 * d + 128] = 1.0
    ones1f = np.ones((1, 128), np.float32)
    return dict(idf=idf, idb=idb, triu=triu,
                e16=e16.astype(ml_dtypes.bfloat16),
                sel=sel.astype(ml_dtypes.bfloat16), ones1f=ones1f)


def _fm8(v):
    return np.ascontiguousarray(np.asarray(v).reshape(8, 128).T)


def _fm32(v):
    return np.ascontiguousarray(np.asarray(v).reshape(32, 128).T)


def kernel(L0_attn_x, L0_attn_Wq, L0_attn_bq, L0_attn_Wk, L0_attn_bk,
           L0_attn_Wv, L0_attn_bv, L0_attn_Wo, L0_attn_bo, L0_attn_omega,
           L0_attn_ln1_g, L0_attn_ln1_b, L0_attn_ln2_g, L0_attn_ln2_b,
           L0_attn_W1, L0_attn_bf1, L0_attn_W2, L0_attn_bf2, L0_ffn_rhs):
    x = np.ascontiguousarray(np.asarray(L0_attn_x, dtype=np.float32))
    Wq = np.asarray(L0_attn_Wq, np.float32)
    Wk = np.asarray(L0_attn_Wk, np.float32)
    Wv = np.asarray(L0_attn_Wv, np.float32)
    Wo = np.asarray(L0_attn_Wo, np.float32)
    W1 = np.asarray(L0_attn_W1, np.float32)
    W2 = np.asarray(L0_attn_W2, np.float32)
    bq = np.asarray(L0_attn_bq, np.float32)
    bk = np.asarray(L0_attn_bk, np.float32)
    bv = np.asarray(L0_attn_bv, np.float32)
    bo = np.asarray(L0_attn_bo, np.float32)
    bf1 = np.asarray(L0_attn_bf1, np.float32)
    bf2 = np.asarray(L0_attn_bf2, np.float32)
    l1g = np.asarray(L0_attn_ln1_g, np.float32)
    l1b = np.asarray(L0_attn_ln1_b, np.float32)
    l2g = np.asarray(L0_attn_ln2_g, np.float32)
    l2b = np.asarray(L0_attn_ln2_b, np.float32)
    om = np.asarray(L0_attn_omega, np.float32)
    rhs = np.ascontiguousarray(np.asarray(L0_ffn_rhs, np.float32))

    consts = _consts()
    x0s = [np.ascontiguousarray(x[c // 4, 2048 * (c % 4):2048 * (c % 4) + 2048, :])
           for c in range(NCORE)]
    pmasks = []
    for c in range(NCORE):
        g0 = (c // 4) * 4
        m = np.zeros(8, np.float32)
        m[g0:c] = 1.0
        pmasks.append(np.broadcast_to(m, (128, 8)).copy())

    def layer_vecs(l):
        return dict(
            bq_b=_fm8(bq[l]), bo_b=_fm8(bo[l]), ln1g_b=_fm8(l1g[l]),
            ln1b_b=_fm8(l1b[l]), ln2g_b=_fm8(l2g[l]), ln2b_b=_fm8(l2b[l]),
            bf1_b=_fm32(bf1[l]), bf2_b=_fm8(bf2[l]),
            wq_b=np.ascontiguousarray(Wq[l]), wo_b=np.ascontiguousarray(Wo[l]),
            w1_b=np.ascontiguousarray(W1[l]), w2_b=np.ascontiguousarray(W2[l]),
            om_b=np.ascontiguousarray(np.vstack([om[l], om[l]])))

    def layer_a_in(l):
        return dict(
            wk_a=np.ascontiguousarray(Wk[l]), wv_a=np.ascontiguousarray(Wv[l]),
            bk_a=_fm8(bk[l]), bv_a=np.ascontiguousarray(bv[l]).reshape(1, D),
            om_a=np.ascontiguousarray(np.vstack([om[l], om[l]])))

    state = [dict() for _ in range(NCORE)]

    nc0 = build_launch("A", 0, None)
    in_maps = []
    for c in range(NCORE):
        m = dict(consts)
        m.update(layer_a_in(0))
        m["x0"] = x0s[c]
        in_maps.append(m)
    res = run_bass_kernel_spmd(nc0, in_maps, core_ids=list(range(NCORE)))
    for c in range(NCORE):
        state[c] = {k: res.results[c][k]
                    for k in ("xout", "kf_out", "vt_out", "sch_out", "stot")}

    for k in (1, 2, 3):
        lb = k - 1
        la = k if k < 3 else None
        mode = "BA" if k < 3 else "BF"
        nck = build_launch(mode, la, lb)
        gst = np.stack([state[c]["stot"] for c in range(NCORE)])
        in_maps = []
        for c in range(NCORE):
            m = dict(consts)
            m.update(layer_vecs(lb))
            m["xin"] = state[c]["xout"]
            m["kf_in"] = state[c]["kf_out"]
            m["vt_in"] = state[c]["vt_out"]
            m["sch_in"] = state[c]["sch_out"]
            m["gst"] = gst
            m["pmask"] = pmasks[c]
            if mode == "BA":
                m.update(layer_a_in(la))
            else:
                m["x0"] = x0s[c]
                m["rhs"] = rhs
            in_maps.append(m)
        res = run_bass_kernel_spmd(nck, in_maps, core_ids=list(range(NCORE)))
        if mode == "BA":
            ns = []
            for c in range(NCORE):
                ns.append({kk: res.results[c][kk]
                           for kk in ("xout", "kf_out", "vt_out", "sch_out", "stot")})
            state = ns
        else:
            outs = [res.results[c]["out"] for c in range(NCORE)]

    full = np.empty((2, 8192, D), np.float32)
    for c in range(NCORE):
        full[c // 4, 2048 * (c % 4):2048 * (c % 4) + 2048, :] = outs[c]
    return full


# version 09
# speedup vs baseline: 1.2752x; 1.0477x over previous
import numpy as np
import ml_dtypes
import concourse.bass as bass
import concourse.mybir as mybir
from concourse.bass_utils import run_bass_kernel_spmd

F32 = mybir.dt.float32
BF16 = mybir.dt.bfloat16
I32 = mybir.dt.int32
AF = mybir.ActivationFunctionType
OP = mybir.AluOpType
AX = mybir.AxisListType

NCORE = 8
T = 2048
D = 1024
H = 16
FF = 4096
NL = 3
C = 128
NCH = 16
SLAB = 512
NSLAB = 4
CQ = 0.0625
LN8 = 2.0794415416798357
CS = 0.35355339059327373
LN_EPS = 1e-5
ATTN_EPS = 1e-6
INV127 = 1.0 / 127.0

ENGINES = ("tensor", "vector", "scalar", "gpsimd", "sync", "gdma", "sdma")
QUEUE_LABELS = {"tensor": ("tensor",), "vector": ("vector",),
                "scalar": ("scalar", "sdma"), "gpsimd": ("gpsimd", "gdma"),
                "sync": ("sync",)}
DEBUG = False
SCAN_BATCHED = 4
SCAN_SER_COLS = False


class Rec:

    SPACED = ("vector", "gpsimd", "scalar")

    def __init__(self):
        self.items = []
        self.cnt = {e: 0 for e in ENGINES}
        self.opi = {e: 0 for e in ENGINES}
        self.writer = {}
        self.readers = {}

    def barrier(self):
        snap = dict(self.cnt)
        self._pend = {e: {d: v for d, v in snap.items() if d != e and v > 0}
                      for e in ENGINES}

    def op(self, eng, fn, reads=(), writes=(), dma=False, big=False):
        inc = 16 if dma else 1
        waits = {}
        pend = getattr(self, "_pend", None)
        if pend and pend.get(eng):
            waits.update(pend.pop(eng))

        def need(dep):
            deng, dcnt = dep[0], dep[1]
            dep_dma = dep[3] if len(dep) > 3 else False
            if deng == eng and not dep_dma and eng in ("tensor", "sync"):
                return
            if (eng == "gdma" and deng == "gpsimd") or (eng == "sdma" and deng == "scalar"):
                return
            if waits.get(deng, 0) < dcnt:
                waits[deng] = dcnt

        for t in reads:
            w = self.writer.get(t)
            if w:
                need(w)
        for t in writes:
            w = self.writer.get(t)
            if w:
                need(w)
            for rd in self.readers.get(t, ()):
                need(rd)
        self.cnt[eng] += inc
        self.opi[eng] += 1
        here = (eng, self.cnt[eng], self.opi[eng], dma, big or dma)
        for t in reads:
            self.readers.setdefault(t, []).append(here)
        for t in writes:
            self.writer[t] = here
            self.readers[t] = []
        self.items.append((eng, fn, waits, inc))

    def emit(self, block, sems):
        finals = dict(self.cnt)
        fence = {e: set() for e in ENGINES}
        for eng, fn, waits, inc in self.items:
            for d, v in waits.items():
                fence[d].add(v)

        def mk(qname, items):
            cnt = {l: 0 for l in QUEUE_LABELS[qname]}
            selfhw = {l: 0 for l in QUEUE_LABELS[qname]}
            fi = {l: 0 for l in QUEUE_LABELS[qname]}
            fv = {l: sorted(fence[l]) for l in QUEUE_LABELS[qname]}
            dma_lbl = [l for l in QUEUE_LABELS[qname] if l in ("sync", "gdma", "sdma")]
            hwq = {d: 0 for d in ENGINES}

            def body(eng):
                for lbl, fn, waits, inc in items:
                    if lbl in dma_lbl or inc == 16:
                        fmax = 0
                        while fi[lbl] < len(fv[lbl]) and fv[lbl][fi[lbl]] <= cnt[lbl]:
                            fmax = fv[lbl][fi[lbl]]
                            fi[lbl] += 1
                        if fmax > selfhw[lbl]:
                            eng.wait_ge(sems[lbl], fmax)
                            selfhw[lbl] = fmax
                    for d, v in sorted(waits.items()):
                        if hwq[d] < v:
                            eng.wait_ge(sems[d], v)
                            hwq[d] = v
                    fn(eng).then_inc(sems[lbl], inc)
                    cnt[lbl] += inc
                for d in ENGINES:
                    if d not in QUEUE_LABELS[qname] and finals[d] > 0 and hwq[d] < finals[d]:
                        eng.wait_ge(sems[d], finals[d])
                for l in QUEUE_LABELS[qname]:
                    if finals[l] > cnt[l] or (finals[l] > selfhw[l] and l in dma_lbl):
                        eng.wait_ge(sems[l], finals[l])
            return body

        for qname in QUEUE_LABELS:
            qitems = [(l, fn, w, inc) for (l, fn, w, inc) in
                      [(e, f, w, i) for (e, f, w, i) in self.items]
                      if l in QUEUE_LABELS[qname]]
            getattr(block, qname)(mk(qname, qitems))


SBUF_BASE = 16512


class Alloc:
    def __init__(self, nc):
        self.nc = nc
        self.off = SBUF_BASE
        self.offs = {}

    def at(self, name, shape, dtype, offset):
        return self.nc.alloc_sbuf_tensor_at(name, shape, dtype, offset=offset)

    def buf(self, name, shape, dtype):
        per = int(np.prod(shape[1:])) * mybir.dt.size(dtype)
        per = (per + 31) // 32 * 32
        t = self.nc.alloc_sbuf_tensor_at(name, shape, dtype, offset=self.off)
        self.offs[name] = self.off
        self.off += per
        return t

    def mark(self):
        return self.off


def ceil32(x):
    return (x + 31) // 32 * 32


def build_launch(mode, la, lb):
    nc = bass.Bass("TRN2", target_bir_lowering=False, debug=False)
    r = Rec()

    dram = {}

    def din(name, shape, dtype):
        dram[name] = nc.dram_tensor(name, shape, dtype, kind="ExternalInput").ap()
        return dram[name]

    def dout(name, shape, dtype):
        dram[name] = nc.dram_tensor(name, shape, dtype, kind="ExternalOutput").ap()
        return dram[name]

    din("idf", [128, 128], F32)
    din("idb", [128, 128], BF16)
    din("triu", [128, 128], F32)
    din("e16", [128, 128], BF16)
    din("sel", [16, 1024], BF16)
    din("ones1f", [1, 128], F32)

    has_a = mode in ("A", "BA")
    has_b = mode in ("BA", "BF", "DBGC")
    has_f = mode == "BF"
    if mode == "DBGC":
        has_a = False

    if mode == "A":
        din("x0", [T, D], F32)
    else:
        din("xin", [8, NSLAB, 128, SLAB], F32)
    if mode == "DBGC":
        dout("dbgc_sbf", [128, 1040], BF16)
        dout("dbgc_kf0", [128, 1040], BF16)
    if has_a:
        din("wk_a", [D, D], F32)
        din("wv_a", [D, D], F32)
        din("bk_a", [128, 8], F32)
        din("bv_a", [1, D], F32)
        din("om_a", [128, 32], F32)
        dout("xout", [8, NSLAB, 128, SLAB], F32)
        dout("kf_out", [8, 128, T], BF16)
        dout("vt_out", [NSLAB, 128, 4160], BF16)
        dout("sch_out", [NCH, 64, 1040], BF16)
        dout("stot", [64, 1040], BF16)
    if has_b:
        din("kf_in", [8, 128, T], BF16)
        din("vt_in", [NSLAB, 128, 4160], BF16)
        din("sch_in", [NCH, 64, 1040], BF16)
        din("gst", [8, 64, 1040], BF16)
        din("pmask", [128, 8], F32)
        din("wq_b", [D, D], F32)
        din("wo_b", [D, D], F32)
        din("w1_b", [D, FF], F32)
        din("w2_b", [FF, D], F32)
        din("bq_b", [128, 8], F32)
        din("bo_b", [128, 8], F32)
        din("ln1g_b", [128, 8], F32)
        din("ln1b_b", [128, 8], F32)
        din("ln2g_b", [128, 8], F32)
        din("ln2b_b", [128, 8], F32)
        din("bf1_b", [128, 32], F32)
        din("bf2_b", [128, 8], F32)
        din("om_b", [128, 32], F32)
        dram["rmid"] = nc.dram_tensor("rmid", [NSLAB, 8, 128, SLAB], F32).ap()
        dram["xmid"] = nc.dram_tensor("xmid", [NSLAB, 8, 128, SLAB], F32).ap()
    if has_f:
        din("x0", [T, D], F32)
        din("rhs", [D, D], F32)
        dout("out", [T, D], F32)
    if DEBUG and has_b:
        dout("dbg_qf", [8, 128, SLAB], BF16)
        dout("dbg_sbf", [128, 1040], BF16)
        dout("dbg_atm", [128, 1024], F32)
        dout("dbg_hy", [8, 128, SLAB], F32)
        dout("dbg_num", [2, 128, 65], F32)
        dout("dbg_a2m", [2, 128, 128], BF16)

    al = Alloc(nc)
    sb = {}

    def B(name, shape, dtype):
        sb[name] = al.buf(name, shape, dtype)
        return sb[name]

    for i in range(8):
        B(f"xsb{i}", [128, SLAB], F32)
    for i in range(8):
        B(f"xbf{i}", [128, SLAB], BF16)
    B("Wa", [128, 8192], BF16)
    B("Wb", [128, 8192], BF16)
    kf_off = al.mark()
    for i in range(8):
        B(f"kf{i}", [128, T], BF16)
    vt_off = al.mark()
    for i in range(NSLAB):
        B(f"vt{i}", [128, 4160], BF16)
    for i in range(8):
        B(f"qs{i}", [128, SLAB], BF16)
    for i in range(8):
        B(f"qf{i}", [128, SLAB], BF16)
    B("srun", [128, 1040], F32)
    B("sbf", [128, 1040], BF16)
    B("schb0", [128, 1040], BF16)
    B("schb1", [128, 1040], BF16)
    B("atm", [128, 1024], F32)
    B("etm", [128, 1024], F32)
    B("escr", [128, SLAB], F32)
    B("sq0", [128, SLAB], BF16)
    B("sq1", [128, SLAB], BF16)
    B("t1", [128, SLAB], F32)
    for i in range(8):
        B(f"hy{i}", [128, SLAB], F32)
    B("a2m", [128, 2048], BF16)
    B("cb", [128, 4], F32)
    r._junk = B("junk", [128, 64], F32)
    B("onesc", [128, 1], F32)
    B("onescb", [128, 1], BF16)
    B("oq", [128, 128], BF16)
    B("omsb", [128, 32], F32)
    B("dcol", [128, 32], F32)
    B("st8", [128, 12], F32)
    B("rowb", [1, SLAB], F32)
    B("rowb2", [1, SLAB], F32)
    if has_a:
        B("bvbc", [128, 1024], F32)
        B("stotsb", [128, 1040], F32)
        B("bvrow", [1, 1024], F32)
    B("bsum", [128, 8], F32)
    B("idf", [128, 128], F32)
    B("idb", [128, 128], BF16)
    B("triu", [128, 128], F32)
    B("e16", [128, 128], BF16)
    B("sel", [16, 1024], BF16)
    B("ones1f", [1, 128], F32)
    B("pm", [128, 8], F32)
    B("Ftile", [16, SLAB], F32)
    B("Fbf", [16, SLAB], BF16)
    for nm in ("bq", "bo", "l1g", "l1b", "l2g", "l2b", "bf2", "bk"):
        B(nm, [128, 8], F32)
    B("bf1", [128, 32], F32)
    if has_f:
        for i in range(8):
            sb[f"qrhs{i}"] = al.at(f"qrhs{i}", [128, 1024], BF16,
                                   offset=al.offs[f"xsb{i}"])
        sb["scolbc"] = al.at("scolbc", [128, 1024], F32, offset=al.offs["xbf0"])
        sb["rcolbc"] = al.at("rcolbc", [128, 1024], F32, offset=al.offs["xbf4"])
        B("scall", [128, 8], F32)
        B("rcall", [128, 8], F32)
        for i in range(8):
            sb[f"rhsf{i}"] = al.at(f"rhsf{i}", [128, 1024], F32,
                                   offset=kf_off + i * ceil32(T * 2))
        sb["x1t"] = al.at("x1t", [128, 1024], F32, offset=al.offs["qf0"])
        sb["qi"] = al.at("qi", [128, 1024], I32, offset=al.offs["qf4"])
        sb["qlf"] = al.at("qlf", [128, 1024], BF16, offset=al.offs["escr"])
        sb["x0t"] = al.at("x0t", [128, 1024], F32, offset=al.offs["srun"])

    assert al.off <= SBUF_BASE + 208 * 1024, f"SBUF over budget: {al.off}"

    ps = [nc.alloc_psum_tensor(f"ps{i}", [128, 512], F32) for i in range(8)]

    def dma(q, out_ap, in_ap, reads, writes):
        lbl = {"gpsimd": "gdma", "scalar": "sdma"}.get(q, q)
        r.op(lbl, lambda e, o=out_ap, i=in_ap: e.dma_start(out=o, in_=i),
             reads=reads, writes=writes, dma=True)

    def mm(out_ap, lh, rh, st, sp, reads, writes, tile_position=None):
        r.op("tensor",
             lambda e, o=out_ap, a=lh, b=rh, s=st, p=sp, tp=tile_position:
             e.matmul(o, a, b, start=s, stop=p, tile_position=tp),
             reads=reads, writes=writes)

    def vop(fn, reads, writes, eng="vector", big=True):
        r.op(eng, fn, reads=reads, writes=writes, big=big)

    for nm, key in (("idf", "idf"), ("idb", "idb"), ("triu", "triu"),
                    ("e16", "e16"), ("sel", "sel"), ("ones1f", "ones1f")):
        dma("sync", sb[nm][:], dram[key], [], [nm])
    if has_b:
        dma("sync", sb["pm"][:], dram["pmask"], [], ["pm"])
    vop(lambda e: e.memset(sb["onesc"][:], 1.0), [], ["onesc"])
    vop(lambda e: e.memset(sb["onescb"][:], 1.0), [], ["onescb"])
    vop(lambda e: e.memset(sb["cb"][:, 0:1], -LN8), [], ["cb"])
    vop(lambda e: e.memset(sb["cb"][:, 1:2], LN_EPS), [], ["cb"])
    vop(lambda e: e.memset(sb["cb"][:, 2:3], ATTN_EPS), [], ["cb"])

    def load_layer_vecs(which):
        if which == "B":
            for nm, key in (("bq", "bq_b"), ("bo", "bo_b"), ("l1g", "ln1g_b"),
                            ("l1b", "ln1b_b"), ("l2g", "ln2g_b"), ("l2b", "ln2b_b"),
                            ("bf2", "bf2_b")):
                dma("sync", sb[nm][:], dram[key], [], [nm])
            dma("sync", sb["bf1"][:], dram["bf1_b"], [], ["bf1"])
            vop(lambda e: e.tensor_tensor(sb["bsum"][:], sb["l1b"][:], sb["bf2"][:], OP.add),
                ["l1b", "bf2"], ["bsum"])
        else:
            dma("sync", sb["bk"][:], dram["bk_a"], [], ["bk"])
            dma("sync", sb["bvrow"][:], dram["bv_a"], [], ["bvrow"])

    def build_oq(omkey):
        dma("sync", sb["omsb"][:], dram[omkey], [], ["om"])
        vop(lambda e: e.memset(sb["oq"][:], 0.0), [], ["oq"])
        for (r0, c0, sc) in ((0, 0, CS), (0, 32, -CS), (64, 64, CS), (64, 96, -CS)):
            vop(lambda e, r0=r0, c0=c0, sc=sc:
                e.tensor_scalar(sb["oq"][r0:r0 + 64, c0:c0 + 32],
                                sb["omsb"][r0:r0 + 64, 0:32], sc, None, OP.mult),
                ["om", "oq"], ["oq"], big=False)

    def build_bvbc():
        for n2 in range(2):
            mm(ps[2][0:128, 0:512], sb["ones1f"][0:1, 0:128],
               sb["bvrow"][0:1, 512 * n2:512 * n2 + 512], True, True,
               ["ones1f", "bvrow", "ps2"], ["ps2"])
            vop(lambda e, n2=n2: e.tensor_copy(sb["bvbc"][:, 512 * n2:512 * n2 + 512],
                                               ps[2][0:128, 0:512]),
                ["ps2"], ["bvbc"])

    def load_w(dst, key, ncols=D):
        for kc in range(8):
            dma("gpsimd", sb[dst][:, ncols * kc:ncols * kc + ncols],
                dram[key][128 * kc:128 * kc + 128, :], [dst], [dst])

    def load_w_slice(dst, key, r0, r1c, c0, c1):
        n = c1 - c0
        for kc in range((r1c - r0) // 128):
            dma("gpsimd", sb[dst][:, n * kc:n * kc + n],
                dram[key][r0 + 128 * kc:r0 + 128 * kc + 128, c0:c1], [dst], [dst])

    def favor_fm(s, wkey, biaskey, dst_pref, store_whole):
        col = 512 * s if store_whole else 0
        for mc in range(8):
            pb = ps[mc % 2]
            for kc in range(8):
                mm(pb[0:128, 0:512],
                   sb[wkey][:, 1024 * kc + 128 * mc:1024 * kc + 128 * mc + 128],
                   sb[f"xbf{kc}"][:, 0:512], kc == 0, kc == 7,
                   [wkey, f"xbf{kc}", f"ps{mc % 2}"], [f"ps{mc % 2}"])
            vop(lambda e, pb=pb, mc=mc: e.activation(
                sb[f"qs{mc}"][:, 0:512], pb[0:128, 0:512], AF.Identity,
                bias=sb[biaskey][:, mc:mc + 1], scale=1.0),
                [f"ps{mc % 2}", biaskey], [f"qs{mc}"], eng="scalar")
        for mc in range(8):
            sqb = f"sq{mc % 2}"
            vop(lambda e, mc=mc, sqb=sqb: e.activation(
                sb[sqb][:, 0:512], sb[f"qs{mc}"][:, 0:512], AF.Square),
                [f"qs{mc}"], [sqb], eng="scalar")
            mm(ps[3][0:16, 0:512], sb["e16"][:, 16 * mc:16 * mc + 16],
               sb[sqb][:, 0:512], mc == 0, mc == 7,
               ["e16", sqb, "ps3"], ["ps3"])
        vop(lambda e: e.activation(sb["Ftile"][:, 0:512], ps[3][0:16, 0:512],
                                   AF.Exp, bias=sb["cb"][0:16, 0:1], scale=-CQ),
            ["ps3", "cb"], ["Ftile"], eng="scalar")
        vop(lambda e: e.tensor_copy(sb["Fbf"][:, 0:512], sb["Ftile"][:, 0:512]),
            ["Ftile"], ["Fbf"])
        for d in range(8):
            mm(ps[4][0:128, 0:512], sb["oq"][:, 0:128], sb[f"qs{d}"][:, 0:512],
               True, True, ["oq", f"qs{d}", "ps4"], ["ps4"])
            mm(ps[5][0:128, 0:512], sb["sel"][0:16, 128 * d:128 * d + 128],
               sb["Fbf"][0:16, 0:512], True, True,
               ["sel", "Fbf", "ps5"], ["ps5"])
            vop(lambda e: e.activation(sb["escr"][:, 0:512], ps[4][0:128, 0:512], AF.Exp),
                ["ps4"], ["escr"], eng="scalar")
            tgt = f"{dst_pref}{d}"
            vop(lambda e, tgt=tgt, col=col: e.tensor_tensor(
                sb[tgt][:, col:col + 512], sb["escr"][:, 0:512],
                ps[5][0:128, 0:512], OP.mult),
                ["escr", "ps5"], [tgt])

    def phase_a(layer_first):
        build_oq("om_a")
        load_layer_vecs("A")
        build_bvbc()
        load_w("Wa", "wk_a")
        load_w("Wb", "wv_a")
        for s in range(NSLAB):
            for tcl in range(4):
                vop(lambda e, s=s, tcl=tcl: e.memset(
                    sb[f"vt{s}"][:, 1040 * tcl:1040 * tcl + 1040]
                    .rearrange("p (h j) -> p h j", j=65)[:, :, 64:65], 1.0),
                    [f"vt{s}"], [f"vt{s}"])
        vop(lambda e: e.memset(sb["stotsb"][:], 0.0), [], ["stotsb"])
        for s in range(NSLAB):
            if mode == "A":
                for tcl in range(4):
                    c = 4 * s + tcl
                    dma("sync", sb["atm"][:], dram["x0"][128 * c:128 * c + 128, :],
                        ["atm"], ["atm"])
                    for kc in range(8):
                        r.op("tensor", lambda e, kc=kc: e.transpose(
                            ps[6][0:128, 0:128],
                            sb["atm"][:, 128 * kc:128 * kc + 128], sb["idf"][:]),
                            reads=["atm", "idf", "ps6"], writes=["ps6"])
                        vop(lambda e, kc=kc, tcl=tcl: e.tensor_copy(
                            sb[f"xsb{kc}"][:, 128 * tcl:128 * tcl + 128],
                            ps[6][0:128, 0:128]),
                            ["ps6"], [f"xsb{kc}"])
                for kc in range(8):
                    dma("scalar", dram["xout"][kc, s], sb[f"xsb{kc}"][:],
                        [f"xsb{kc}"], [])
            else:
                for kc in range(8):
                    dma("sync", sb[f"xsb{kc}"][:], dram["xmid"][s, kc],
                        [f"xsb{kc}", "xmid"], [f"xsb{kc}"])
            for kc in range(8):
                vop(lambda e, kc=kc: e.tensor_copy(sb[f"xbf{kc}"][:],
                                                   sb[f"xsb{kc}"][:]),
                    [f"xsb{kc}"], [f"xbf{kc}"], eng="gpsimd")
            for tcl in range(4):
                for n2 in range(2):
                    pb = ps[n2]
                    for kc in range(8):
                        mm(pb[0:128, 0:512],
                           sb[f"xbf{kc}"][:, 128 * tcl:128 * tcl + 128],
                           sb["Wb"][:, 1024 * kc + 512 * n2:1024 * kc + 512 * n2 + 512],
                           kc == 0, kc == 7,
                           [f"xbf{kc}", "Wb", f"ps{n2}"], [f"ps{n2}"])
                    vop(lambda e, s=s, tcl=tcl, n2=n2, pb=pb: e.tensor_tensor(
                        sb[f"vt{s}"][:, 1040 * tcl + 520 * n2:1040 * tcl + 520 * n2 + 520]
                        .rearrange("p (h j) -> p h j", j=65)[:, :, 0:64],
                        pb[0:128, 0:512].rearrange("p (h j) -> p h j", j=64),
                        sb["bvbc"][:, 512 * n2:512 * n2 + 512]
                        .rearrange("p (h j) -> p h j", j=64), OP.add),
                        [f"ps{n2}", "bvbc", f"vt{s}"], [f"vt{s}"])
            favor_fm(s, "Wa", "bk", "kf", True)
            for tcl in range(4):
                c = 4 * s + tcl
                for d in range(8):
                    pb = ps[2 + d // 4]
                    mm(pb[0:128, 128 * (d % 4):128 * (d % 4) + 128],
                       sb[f"qs{d}"][:, 128 * tcl:128 * tcl + 128],
                       sb["oq"][:, 0:128], True, True,
                       [f"qs{d}", "oq", f"ps{2 + d // 4}"], [f"ps{2 + d // 4}"])
                r.op("tensor", lambda e, tcl=tcl: e.transpose(
                    ps[7][0:128, 0:16],
                    sb["Ftile"][0:16, 128 * tcl:128 * tcl + 128],
                    sb["idf"][0:16, 0:16]),
                    reads=["Ftile", "idf", "ps7"], writes=["ps7"])
                for half in range(2):
                    vop(lambda e, half=half: e.activation(
                        sb["etm"][:, 512 * half:512 * half + 512],
                        ps[2 + half][0:128, 0:512], AF.Exp),
                        [f"ps{2 + half}"], ["etm"], eng="scalar")
                vop(lambda e: e.tensor_copy(sb["atm"][:, 0:16], ps[7][0:128, 0:16]),
                    ["ps7"], ["atm"], big=False)
                vop(lambda e: e.tensor_tensor(
                    sb["schb1"][:, 0:1024].rearrange("p (h j) -> p h j", j=64),
                    sb["etm"][:, 0:1024].rearrange("p (h j) -> p h j", j=64),
                    sb["atm"][:, 0:16].rearrange("p (h j) -> p h j", j=1)
                    .broadcast_to((128, 16, 64)), OP.mult),
                    ["etm", "atm"], ["schb1"])
                for h in range(16):
                    bk, col = ("ps4", 65 * h) if h < 7 else (
                        ("ps5", 65 * (h - 7)) if h < 14 else ("ps6", 65 * (h - 14)))
                    mm(ps[int(bk[2])][0:64, col:col + 65],
                       sb["schb1"][:, 64 * h:64 * h + 64],
                       sb[f"vt{s}"][:, 1040 * tcl + 65 * h:1040 * tcl + 65 * h + 65],
                       True, True, ["schb1", f"vt{s}", bk], [bk])
                for (bk, c0, w) in (("ps4", 0, 455), ("ps5", 455, 455), ("ps6", 910, 130)):
                    vop(lambda e, bk=bk, c0=c0, w=w: e.tensor_copy(
                        sb["schb0"][0:64, c0:c0 + w], ps[int(bk[2])][0:64, 0:w]),
                        [bk], ["schb0"])
                vop(lambda e: e.tensor_tensor(sb["stotsb"][0:64, :],
                                              sb["stotsb"][0:64, :],
                                              sb["schb0"][0:64, :], OP.add),
                    ["schb0", "stotsb"], ["stotsb"])
                dma("scalar", dram["sch_out"][c], sb["schb0"][0:64, :], ["schb0"], [])
        for d in range(8):
            dma("scalar", dram["kf_out"][d], sb[f"kf{d}"][:], [f"kf{d}"], [])
        for s in range(NSLAB):
            dma("scalar", dram["vt_out"][s], sb[f"vt{s}"][:], [f"vt{s}"], [])
        vop(lambda e: e.tensor_copy(sb["schb0"][0:64, :], sb["stotsb"][0:64, :]),
            ["stotsb"], ["schb0"])
        dma("scalar", dram["stot"], sb["schb0"][0:64, :], ["schb0"], [])

    def phase_b():
        build_oq("om_b")
        load_layer_vecs("B")
        vop(lambda e: e.memset(sb["srun"][:], 0.0), [], ["srun"])
        for j in range(8):
            dma("sync", sb[f"kf{j}"][0:64, 0:1040], dram["gst"][j],
                [f"kf{j}"], [f"kf{j}"])
            dma("sync", sb[f"kf{j}"][64:128, 0:1040], dram["gst"][j],
                [f"kf{j}"], [f"kf{j}"])
        for j in range(8):
            vop(lambda e, j=j: e.scalar_tensor_tensor(
                sb["srun"][:], sb[f"kf{j}"][:, 0:1040], sb["pm"][:, j:j + 1],
                sb["srun"][:], OP.mult, OP.add),
                [f"kf{j}", "pm", "srun"], ["srun"])
        vop(lambda e: e.tensor_copy(sb["sbf"][:], sb["srun"][:]), ["srun"], ["sbf"])
        load_w("Wa", "wq_b")
        load_w("Wb", "wo_b")
        for d in range(8):
            dma("sync", sb[f"kf{d}"][:], dram["kf_in"][d], [f"kf{d}"], [f"kf{d}"])
        for s in range(NSLAB):
            dma("sync", sb[f"vt{s}"][:], dram["vt_in"][s], [f"vt{s}"], [f"vt{s}"])
        if DEBUG:
            dma("sync", dram["dbg_sbf"], sb["sbf"][:], ["sbf"], [])

        for s in range(NSLAB):
            for kc in range(8):
                dma("sync", sb[f"xsb{kc}"][:], dram["xin"][kc, s],
                    [f"xsb{kc}"], [f"xsb{kc}"])
                vop(lambda e, kc=kc: e.tensor_copy(sb[f"xbf{kc}"][:],
                                                   sb[f"xsb{kc}"][:]),
                    [f"xsb{kc}"], [f"xbf{kc}"], eng="gpsimd")
            favor_fm(s, "Wa", "bq", "qf", False)
            if DEBUG and s == 0:
                for dd in range(8):
                    dma("sync", dram["dbg_qf"][dd], sb[f"qf{dd}"][:], [f"qf{dd}"], [])
            for tcl in range(4):
                c = 4 * s + tcl
                if c > 0:
                    buf = f"schb{c % 2}"
                    dma("sync", sb[buf][0:64, :], dram["sch_in"][c - 1], [buf], [buf])
                    dma("sync", sb[buf][64:128, :], dram["sch_in"][c - 1], [buf], [buf])
                    vop(lambda e, buf=buf: e.tensor_tensor(
                        sb["srun"][:], sb["srun"][:], sb[buf][:], OP.add),
                        [buf, "srun"], ["srun"])
                    vop(lambda e: e.tensor_copy(sb["sbf"][:], sb["srun"][:]),
                        ["srun"], ["sbf"])
                if not SCAN_BATCHED:
                    A2COL = 128 if SCAN_SER_COLS else 0
                    for h in range(16):
                        d, base = h // 2, 64 * (h % 2)
                        acol = A2COL * h if SCAN_SER_COLS else 0
                        mm(ps[3][0:128, 0:128],
                           sb[f"kf{d}"][base:base + 64, 128 * c:128 * c + 128],
                           sb[f"qf{d}"][base:base + 64, 128 * tcl:128 * tcl + 128],
                           True, True, [f"kf{d}", f"qf{d}", "ps3"], ["ps3"])
                        vop(lambda e, acol=acol: e.tensor_tensor(
                            sb["a2m"][:, acol:acol + 128],
                            ps[3][0:128, 0:128],
                            sb["triu"][:], OP.mult),
                            ["ps3", "triu"], ["am0"])
                        mm(ps[4][0:128, 0:65], sb["a2m"][:, acol:acol + 128],
                           sb[f"vt{s}"][:, 1040 * tcl + 65 * h:1040 * tcl + 65 * h + 65],
                           True, False, ["am0", f"vt{s}"], ["nm0", "ps4"])
                        mm(ps[4][0:128, 0:65],
                           sb[f"qf{d}"][base:base + 64, 128 * tcl:128 * tcl + 128],
                           sb["sbf"][base:base + 64, 65 * h:65 * h + 65],
                           False, True, [f"qf{d}", "sbf", "nm0"], ["nm0", "ps4"])
                        vop(lambda e: e.activation(sb["dcol"][:, 0:1],
                                                   ps[4][0:128, 64:65], AF.Identity,
                                                   bias=sb["cb"][:, 2:3], scale=1.0),
                            ["nm0", "cb", "ps4"], ["dc0"], eng="scalar", big=False)
                        vop(lambda e: e.reciprocal(sb["dcol"][:, 1:2],
                                                   sb["dcol"][:, 0:1]),
                            ["dc0"], ["dr0"], big=False)
                        vop(lambda e, h=h: e.activation(
                            sb["atm"][:, 64 * h:64 * h + 64], ps[4][0:128, 0:64],
                            AF.Identity, bias=0.0, scale=sb["dcol"][:, 1:2]),
                            ["nm0", "dr0", "ps4"], ["atm"], eng="scalar", big=False)
                if SCAN_BATCHED == 4:
                    banks4 = ["ps2", "ps3", "ps5", "ps6"]
                    for h in range(16):
                        d, base = h // 2, 64 * (h % 2)
                        bk = banks4[h % 4]
                        pb = ps[int(bk[2])]
                        mm(pb[0:128, 0:128],
                           sb[f"kf{d}"][base:base + 64, 128 * c:128 * c + 128],
                           sb[f"qf{d}"][base:base + 64, 128 * tcl:128 * tcl + 128],
                           True, True, [f"kf{d}", f"qf{d}", bk], [bk])
                        vop(lambda e, h=h, pb=pb: e.tensor_tensor(
                            sb["a2m"][:, 128 * h:128 * h + 128],
                            pb[0:128, 0:128], sb["triu"][:], OP.mult),
                            [bk, "triu"], [f"am{h}"])
                    for h in range(16):
                        d, base = h // 2, 64 * (h % 2)
                        nb = "ps4" if h % 2 == 0 else "ps7"
                        nap = ps[int(nb[2])][0:128, 0:65]
                        mm(nap, sb["a2m"][:, 128 * h:128 * h + 128],
                           sb[f"vt{s}"][:, 1040 * tcl + 65 * h:1040 * tcl + 65 * h + 65],
                           True, False, [f"am{h}", f"vt{s}"], [f"nm{h}", nb])
                        mm(nap,
                           sb[f"qf{d}"][base:base + 64, 128 * tcl:128 * tcl + 128],
                           sb["sbf"][base:base + 64, 65 * h:65 * h + 65],
                           False, True, [f"qf{d}", "sbf", f"nm{h}"], [f"nm{h}", nb])
                        vop(lambda e, h=h, nb=nb: e.activation(
                            sb["dcol"][:, 2 * h:2 * h + 1],
                            ps[int(nb[2])][0:128, 64:65], AF.Identity,
                            bias=sb["cb"][:, 2:3], scale=1.0),
                            [f"nm{h}", "cb", nb], [f"dc{h}"], eng="scalar", big=False)
                        vop(lambda e, h=h: e.reciprocal(
                            sb["dcol"][:, 2 * h + 1:2 * h + 2],
                            sb["dcol"][:, 2 * h:2 * h + 1]),
                            [f"dc{h}"], [f"dr{h}"], big=False)
                        vop(lambda e, h=h, nb=nb: e.activation(
                            sb["atm"][:, 64 * h:64 * h + 64],
                            ps[int(nb[2])][0:128, 0:64],
                            AF.Identity, bias=0.0,
                            scale=sb["dcol"][:, 2 * h + 1:2 * h + 2]),
                            [f"nm{h}", f"dr{h}", nb], ["atm"], eng="scalar", big=False)
                A2B = {0: "ps2", 1: "ps3", 2: "ps5", 3: "ps6"}
                if SCAN_BATCHED != 2:
                    A2B = None
                NB = {}
                for h in range(16):
                    if h < 7:
                        NB[h] = ("ps4", 66 * h)
                    elif h < 14:
                        NB[h] = ("ps7", 66 * (h - 7))
                    else:
                        NB[h] = ("ps2", 66 * (h - 14))
                a2ap = {}
                for h in (range(16) if SCAN_BATCHED == 2 else []):
                    d, base = h // 2, 64 * (h % 2)
                    bk = A2B[h // 4]
                    a2ap[h] = ps[int(bk[2])][0:128, 128 * (h % 4):128 * (h % 4) + 128]
                    mm(a2ap[h],
                       sb[f"kf{d}"][base:base + 64, 128 * c:128 * c + 128],
                       sb[f"qf{d}"][base:base + 64, 128 * tcl:128 * tcl + 128],
                       True, True, [f"kf{d}", f"qf{d}", bk], [bk])
                for h in (range(16) if SCAN_BATCHED == 2 else []):
                    bk = A2B[h // 4]
                    vop(lambda e, h=h, ap=a2ap[h]: e.tensor_tensor(
                        sb["a2m"][:, 128 * h:128 * h + 128], ap,
                        sb["triu"][:], OP.mult),
                        [bk, "triu"], [f"am{h}"])
                if SCAN_BATCHED == 1:
                    for h in range(16):
                        d, base = h // 2, 64 * (h % 2)
                        mm(ps[4][0:128, 0:65], sb["a2m"][:, 128 * h:128 * h + 128],
                           sb[f"vt{s}"][:, 1040 * tcl + 65 * h:1040 * tcl + 65 * h + 65],
                           True, False, [f"am{h}", f"vt{s}"], ["nm0", "ps4"])
                        mm(ps[4][0:128, 0:65],
                           sb[f"qf{d}"][base:base + 64, 128 * tcl:128 * tcl + 128],
                           sb["sbf"][base:base + 64, 65 * h:65 * h + 65],
                           False, True, [f"qf{d}", "sbf", "nm0"], ["nm0", "ps4"])
                        vop(lambda e: e.activation(sb["dcol"][:, 0:1],
                                                   ps[4][0:128, 64:65], AF.Identity,
                                                   bias=sb["cb"][:, 2:3], scale=1.0),
                            ["nm0", "cb", "ps4"], ["dc0"], eng="scalar", big=False)
                        vop(lambda e: e.reciprocal(sb["dcol"][:, 1:2],
                                                   sb["dcol"][:, 0:1]),
                            ["dc0"], ["dr0"], big=False)
                        vop(lambda e, h=h: e.activation(
                            sb["atm"][:, 64 * h:64 * h + 64], ps[4][0:128, 0:64],
                            AF.Identity, bias=0.0, scale=sb["dcol"][:, 1:2]),
                            ["nm0", "dr0", "ps4"], ["atm"], eng="scalar", big=False)
                for h in (range(16) if SCAN_BATCHED == 2 else []):
                    d, base = h // 2, 64 * (h % 2)
                    nb, ncol = NB[h]
                    nap = ps[int(nb[2])][0:128, ncol:ncol + 65]
                    mm(nap, sb["a2m"][:, 128 * h:128 * h + 128],
                       sb[f"vt{s}"][:, 1040 * tcl + 65 * h:1040 * tcl + 65 * h + 65],
                       True, False, [f"am{h}", f"vt{s}"], [f"nm{h}", nb])
                    mm(nap,
                       sb[f"qf{d}"][base:base + 64, 128 * tcl:128 * tcl + 128],
                       sb["sbf"][base:base + 64, 65 * h:65 * h + 65],
                       False, True, [f"qf{d}", "sbf", f"nm{h}"], [f"nm{h}", nb])
                for h in (range(16) if SCAN_BATCHED == 2 else []):
                    nb, ncol = NB[h]
                    vop(lambda e, h=h, nb=nb, ncol=ncol: e.activation(
                        sb["dcol"][:, 2 * h:2 * h + 1],
                        ps[int(nb[2])][0:128, ncol + 64:ncol + 65], AF.Identity,
                        bias=sb["cb"][:, 2:3], scale=1.0),
                        [f"nm{h}", "cb", nb], [f"dc{h}"], eng="scalar", big=False)
                    vop(lambda e, h=h: e.reciprocal(
                        sb["dcol"][:, 2 * h + 1:2 * h + 2],
                        sb["dcol"][:, 2 * h:2 * h + 1]),
                        [f"dc{h}"], [f"dr{h}"], big=False)
                for h in (range(16) if SCAN_BATCHED == 2 else []):
                    nb, ncol = NB[h]
                    vop(lambda e, h=h, nb=nb, ncol=ncol: e.activation(
                        sb["atm"][:, 64 * h:64 * h + 64],
                        ps[int(nb[2])][0:128, ncol:ncol + 64],
                        AF.Identity, bias=0.0,
                        scale=sb["dcol"][:, 2 * h + 1:2 * h + 2]),
                        [f"nm{h}", f"dr{h}", nb], ["atm"], eng="scalar", big=False)
                if DEBUG and s == 0 and tcl == 0:
                    dma("sync", dram["dbg_atm"], sb["atm"][:], ["atm"], [])
                for j in range(8):
                    bk = "ps3" if j % 2 == 0 else "ps5"
                    pb = ps[int(bk[2])]
                    r.op("tensor", lambda e, j=j, pb=pb: e.transpose(
                        pb[0:128, 0:128], sb["atm"][:, 128 * j:128 * j + 128],
                        sb["idf"][:]),
                        reads=["atm", "idf", bk] + [f"am{h}" for h in range(16)] if j == 0 else ["atm", "idf", bk],
                        writes=[bk])
                    vop(lambda e, j=j, tcl=tcl, pb=pb: e.tensor_copy(
                        sb[f"qs{j}"][:, 128 * tcl:128 * tcl + 128],
                        pb[0:128, 0:128]),
                        [bk], [f"qs{j}"])
            for mc in range(8):
                pb = ps[mc % 2]
                for kc in range(8):
                    mm(pb[0:128, 0:512],
                       sb["Wb"][:, 1024 * kc + 128 * mc:1024 * kc + 128 * mc + 128],
                       sb[f"qs{kc}"][:, 0:512], kc == 0, kc == 7,
                       ["Wb", f"qs{kc}", f"ps{mc % 2}"], [f"ps{mc % 2}"])
                vop(lambda e, pb=pb, mc=mc: e.scalar_tensor_tensor(
                    sb[f"hy{mc}"][:], pb[0:128, 0:512], sb["bo"][:, mc:mc + 1],
                    sb[f"xsb{mc}"][:], OP.add, OP.add),
                    [f"ps{mc % 2}", "bo", f"xsb{mc}"], [f"hy{mc}"])
                dma("scalar", dram["rmid"][s, mc], sb[f"hy{mc}"][:],
                    [f"hy{mc}"], ["rmid"])
                if DEBUG and s == 0:
                    dma("sync", dram["dbg_hy"][mc], sb[f"hy{mc}"][:],
                        [f"hy{mc}"], [])

        def ln_stats(src_tensors, src_tokens):
            for kc in range(8):
                vop(lambda e, kc=kc: e.tensor_copy(sb[f"xbf{kc}"][:, 0:512],
                                                   src_tensors[kc][:, 0:512]),
                    [src_tokens[kc]], [f"xbf{kc}"], eng="gpsimd")
            for kc in range(8):
                mm(ps[6][0:1, 0:512], sb["onescb"][:, 0:1],
                   sb[f"xbf{kc}"][:, 0:512], kc == 0, kc == 7,
                   ["onescb", f"xbf{kc}", "ps6"], ["ps6"])
            for kc in range(8):
                sq = f"sq{kc % 2}"
                vop(lambda e, kc=kc, sq=sq: e.activation(
                    sb[sq][:, 0:512], sb[f"xbf{kc}"][:, 0:512], AF.Square),
                    [f"xbf{kc}"], [sq], eng="scalar")
                mm(ps[7][0:1, 0:512], sb["onescb"][:, 0:1],
                   sb[sq][:, 0:512], kc == 0, kc == 7,
                   ["onescb", sq, "ps7"], ["ps7"])
            vop(lambda e: e.tensor_scalar(sb["rowb"][0:1, :], ps[6][0:1, 0:512],
                                          1.0 / 1024.0, None, OP.mult),
                ["ps6"], ["rowb"])
            vop(lambda e: e.tensor_scalar(sb["rowb2"][0:1, :], ps[7][0:1, 0:512],
                                          1.0 / 1024.0, None, OP.mult),
                ["ps7"], ["rowb2"])
            vop(lambda e: e.tensor_tensor(sb["Ftile"][0:1, 0:512],
                                          sb["rowb"][0:1, :], sb["rowb"][0:1, :],
                                          OP.mult),
                ["rowb"], ["Ftile"])
            vop(lambda e: e.tensor_tensor(sb["rowb2"][0:1, :], sb["rowb2"][0:1, :],
                                          sb["Ftile"][0:1, 0:512], OP.subtract),
                ["rowb2", "Ftile"], ["rowb2"])
            vop(lambda e: e.activation(sb["rowb2"][0:1, :], sb["rowb2"][0:1, :],
                                       AF.Sqrt, bias=sb["cb"][0:1, 1:2], scale=1.0),
                ["rowb2", "cb"], ["rowb2"], eng="scalar")
            vop(lambda e: e.reciprocal(sb["rowb2"][0:1, :], sb["rowb2"][0:1, :]),
                ["rowb2"], ["rowb2"])
            vop(lambda e: e.tensor_tensor(sb["rowb"][0:1, :], sb["rowb"][0:1, :],
                                          sb["rowb2"][0:1, :], OP.mult),
                ["rowb", "rowb2"], ["rowb"])
            mm(ps[6][0:128, 0:512], sb["ones1f"][0:1, 0:128], sb["rowb2"][0:1, :],
               True, True, ["ones1f", "rowb2", "ps6"], ["ps6"])
            mm(ps[7][0:128, 0:512], sb["ones1f"][0:1, 0:128], sb["rowb"][0:1, :],
               True, True, ["ones1f", "rowb", "ps7"], ["ps7"])

        for s in range(NSLAB):
            for kc in range(8):
                dma("sync", sb[f"xsb{kc}"][:], dram["rmid"][s, kc],
                    [f"xsb{kc}", "rmid"], [f"xsb{kc}"])
            ln_stats([sb[f"xsb{kc}"] for kc in range(8)],
                     [f"xsb{kc}" for kc in range(8)])
            for kc in range(8):
                vop(lambda e, kc=kc: e.tensor_tensor(sb["t1"][:], sb[f"xsb{kc}"][:],
                                                     ps[6][0:128, 0:512], OP.mult),
                    [f"xsb{kc}", "ps6"], ["t1"])
                vop(lambda e: e.tensor_tensor(sb["t1"][:], sb["t1"][:],
                                              ps[7][0:128, 0:512], OP.subtract),
                    ["t1", "ps7"], ["t1"])
                vop(lambda e, kc=kc, s=s: e.tensor_scalar(
                    sb[f"kf{kc}"][:, 512 * s:512 * s + 512], sb["t1"][:],
                    sb["l1g"][:, kc:kc + 1], sb["l1b"][:, kc:kc + 1],
                    OP.mult, OP.add),
                    ["t1", "l1g", "l1b"], [f"kf{kc}"])
        for g2 in range(8):
            par = g2 % 2
            wa, wb = f"Wa{par}", f"Wb{par}"
            for kc in range(8):
                dma("gpsimd", sb["Wa"][:, 4096 * par + 512 * kc:4096 * par + 512 * kc + 512],
                    dram["w1_b"][128 * kc:128 * kc + 128, 512 * g2:512 * g2 + 512],
                    [wa], [wa])
            for j in range(4):
                dma("gpsimd", sb["Wb"][:, 4096 * par + 1024 * j:4096 * par + 1024 * j + 1024],
                    dram["w2_b"][512 * g2 + 128 * j:512 * g2 + 128 * j + 128, :],
                    [wb], [wb])
            for s in range(NSLAB):
                for mc in range(4):
                    pb = ps[mc % 2]
                    for kc in range(8):
                        mm(pb[0:128, 0:512],
                           sb["Wa"][:, 4096 * par + 512 * kc + 128 * mc:4096 * par + 512 * kc + 128 * mc + 128],
                           sb[f"kf{kc}"][:, 512 * s:512 * s + 512], kc == 0, kc == 7,
                           [wa, f"kf{kc}", f"ps{mc % 2}"], [f"ps{mc % 2}"])
                    vop(lambda e, pb=pb, g2=g2, mc=mc: e.activation(
                        sb[f"qs{mc}"][:, 0:512], pb[0:128, 0:512], AF.Relu,
                        bias=sb["bf1"][:, 4 * g2 + mc:4 * g2 + mc + 1], scale=1.0),
                        [f"ps{mc % 2}", "bf1"], [f"qs{mc}"], eng="scalar")
                for mc2 in range(8):
                    pb = ps[2 + mc2 % 2]
                    for mc in range(4):
                        mm(pb[0:128, 0:512],
                           sb["Wb"][:, 4096 * par + 1024 * mc + 128 * mc2:4096 * par + 1024 * mc + 128 * mc2 + 128],
                           sb[f"qs{mc}"][:, 0:512], mc == 0, mc == 3,
                           [wb, f"qs{mc}", f"ps{2 + mc2 % 2}"], [f"ps{2 + mc2 % 2}"])
                    yt = f"vt{mc2 // 2}"
                    ycols = slice(2080 * (mc2 % 2) + 512 * s,
                                  2080 * (mc2 % 2) + 512 * s + 512)
                    if g2 == 0:
                        vop(lambda e, pb=pb, yt=yt, yc=ycols: e.tensor_copy(
                            sb[yt][:, yc], pb[0:128, 0:512]),
                            [f"ps{2 + mc2 % 2}"], [yt])
                    else:
                        vop(lambda e, pb=pb, yt=yt, yc=ycols: e.tensor_tensor(
                            sb[yt][:, yc], sb[yt][:, yc], pb[0:128, 0:512], OP.add),
                            [f"ps{2 + mc2 % 2}", yt], [yt])
        for s in range(NSLAB):
            for kc in range(8):
                dma("sync", sb[f"xsb{kc}"][:], dram["rmid"][s, kc],
                    [f"xsb{kc}", "rmid"], [f"xsb{kc}"])
            ln_stats([sb[f"xsb{kc}"] for kc in range(8)],
                     [f"xsb{kc}" for kc in range(8)])
            for kc in range(8):
                vop(lambda e, kc=kc: e.tensor_tensor(sb["t1"][:], sb[f"xsb{kc}"][:],
                                                     ps[6][0:128, 0:512], OP.mult),
                    [f"xsb{kc}", "ps6"], ["t1"])
                vop(lambda e: e.tensor_tensor(sb["t1"][:], sb["t1"][:],
                                              ps[7][0:128, 0:512], OP.subtract),
                    ["t1", "ps7"], ["t1"])
                vop(lambda e, kc=kc: e.tensor_scalar(
                    sb["t1"][:], sb["t1"][:], sb["l1g"][:, kc:kc + 1],
                    sb["bsum"][:, kc:kc + 1], OP.mult, OP.add),
                    ["t1", "l1g", "bsum"], ["t1"])
                yt = f"vt{kc // 2}"
                ycols = slice(2080 * (kc % 2) + 512 * s, 2080 * (kc % 2) + 512 * s + 512)
                vop(lambda e, yt=yt, yc=ycols, kc=kc: e.tensor_tensor(
                    sb[f"hy{kc}"][:], sb["t1"][:], sb[yt][:, yc], OP.add),
                    ["t1", yt], [f"hy{kc}"])
            ln_stats([sb[f"hy{kc}"] for kc in range(8)],
                     [f"hy{kc}" for kc in range(8)])
            for kc in range(8):
                vop(lambda e, kc=kc: e.tensor_tensor(sb["t1"][:], sb[f"hy{kc}"][:],
                                                     ps[6][0:128, 0:512], OP.mult),
                    [f"hy{kc}", "ps6"], ["t1"])
                vop(lambda e: e.tensor_tensor(sb["t1"][:], sb["t1"][:],
                                              ps[7][0:128, 0:512], OP.subtract),
                    ["t1", "ps7"], ["t1"])
                vop(lambda e, kc=kc: e.tensor_scalar(
                    sb[f"xsb{kc}"][:], sb["t1"][:], sb["l2g"][:, kc:kc + 1],
                    sb["l2b"][:, kc:kc + 1], OP.mult, OP.add),
                    ["t1", "l2g", "l2b"], [f"xsb{kc}"])
                dma("scalar", dram["xmid"][s, kc], sb[f"xsb{kc}"][:],
                    [f"xsb{kc}"], ["xmid"])
                if has_a:
                    dma("scalar", dram["xout"][kc, s], sb[f"xsb{kc}"][:],
                        [f"xsb{kc}"], [])

    def rhs_prep():
        for kc in range(8):
            dma("sync", sb[f"rhsf{kc}"][:], dram["rhs"][128 * kc:128 * kc + 128, :],
                [f"kf{kc}"], [f"kf{kc}"])
        for nc2 in range(8):
            for kc in range(8):
                r.op("tensor", lambda e, kc=kc, nc2=nc2: e.transpose(
                    ps[5][0:128, 0:128],
                    sb[f"rhsf{kc}"][:, 128 * nc2:128 * nc2 + 128], sb["idf"][:]),
                    reads=[f"kf{kc}", "idf", "ps5"], writes=["ps5"])
                vop(lambda e, kc=kc: e.tensor_copy(
                    sb["etm"][:, 128 * kc:128 * kc + 128], ps[5][0:128, 0:128]),
                    ["ps5"], ["etm"])
            vop(lambda e: e.tensor_reduce(sb["st8"][:, 0:1], sb["etm"][:],
                                          axis=AX.X, op=OP.max,
                                          apply_absolute_value=True),
                ["etm"], ["st8"], big=False)
            vop(lambda e: e.tensor_scalar(sb["st8"][:, 1:2], sb["st8"][:, 0:1],
                                          1e-12, INV127, OP.max, OP.mult),
                ["st8"], ["st8"], big=False)
            vop(lambda e, nc2=nc2: e.tensor_copy(sb["scall"][:, nc2:nc2 + 1],
                                                 sb["st8"][:, 1:2]),
                ["st8"], ["scall"], big=False)
            vop(lambda e, nc2=nc2: e.reciprocal(sb["rcall"][:, nc2:nc2 + 1],
                                                sb["st8"][:, 1:2]),
                ["st8"], ["rcall"], big=False)
        for which, src, dst in (("s", "scall", "scolbc"), ("r", "rcall", "rcolbc")):
            for nc2 in range(8):
                mm(ps[5][0:1, 0:128], sb[src][:, nc2:nc2 + 1], sb["idf"][:],
                   True, True, [src, "idf", "ps5"], ["ps5"])
                vop(lambda e: e.tensor_copy(sb["rowb"][0:1, 0:128], ps[5][0:1, 0:128]),
                    ["ps5"], ["rowb"])
                mm(ps[6][0:128, 0:128], sb["ones1f"][0:1, 0:128],
                   sb["rowb"][0:1, 0:128], True, True,
                   ["ones1f", "rowb", "ps6"], ["ps6"])
                vop(lambda e, dst=dst, nc2=nc2: e.tensor_copy(
                    sb[dst][:, 128 * nc2:128 * nc2 + 128], ps[6][0:128, 0:128]),
                    ["ps6"], [dst])
        for kc in range(8):
            vop(lambda e, kc=kc: e.tensor_tensor(sb["atm"][:], sb[f"rhsf{kc}"][:],
                                                 sb["rcolbc"][:], OP.mult),
                [f"kf{kc}", "rcolbc"], ["atm"])
            vop(lambda e: e.tensor_copy(sb["qi"][:], sb["atm"][:]), ["atm"], ["qi"])
            vop(lambda e, kc=kc: e.tensor_copy(sb[f"qrhs{kc}"][:], sb["qi"][:]),
                ["qi"], [f"qrhs{kc}"])

    def ln_plain_tm(src_key, out_key, out_tensor, src_tensor):
        vop(lambda e: e.tensor_reduce(sb["st8"][:, 2:3], src_tensor[:],
                                      axis=AX.X, op=OP.add),
            [src_key], ["st8"])
        vop(lambda e: e.activation(sb["atm"][:], src_tensor[:], AF.Square),
            [src_key], ["atm"], eng="scalar")
        vop(lambda e: e.tensor_reduce(sb["st8"][:, 3:4], sb["atm"][:],
                                      axis=AX.X, op=OP.add),
            ["atm"], ["st8"], big=False)
        vop(lambda e: e.tensor_scalar(sb["st8"][:, 4:5], sb["st8"][:, 2:3],
                                      1.0 / 1024.0, None, OP.mult),
            ["st8"], ["st8"], big=False)
        vop(lambda e: e.tensor_scalar(sb["st8"][:, 5:6], sb["st8"][:, 3:4],
                                      1.0 / 1024.0, None, OP.mult),
            ["st8"], ["st8"], big=False)
        vop(lambda e: e.tensor_tensor(sb["st8"][:, 6:7], sb["st8"][:, 4:5],
                                      sb["st8"][:, 4:5], OP.mult),
            ["st8"], ["st8"], big=False)
        vop(lambda e: e.tensor_tensor(sb["st8"][:, 6:7], sb["st8"][:, 5:6],
                                      sb["st8"][:, 6:7], OP.subtract),
            ["st8"], ["st8"], big=False)
        vop(lambda e: e.activation(sb["st8"][:, 7:8], sb["st8"][:, 6:7],
                                   AF.Sqrt, bias=sb["cb"][:, 1:2], scale=1.0),
            ["st8", "cb"], ["st8"], eng="scalar", big=False)
        vop(lambda e: e.reciprocal(sb["st8"][:, 8:9], sb["st8"][:, 7:8]),
            ["st8"], ["st8"], big=False)
        vop(lambda e: e.tensor_scalar(out_tensor[:], src_tensor[:],
                                      sb["st8"][:, 4:5], sb["st8"][:, 8:9],
                                      OP.subtract, OP.mult),
            [src_key, "st8"], [out_key])

    def final_block():
        for tc in range(16):
            s, tcl = tc // 4, tc % 4
            for kc in range(8):
                dma("sync", sb["atm"][:, 128 * kc:128 * kc + 128],
                    dram["xmid"][s, kc][:, 128 * tcl:128 * tcl + 128],
                    ["atm", "xmid"], ["atm"])
            dma("sync", sb["x0t"][:], dram["x0"][128 * tc:128 * tc + 128, :],
                ["x0t"], ["x0t"])
            for kc in range(8):
                r.op("tensor", lambda e, kc=kc: e.transpose(
                    ps[5][0:128, 0:128], sb["atm"][:, 128 * kc:128 * kc + 128],
                    sb["idf"][:]),
                    reads=["atm", "idf", "ps5"], writes=["ps5"])
                vop(lambda e, kc=kc: e.tensor_tensor(
                    sb["etm"][:, 128 * kc:128 * kc + 128], ps[5][0:128, 0:128],
                    sb["x0t"][:, 128 * kc:128 * kc + 128], OP.add),
                    ["ps5", "x0t"], ["etm"])
            ln_plain_tm("etm", "x1t", sb["x1t"], sb["etm"])
            vop(lambda e: e.tensor_reduce(sb["st8"][:, 0:1], sb["x1t"][:],
                                          axis=AX.X, op=OP.max,
                                          apply_absolute_value=True),
                ["x1t"], ["st8"], big=False)
            vop(lambda e: e.tensor_scalar(sb["st8"][:, 1:2], sb["st8"][:, 0:1],
                                          1e-12, INV127, OP.max, OP.mult),
                ["st8"], ["st8"], big=False)
            vop(lambda e: e.reciprocal(sb["st8"][:, 9:10], sb["st8"][:, 1:2]),
                ["st8"], ["st8"], big=False)
            vop(lambda e: e.tensor_scalar(sb["atm"][:], sb["x1t"][:],
                                          sb["st8"][:, 9:10], None, OP.mult),
                ["x1t", "st8"], ["atm"])
            vop(lambda e: e.tensor_copy(sb["qi"][:], sb["atm"][:]), ["atm"], ["qi"])
            vop(lambda e: e.tensor_copy(sb["atm"][:], sb["qi"][:]), ["qi"], ["atm"])
            for j in range(8):
                r.op("tensor", lambda e, j=j: e.transpose(
                    ps[5][0:128, 0:128], sb["atm"][:, 128 * j:128 * j + 128],
                    sb["idf"][:]),
                    reads=["atm", "idf", "ps5"], writes=["ps5"])
                vop(lambda e, j=j: e.tensor_copy(
                    sb["qlf"][:, 128 * j:128 * j + 128], ps[5][0:128, 0:128]),
                    ["ps5"], ["qlf"])
            for n2 in range(2):
                pb = ps[n2]
                for kc in range(8):
                    mm(pb[0:128, 0:512], sb["qlf"][:, 128 * kc:128 * kc + 128],
                       sb[f"qrhs{kc}"][:, 512 * n2:512 * n2 + 512], kc == 0, kc == 7,
                       ["qlf", f"qrhs{kc}", f"ps{n2}"], [f"ps{n2}"])
                vop(lambda e, pb=pb, n2=n2: e.scalar_tensor_tensor(
                    sb[f"hy{n2}"][:], pb[0:128, 0:512], sb["st8"][:, 1:2],
                    sb["scolbc"][:, 512 * n2:512 * n2 + 512], OP.mult, OP.mult),
                    [f"ps{n2}", "st8", "scolbc"], [f"hy{n2}"])
                vop(lambda e, n2=n2: e.tensor_tensor(
                    sb[f"hy{n2}"][:], sb[f"hy{n2}"][:],
                    sb["x1t"][:, 512 * n2:512 * n2 + 512], OP.add),
                    [f"hy{n2}", "x1t"], [f"hy{n2}"])
            vop(lambda e: e.tensor_reduce(sb["st8"][:, 2:3], sb["hy0"][:],
                                          axis=AX.X, op=OP.add),
                ["hy0"], ["st8"], big=False)
            vop(lambda e: e.tensor_reduce(sb["st8"][:, 3:4], sb["hy1"][:],
                                          axis=AX.X, op=OP.add),
                ["hy1"], ["st8"], big=False)
            vop(lambda e: e.tensor_tensor(sb["st8"][:, 2:3], sb["st8"][:, 2:3],
                                          sb["st8"][:, 3:4], OP.add),
                ["st8"], ["st8"], big=False)
            vop(lambda e: e.activation(sb["atm"][:, 0:512], sb["hy0"][:], AF.Square),
                ["hy0"], ["atm"], eng="scalar")
            vop(lambda e: e.activation(sb["atm"][:, 512:1024], sb["hy1"][:], AF.Square),
                ["hy1"], ["atm"], eng="scalar")
            vop(lambda e: e.tensor_reduce(sb["st8"][:, 3:4], sb["atm"][:],
                                          axis=AX.X, op=OP.add),
                ["atm"], ["st8"], big=False)
            vop(lambda e: e.tensor_scalar(sb["st8"][:, 4:5], sb["st8"][:, 2:3],
                                          1.0 / 1024.0, None, OP.mult),
                ["st8"], ["st8"], big=False)
            vop(lambda e: e.tensor_scalar(sb["st8"][:, 5:6], sb["st8"][:, 3:4],
                                          1.0 / 1024.0, None, OP.mult),
                ["st8"], ["st8"], big=False)
            vop(lambda e: e.tensor_tensor(sb["st8"][:, 6:7], sb["st8"][:, 4:5],
                                          sb["st8"][:, 4:5], OP.mult),
                ["st8"], ["st8"], big=False)
            vop(lambda e: e.tensor_tensor(sb["st8"][:, 6:7], sb["st8"][:, 5:6],
                                          sb["st8"][:, 6:7], OP.subtract),
                ["st8"], ["st8"], big=False)
            vop(lambda e: e.activation(sb["st8"][:, 7:8], sb["st8"][:, 6:7],
                                       AF.Sqrt, bias=sb["cb"][:, 1:2], scale=1.0),
                ["st8", "cb"], ["st8"], eng="scalar", big=False)
            vop(lambda e: e.reciprocal(sb["st8"][:, 8:9], sb["st8"][:, 7:8]),
                ["st8"], ["st8"], big=False)
            for n2 in range(2):
                vop(lambda e, n2=n2: e.tensor_scalar(
                    sb["x1t"][:, 512 * n2:512 * n2 + 512], sb[f"hy{n2}"][:],
                    sb["st8"][:, 4:5], sb["st8"][:, 8:9], OP.subtract, OP.mult),
                    [f"hy{n2}", "st8"], ["x1t"])
            dma("scalar", dram["out"][128 * tc:128 * tc + 128, :], sb["x1t"][:],
                ["x1t"], [])

    if mode == "DBGC":
        vop(lambda e: e.memset(sb["srun"][:], 0.0), [], ["srun"])
        for j in range(8):
            dma("sync", sb[f"kf{j}"][0:64, 0:1040], dram["gst"][j],
                [f"kf{j}"], [f"kf{j}"])
            dma("sync", sb[f"kf{j}"][64:128, 0:1040], dram["gst"][j],
                [f"kf{j}"], [f"kf{j}"])
        for j in range(8):
            vop(lambda e, j=j: e.scalar_tensor_tensor(
                sb["srun"][:], sb[f"kf{j}"][:, 0:1040], sb["pm"][:, j:j + 1],
                sb["srun"][:], OP.mult, OP.add),
                [f"kf{j}", "pm", "srun"], ["srun"])
        vop(lambda e: e.tensor_copy(sb["sbf"][:], sb["srun"][:]), ["srun"], ["sbf"])
        dma("sync", dram["dbgc_sbf"], sb["sbf"][:], ["sbf"], [])
        dma("sync", dram["dbgc_kf0"], sb["kf0"][:, 0:1040], ["kf0"], [])
    if has_b and mode != "DBGC":
        phase_b()
    if has_a:
        phase_a(la)
    if has_f:
        r.barrier()
        rhs_prep()
        final_block()

    sems = {}
    from contextlib import ExitStack
    stk = ExitStack()
    for e in ENGINES:
        sems[e] = stk.enter_context(nc.semaphore(f"sem_{e}"))
    block = stk.enter_context(nc.Block())
    r.emit(block, sems)
    stk.close()
    nc._rec_counts = dict(r.cnt)
    return nc


def _consts():
    idf = np.eye(128, dtype=np.float32)
    idb = np.eye(128).astype(ml_dtypes.bfloat16)
    tk, tq = np.meshgrid(np.arange(128), np.arange(128), indexing="ij")
    triu = (tq >= tk).astype(np.float32)
    e16 = np.zeros((128, 128), np.float32)
    for c in range(8):
        e16[0:64, 16 * c + 2 * c] = 1.0
        e16[64:128, 16 * c + 2 * c + 1] = 1.0
    sel = np.zeros((16, 1024), np.float32)
    for d in range(8):
        sel[2 * d, 128 * d:128 * d + 64] = 1.0
        sel[2 * d + 1, 128 * d + 64:128 * d + 128] = 1.0
    ones1f = np.ones((1, 128), np.float32)
    return dict(idf=idf, idb=idb, triu=triu,
                e16=e16.astype(ml_dtypes.bfloat16),
                sel=sel.astype(ml_dtypes.bfloat16), ones1f=ones1f)


def _fm8(v):
    return np.ascontiguousarray(np.asarray(v).reshape(8, 128).T)


def _fm32(v):
    return np.ascontiguousarray(np.asarray(v).reshape(32, 128).T)


def kernel(L0_attn_x, L0_attn_Wq, L0_attn_bq, L0_attn_Wk, L0_attn_bk,
           L0_attn_Wv, L0_attn_bv, L0_attn_Wo, L0_attn_bo, L0_attn_omega,
           L0_attn_ln1_g, L0_attn_ln1_b, L0_attn_ln2_g, L0_attn_ln2_b,
           L0_attn_W1, L0_attn_bf1, L0_attn_W2, L0_attn_bf2, L0_ffn_rhs):
    x = np.ascontiguousarray(np.asarray(L0_attn_x, dtype=np.float32))
    Wq = np.asarray(L0_attn_Wq, np.float32)
    Wk = np.asarray(L0_attn_Wk, np.float32)
    Wv = np.asarray(L0_attn_Wv, np.float32)
    Wo = np.asarray(L0_attn_Wo, np.float32)
    W1 = np.asarray(L0_attn_W1, np.float32)
    W2 = np.asarray(L0_attn_W2, np.float32)
    bq = np.asarray(L0_attn_bq, np.float32)
    bk = np.asarray(L0_attn_bk, np.float32)
    bv = np.asarray(L0_attn_bv, np.float32)
    bo = np.asarray(L0_attn_bo, np.float32)
    bf1 = np.asarray(L0_attn_bf1, np.float32)
    bf2 = np.asarray(L0_attn_bf2, np.float32)
    l1g = np.asarray(L0_attn_ln1_g, np.float32)
    l1b = np.asarray(L0_attn_ln1_b, np.float32)
    l2g = np.asarray(L0_attn_ln2_g, np.float32)
    l2b = np.asarray(L0_attn_ln2_b, np.float32)
    om = np.asarray(L0_attn_omega, np.float32)
    rhs = np.ascontiguousarray(np.asarray(L0_ffn_rhs, np.float32))

    consts = _consts()
    x0s = [np.ascontiguousarray(x[c // 4, 2048 * (c % 4):2048 * (c % 4) + 2048, :])
           for c in range(NCORE)]
    pmasks = []
    for c in range(NCORE):
        g0 = (c // 4) * 4
        m = np.zeros(8, np.float32)
        m[g0:c] = 1.0
        pmasks.append(np.broadcast_to(m, (128, 8)).copy())

    def layer_vecs(l):
        return dict(
            bq_b=_fm8(bq[l]), bo_b=_fm8(bo[l]), ln1g_b=_fm8(l1g[l]),
            ln1b_b=_fm8(l1b[l]), ln2g_b=_fm8(l2g[l]), ln2b_b=_fm8(l2b[l]),
            bf1_b=_fm32(bf1[l]), bf2_b=_fm8(bf2[l]),
            wq_b=np.ascontiguousarray(Wq[l]), wo_b=np.ascontiguousarray(Wo[l]),
            w1_b=np.ascontiguousarray(W1[l]), w2_b=np.ascontiguousarray(W2[l]),
            om_b=np.ascontiguousarray(np.vstack([om[l], om[l]])))

    def layer_a_in(l):
        return dict(
            wk_a=np.ascontiguousarray(Wk[l]), wv_a=np.ascontiguousarray(Wv[l]),
            bk_a=_fm8(bk[l]), bv_a=np.ascontiguousarray(bv[l]).reshape(1, D),
            om_a=np.ascontiguousarray(np.vstack([om[l], om[l]])))

    state = [dict() for _ in range(NCORE)]

    nc0 = build_launch("A", 0, None)
    in_maps = []
    for c in range(NCORE):
        m = dict(consts)
        m.update(layer_a_in(0))
        m["x0"] = x0s[c]
        in_maps.append(m)
    res = run_bass_kernel_spmd(nc0, in_maps, core_ids=list(range(NCORE)))
    for c in range(NCORE):
        state[c] = {k: res.results[c][k]
                    for k in ("xout", "kf_out", "vt_out", "sch_out", "stot")}

    for k in (1, 2, 3):
        lb = k - 1
        la = k if k < 3 else None
        mode = "BA" if k < 3 else "BF"
        nck = build_launch(mode, la, lb)
        gst = np.stack([state[c]["stot"] for c in range(NCORE)])
        in_maps = []
        for c in range(NCORE):
            m = dict(consts)
            m.update(layer_vecs(lb))
            m["xin"] = state[c]["xout"]
            m["kf_in"] = state[c]["kf_out"]
            m["vt_in"] = state[c]["vt_out"]
            m["sch_in"] = state[c]["sch_out"]
            m["gst"] = gst
            m["pmask"] = pmasks[c]
            if mode == "BA":
                m.update(layer_a_in(la))
            else:
                m["x0"] = x0s[c]
                m["rhs"] = rhs
            in_maps.append(m)
        res = run_bass_kernel_spmd(nck, in_maps, core_ids=list(range(NCORE)))
        if mode == "BA":
            ns = []
            for c in range(NCORE):
                ns.append({kk: res.results[c][kk]
                           for kk in ("xout", "kf_out", "vt_out", "sch_out", "stot")})
            state = ns
        else:
            outs = [res.results[c]["out"] for c in range(NCORE)]

    full = np.empty((2, 8192, D), np.float32)
    for c in range(NCORE):
        full[c // 4, 2048 * (c % 4):2048 * (c % 4) + 2048, :] = outs[c]
    return full
